# version 13
# baseline (speedup 1.0000x reference)
import numpy as np
from contextlib import ExitStack

import concourse.bass as bass
import concourse.mybir as mybir
from concourse.bass_utils import run_bass_kernel_spmd

F32 = mybir.dt.float32
BF16 = mybir.dt.bfloat16
I32 = mybir.dt.int32
AF = mybir.ActivationFunctionType
ALU = mybir.AluOpType
AX = mybir.AxisListType

NCORES = 8
P3_SPLIT_LOADS = True
P3_RNEG_SKIP = True
T = 4096
D = 1024
FC = D // 128
TC = T // 128
NSC = T // 512
EPS_LN = 1e-5


class Sem:
    def __init__(self, h):
        self.h = h
        self.v = 0

    def inc(self, n=1):
        self.v += n
        return self.v


class Prog:
    def __init__(self):
        self.nc = bass.Bass("TRN2", target_bir_lowering=False)
        self.es = ExitStack()
        self._scopes = [self.es]
        self._names = set()
        self.consts = {}
        self.all_sems = []

    def push_scope(self):
        st = ExitStack()
        self._scopes.append(st)
        return st

    def pop_scope(self):
        self._scopes.pop().close()

    def _nm(self, name):
        assert name not in self._names, f"duplicate tensor name {name}"
        self._names.add(name)
        return name

    def sbuf(self, name, shape, dtype=F32):
        return self._scopes[-1].enter_context(self.nc.sbuf_tensor(self._nm(name), shape, dtype))

    def psum(self, name, shape, dtype=F32):
        return self._scopes[-1].enter_context(self.nc.psum_tensor(self._nm(name), shape, dtype))

    def sem(self, name):
        h = Sem(self.es.enter_context(self.nc.semaphore(name)))
        self.all_sems.append(h)
        return h

    def hbm(self, name, shape, dtype=F32, kind="ExternalInput"):
        return self.nc.dram_tensor(self._nm(name), list(shape), dtype, kind=kind).ap()

    def quiesce(self):
        nc = self.nc
        fin = self.sem(f"qsc{len(self.all_sems)}")
        for eng in (nc.tensor, nc.vector, nc.scalar, nc.gpsimd):
            eng.drain().then_inc(fin.h, 1)
            fin.inc()
        for eng in (nc.tensor, nc.vector, nc.scalar, nc.gpsimd, nc.sync):
            eng.wait_ge(fin.h, fin.v)
            for s_ in self.all_sems:
                if s_ is not fin and s_.v > 0:
                    eng.wait_ge(s_.h, s_.v)

    def done(self):
        while len(self._scopes) > 1:
            self.pop_scope()
        self.es.close()
        return self.nc


def emit_p0(p, x_h, xnfm_h, pre="p0_"):
    nc = p.nc
    ident = p.sbuf(pre + "ident", [128, 128], BF16)
    xt = [p.sbuf(pre + f"xt{i}", [128, 4, 1024], F32) for i in range(2)]
    stg = [p.sbuf(pre + f"stg{i}", [128, 8, 512], BF16) for i in range(2)]
    xn_bf = p.sbuf(pre + "xnbf", [128, 1024], BF16)
    junk = p.sbuf(pre + "junk", [128, 4, 1024], BF16)
    st = p.sbuf(pre + "st", [128, 32], F32)
    pstr = p.psum(pre + "pstr", [128, 512], BF16)
    lds = p.sem(pre + "lds")
    ldg = p.sem(pre + "ldg")
    so = p.sem(pre + "so")
    sv = p.sem(pre + "sv")
    sa = p.sem(pre + "sa")
    pe = p.sem(pre + "pe")

    nc.gpsimd.dma_start(out=ident[:], in_=p.consts["IDENT"]).then_inc(ldg.h, 16)
    ldg.inc(16)

    x_r = x_h.rearrange("(sc c p) d -> sc p c d", p=128, c=4)

    xt_free_at = [None, None]
    stg_free_at = [None, None]
    pe_prev_chunk = 0

    for sc in range(NSC):
        b = sc % 2
        if xt_free_at[b] is not None:
            svv, sav = xt_free_at[b]
            nc.sync.wait_ge(sv.h, svv)
            nc.sync.wait_ge(sa.h, sav)
        nc.sync.dma_start(out=xt[b][:], in_=x_r[sc]).then_inc(lds.h, 16)
        lds.inc(16)
        ld_target = lds.v
        sg = stg[b]
        if stg_free_at[b] is not None:
            nc.vector.wait_ge(so.h, so.v)

        nc.vector.wait_ge(lds.h, ld_target)
        nc.scalar.wait_ge(lds.h, ld_target)
        sb_ = b * 16
        for c in range(4):
            nc.vector.reduce_sum(st[:, sb_ + c:sb_ + c + 1], xt[b][:, c], axis=AX.X)
            nc.scalar.activation(junk[:, c], xt[b][:, c], AF.Square, accum_out=st[:, sb_ + 4 + c:sb_ + 5 + c])
        nc.vector.drain().then_inc(sv.h, 1)
        sv.inc()
        nc.scalar.drain().then_inc(sa.h, 1)
        sa.inc()
        nc.vector.wait_ge(sv.h, sv.v)
        nc.vector.wait_ge(sa.h, sa.v)
        nc.vector.tensor_scalar(st[:, sb_ + 8:sb_ + 12], st[:, sb_ + 0:sb_ + 4], 1.0 / D, None, ALU.mult)
        nc.vector.drain().then_inc(sv.h, 1)
        sv.inc()
        nc.vector.wait_ge(sv.h, sv.v)
        nc.vector.tensor_scalar(st[:, sb_ + 4:sb_ + 8], st[:, sb_ + 4:sb_ + 8], 1.0 / D, None, ALU.mult)
        nc.vector.drain().then_inc(sv.h, 1)
        sv.inc()
        nc.vector.wait_ge(sv.h, sv.v)
        nc.vector.scalar_tensor_tensor(st[:, sb_ + 12:sb_ + 16], st[:, sb_ + 8:sb_ + 12], 1.0, st[:, sb_ + 8:sb_ + 12], ALU.mult, ALU.mult)
        nc.vector.drain().then_inc(sv.h, 1)
        sv.inc()
        nc.vector.wait_ge(sv.h, sv.v)
        nc.vector.tensor_tensor(st[:, sb_ + 12:sb_ + 16], st[:, sb_ + 4:sb_ + 8], st[:, sb_ + 12:sb_ + 16], ALU.subtract)
        nc.vector.drain().then_inc(sv.h, 1)
        sv.inc()
        nc.vector.wait_ge(sv.h, sv.v)
        nc.vector.tensor_scalar(st[:, sb_ + 12:sb_ + 16], st[:, sb_ + 12:sb_ + 16], 1.0, EPS_LN, ALU.mult, ALU.add)
        nc.vector.drain().then_inc(sv.h, 1)
        sv.inc()
        nc.scalar.wait_ge(sv.h, sv.v)
        nc.scalar.activation(st[:, sb_ + 12:sb_ + 16], st[:, sb_ + 12:sb_ + 16], AF.Sqrt)
        nc.scalar.drain().then_inc(sa.h, 1)
        sa.inc()
        nc.vector.wait_ge(sa.h, sa.v)
        nc.vector.reciprocal(st[:, sb_ + 12:sb_ + 16], st[:, sb_ + 12:sb_ + 16])
        nc.vector.drain().then_inc(sv.h, 1)
        sv.inc()

        for c in range(4):
            nc.vector.wait_ge(sv.h, sv.v)
            nc.vector.wait_ge(pe.h, pe_prev_chunk)
            nc.vector.tensor_scalar(xn_bf[:], xt[b][:, c], st[:, sb_ + 8 + c:sb_ + 9 + c], st[:, sb_ + 12 + c:sb_ + 13 + c],
                                    ALU.subtract, ALU.mult)
            nc.vector.drain().then_inc(sv.h, 1)
            sv.inc()
            nc.tensor.wait_ge(sv.h, sv.v)
            if sc == 0 and c == 0:
                nc.tensor.wait_ge(ldg.h, 16)
            for g in range(2):
                if pe.v > 0:
                    nc.tensor.wait_ge(sv.h, sv.v)
                for k in range(4):
                    kk = g * 4 + k
                    nc.tensor.transpose(pstr[:, k * 128:(k + 1) * 128],
                                        xn_bf[:, kk * 128:(kk + 1) * 128], ident[:])
                nc.tensor.drain().then_inc(pe.h, 1)
                pe.inc()
                nc.vector.wait_ge(pe.h, pe.v)
                nc.vector.tensor_copy(sg[:, g * 4:(g + 1) * 4, c * 128:(c + 1) * 128],
                                      pstr[:].rearrange("p (k t) -> p k t", k=4))
                nc.vector.drain().then_inc(sv.h, 1)
                sv.inc()
            pe_prev_chunk = pe.v
        xt_free_at[b] = (sv.v, sa.v)

        nc.sync.wait_ge(sv.h, sv.v)
        for k in range(8):
            nc.sync.dma_start(out=xnfm_h[k, :, sc * 512:(sc + 1) * 512], in_=sg[:, k]).then_inc(so.h, 16)
            so.inc(16)
        stg_free_at[b] = so.v
    nc.sync.wait_ge(so.h, so.v)


def make_consts():
    import ml_dtypes
    c = {}
    c["IDENT"] = np.eye(128, dtype=ml_dtypes.bfloat16)
    return c


def build_p0():
    p = Prog()
    x_h = p.hbm("x", [T, D], F32)
    xnfm_h = p.hbm("xnfm", [FC, 128, T], BF16, kind="ExternalOutput")
    p.consts = {"IDENT": p.hbm("IDENT", [128, 128], BF16)}
    emit_p0(p, x_h, xnfm_h)
    p.done()
    return p.nc


def emit_p2(p, io, pre="p2_"):
    nc = p.nc
    NE = 8
    xnfm_h, x_h, s0_h = io["xnfm"], io["x"], io["s0"]
    wgT_h, w1T_h, w3T_h, w2T_h = io["wgT"], io["w1T"], io["w3T"], io["w2T"]
    x1_h, xnfm1_h, rrho_h = io["x1"], io["xnfm1"], io["rrho"]

    ident = p.sbuf(pre + "ident", [128, 128], BF16)
    xnp = [p.sbuf(pre + f"xnp{i}", [128, FC, 1024], BF16) for i in range(2)]
    wg_sb = p.sbuf(pre + "wg", [128, FC, NE], BF16)
    wexp = p.sbuf(pre + "wexp", [128, TC, NE], F32)
    rr = p.sbuf(pre + "rr", [128, TC], F32)
    act = p.sbuf(pre + "act", [128, 16, 1024], BF16)
    w1s = [p.sbuf(pre + f"w1s{i}", [128, FC, 128], BF16) for i in range(2)]
    w3s = [p.sbuf(pre + f"w3s{i}", [128, FC, 128], BF16) for i in range(2)]
    w2s = p.sbuf(pre + "w2s", [128, 16, 1024], BF16)
    gacc = p.sbuf(pre + "gacc", [128, 8, 1024], F32)
    g_all = p.sbuf(pre + "gall", [128, 8, 1024], BF16)
    tmp_bf = p.sbuf(pre + "tmpbf", [128, 2, 512], BF16)
    rt = p.sbuf(pre + "rt", [128, 40], F32)
    g_sb = p.sbuf(pre + "g", [128, 1024], F32)
    xs_sb = [p.sbuf(pre + f"xs{i}", [128, 1024], F32) for i in range(2)]
    ss_sb = [p.sbuf(pre + f"ss{i}", [128, 1024], F32) for i in range(2)]
    x1_sb = [p.sbuf(pre + f"x1s{i}", [128, 1024], F32) for i in range(2)]
    xn1_bf = p.sbuf(pre + "xn1bf", [128, 2, 1024], BF16)
    stg = [p.sbuf(pre + f"stg{i}", [128, 8, 128], BF16) for i in range(2)]
    st = p.sbuf(pre + "st", [128, 8], F32)
    junk = p.sbuf(pre + "junk", [128, 1024], BF16)

    ps_rt = p.psum(pre + "psrt", [128, 8], F32)
    ps_h1 = [p.psum(pre + f"psh1{i}", [128, 512], F32) for i in range(2)]
    ps_h3 = [p.psum(pre + f"psh3{i}", [128, 512], F32) for i in range(2)]
    ps_y = [p.psum(pre + f"psy{i}", [128, 512], F32) for i in range(2)]
    pstr = p.psum(pre + "pstr", [128, 8, 128], BF16)

    ldr = p.sem(pre + "ldr")
    ldrp = [p.sem(pre + "ldrp0"), p.sem(pre + "ldrp1")]
    lds = p.sem(pre + "lds")
    ldg = p.sem(pre + "ldg")
    ldgp = [p.sem(pre + "ldgp0"), p.sem(pre + "ldgp1")]
    ldw2 = p.sem(pre + "ldw2")
    so = p.sem(pre + "so")
    sv = p.sem(pre + "sv")
    sa = p.sem(pre + "sa")
    pe = p.sem(pre + "pe")

    nc.gpsimd.dma_start(out=ident[:], in_=p.consts["IDENT"]).then_inc(ldg.h, 16)
    nc.gpsimd.dma_start(out=wg_sb[:], in_=wgT_h.rearrange("(k p) e -> p k e", p=128)).then_inc(ldg.h, 16)
    ldg.inc(32)


    w1T_r = w1T_h.rearrange("e (kc pp) i -> e pp kc i", pp=128)
    w3T_r = w3T_h.rearrange("e (kc pp) i -> e pp kc i", pp=128)
    w2T_r = w2T_h.rearrange("e (ic pp) o -> e pp ic o", pp=128)
    x_r = x_h.rearrange("(c pp) d -> c pp d", pp=128)
    s_r = s0_h.rearrange("(c pp) d -> c pp d", pp=128)
    x1_r = x1_h.rearrange("(c pp) d -> c pp d", pp=128)

    flat = [(tt, e, i) for tt in range(4) for e in range(NE) for i in range(16)]
    load_done_tgt = {}
    a_done_pe = {}
    psA_free = {}
    psY_free = {}

    def issue_w13(m):
        tt, e, i = flat[m]
        bw = m % 2
        if m - 2 in a_done_pe:
            nc.gpsimd.wait_ge(pe.h, a_done_pe[m - 2])
        nc.gpsimd.dma_start(out=w1s[bw][:], in_=w1T_r[e, :, :, i * 128:(i + 1) * 128]).then_inc(ldgp[bw].h, 16)
        nc.gpsimd.dma_start(out=w3s[bw][:], in_=w3T_r[e, :, :, i * 128:(i + 1) * 128]).then_inc(ldgp[bw].h, 16)
        ldgp[bw].inc(32)
        load_done_tgt[m] = ldgp[bw].v

    BIG = 1.0e9
    nc.tensor.wait_ge(ldg.h, ldg.v)

    def emit_router_chunk(tt, tc):
        if tc != tt * 8:
            nc.tensor.wait_ge(sv.h, sv.v)
        for k in range(FC):
            nc.tensor.matmul(ps_rt[:], xnp[tt % 2][:, k, (tc - tt * 8) * 128:(tc - tt * 8 + 1) * 128], wg_sb[:, k],
                             start=(k == 0), stop=(k == FC - 1))
        nc.tensor.drain().then_inc(pe.h, 1)
        pe.inc()
        nc.vector.wait_ge(pe.h, pe.v)
        nc.vector.tensor_copy(rt[:, 0:8], ps_rt[:])
        nc.vector.drain()
        nc.vector.reduce_max(rt[:, 8:9], rt[:, 0:8], axis=AX.X)
        nc.vector.drain()
        nc.vector.tensor_scalar(rt[:, 16:24], rt[:, 0:8], rt[:, 8:9], None, ALU.is_ge)
        nc.vector.drain()
        nc.vector.scalar_tensor_tensor(rt[:, 16:24], rt[:, 16:24], -BIG, rt[:, 0:8],
                                       ALU.mult, ALU.add)
        nc.vector.drain()
        nc.vector.reduce_max(rt[:, 9:10], rt[:, 16:24], axis=AX.X)
        nc.vector.drain().then_inc(sv.h, 1)
        sv.inc()
        nc.scalar.wait_ge(sv.h, sv.v)
        nc.scalar.mul(rt[:, 10:11], rt[:, 8:9], -1.0)
        nc.scalar.drain()
        nc.scalar.activation(rt[:, 16:24], rt[:, 0:8], AF.Exp, bias=rt[:, 10:11], scale=1.0)
        nc.scalar.drain().then_inc(sa.h, 1)
        sa.inc()
        nc.vector.wait_ge(sa.h, sa.v)
        nc.vector.tensor_scalar(rt[:, 24:32], rt[:, 0:8], rt[:, 9:10], None, ALU.is_ge)
        nc.vector.drain()
        nc.vector.tensor_tensor(rt[:, 24:32], rt[:, 16:24], rt[:, 24:32], ALU.mult)
        nc.vector.drain()
        nc.vector.reduce_sum(rt[:, 32:33], rt[:, 24:32], axis=AX.X)
        nc.vector.drain()
        nc.vector.reciprocal(rt[:, 33:34], rt[:, 32:33])
        nc.vector.drain()
        nc.vector.tensor_scalar(wexp[:, tc], rt[:, 24:32], rt[:, 33:34], None, ALU.mult)
        nc.vector.drain().then_inc(sv.h, 1)
        sv.inc()



    gall_sa = [None]

    def emit_gall(tt):
        nc.scalar.wait_ge(sv.h, sv.v)
        for kk8 in range(8):
            nc.scalar.activation(g_all[:, kk8], gacc[:, kk8], AF.Silu)
        nc.scalar.drain().then_inc(sa.h, 1)
        sa.inc()
        gall_sa[0] = sa.v

    cmb_sv = {}
    pstr_free = [None]

    def emit_combine_front(tt, ts_):
        c_idx = tt * 8 + ts_
        bb = ts_ % 2
        if c_idx >= 1:
            nc.sync.wait_ge(sv.h, sv.v)
        nc.sync.dma_start(out=xs_sb[bb][:], in_=x_r[c_idx]).then_inc(lds.h, 16)
        nc.sync.dma_start(out=ss_sb[bb][:], in_=s_r[c_idx]).then_inc(lds.h, 16)
        lds.inc(32)
        nc.vector.wait_ge(sa.h, gall_sa[0])
        nc.vector.wait_ge(lds.h, lds.v)
        if c_idx >= 2:
            nc.vector.wait_ge(so.h, so.v)
        nc.vector.tensor_tensor(g_sb[:], ss_sb[bb][:], g_all[:, ts_], ALU.mult)
        nc.vector.drain()
        nc.vector.tensor_tensor(x1_sb[bb][:], xs_sb[bb][:], g_sb[:], ALU.add)
        nc.vector.drain().then_inc(sv.h, 1)
        sv.inc()
        nc.sync.wait_ge(sv.h, sv.v)
        nc.sync.dma_start(out=x1_r[c_idx], in_=x1_sb[bb][:]).then_inc(so.h, 16)
        so.inc(16)
        nc.vector.reduce_sum(st[:, 0:1], x1_sb[bb][:], axis=AX.X)
        nc.vector.drain().then_inc(sv.h, 1)
        sv.inc()
        nc.scalar.wait_ge(sv.h, sv.v)
        nc.scalar.activation(junk[:], x1_sb[bb][:], AF.Square, accum_out=st[:, 1:2])
        nc.scalar.drain().then_inc(sa.h, 1)
        sa.inc()
        nc.vector.wait_ge(sa.h, sa.v)
        nc.vector.tensor_scalar(st[:, 2:3], st[:, 0:1], 1.0 / D, None, ALU.mult)
        nc.vector.tensor_scalar(st[:, 3:4], st[:, 1:2], 1.0 / D, None, ALU.mult)
        nc.vector.drain()
        nc.vector.scalar_tensor_tensor(st[:, 4:5], st[:, 2:3], 1.0, st[:, 2:3], ALU.mult, ALU.mult)
        nc.vector.drain()
        nc.vector.tensor_tensor(st[:, 4:5], st[:, 3:4], st[:, 4:5], ALU.subtract)
        nc.vector.drain()
        nc.vector.tensor_scalar(st[:, 5:6], st[:, 4:5], 1.0, EPS_LN, ALU.mult, ALU.add)
        nc.vector.drain().then_inc(sv.h, 1)
        sv.inc()
        nc.scalar.wait_ge(sv.h, sv.v)
        nc.scalar.activation(st[:, 6:7], st[:, 5:6], AF.Sqrt)
        nc.scalar.drain().then_inc(sa.h, 1)
        sa.inc()
        nc.vector.wait_ge(sa.h, sa.v)
        nc.vector.reciprocal(st[:, 7:8], st[:, 6:7])
        nc.vector.drain()
        nc.vector.reciprocal(st[:, 4:5], st[:, 4:5])
        nc.vector.drain()
        nc.vector.tensor_tensor(st[:, 4:5], st[:, 5:6], st[:, 4:5], ALU.mult)
        nc.vector.drain().then_inc(sv.h, 1)
        sv.inc()
        nc.scalar.wait_ge(sv.h, sv.v)
        nc.scalar.activation(st[:, 4:5], st[:, 4:5], AF.Sqrt, scale=1.0 / D)
        nc.scalar.drain().then_inc(sa.h, 1)
        sa.inc()
        nc.vector.wait_ge(sa.h, sa.v)
        nc.vector.tensor_copy(rr[:, c_idx:c_idx + 1], st[:, 4:5])
        nc.vector.drain()
        nc.vector.tensor_scalar(xn1_bf[:, bb], x1_sb[bb][:], st[:, 2:3], st[:, 7:8],
                                ALU.subtract, ALU.mult)
        nc.vector.drain().then_inc(sv.h, 1)
        sv.inc()
        cmb_sv[(tt, ts_)] = sv.v

    def emit_combine_back(tt, ts_):
        c_idx = tt * 8 + ts_
        bb = ts_ % 2
        nc.tensor.wait_ge(sv.h, cmb_sv[(tt, ts_)])
        if pstr_free[0] is not None:
            nc.tensor.wait_ge(sv.h, pstr_free[0])
        for k in range(FC):
            nc.tensor.transpose(pstr[:, k], xn1_bf[:, bb, k * 128:(k + 1) * 128], ident[:])
        nc.tensor.drain().then_inc(pe.h, 1)
        pe.inc()
        nc.vector.wait_ge(pe.h, pe.v)
        if c_idx >= 2:
            nc.vector.wait_ge(so.h, so.v)
        nc.vector.tensor_copy(stg[bb][:], pstr[:])
        nc.vector.drain().then_inc(sv.h, 1)
        sv.inc()
        pstr_free[0] = sv.v
        nc.sync.wait_ge(sv.h, sv.v)
        nc.sync.dma_start(out=xnfm1_h[:, :, c_idx * 128:(c_idx + 1) * 128].rearrange("k p t -> p k t"),
                          in_=stg[bb][:]).then_inc(so.h, 16)
        so.inc(16)

    router_pe = pe.v
    stageB_done_pe = router_pe
    m_next = 0
    xn_free = {}
    xn_tgt = {}

    def load_xn(tt):
        bx = tt % 2
        fr = xn_free.get(bx)
        if fr is not None:
            nc.sync.wait_ge(pe.h, fr)
        elif tt > 0:
            nc.sync.wait_ge(pe.h, pe.v)
        for k in range(FC):
            nc.sync.dma_start(out=xnp[bx][:, k],
                              in_=xnfm_h[k, :, tt * 1024:(tt + 1) * 1024]).then_inc(ldrp[bx].h, 16)
            ldrp[bx].inc(16)
        xn_tgt[tt] = ldrp[bx].v

    load_xn(0)
    for tt in range(4):
        nc.tensor.wait_ge(ldrp[tt % 2].h, xn_tgt[tt])
        for tc8 in range(8):
            emit_router_chunk(tt, tt * 8 + tc8)
        if tt + 1 < 4:
            load_xn(tt + 1)
        for e in range(NE):
            m0 = (tt * NE + e) * 16
            nc.gpsimd.wait_ge(pe.h, stageB_done_pe)
            nc.gpsimd.dma_start(out=w2s[:], in_=w2T_r[e]).then_inc(ldw2.h, 16)
            ldw2.inc(16)
            w2_tgt = ldw2.v
            while m_next < m0 + 2:
                issue_w13(m_next)
                m_next += 1
            for i in range(16):
                m = m0 + i
                bw = m % 2
                nc.tensor.wait_ge(ldgp[bw].h, load_done_tgt[m])
                for n in range(2):
                    bp = n
                    fa = psA_free.get(bp)
                    if fa is not None:
                        nc.tensor.wait_ge(sa.h, fa[0])
                        nc.tensor.wait_ge(sv.h, fa[1])
                    tok0 = tt * 1024 + n * 512
                    for k in range(FC):
                        nc.tensor.matmul(ps_h1[bp][:], w1s[bw][:, k], xnp[tt % 2][:, k, tok0 - tt * 1024:tok0 - tt * 1024 + 512],
                                         start=(k == 0), stop=(k == FC - 1))
                    for k in range(FC):
                        nc.tensor.matmul(ps_h3[bp][:], w3s[bw][:, k], xnp[tt % 2][:, k, tok0 - tt * 1024:tok0 - tt * 1024 + 512],
                                         start=(k == 0), stop=(k == FC - 1))
                    nc.tensor.drain().then_inc(pe.h, 1)
                    pe.inc()
                    nc.scalar.wait_ge(pe.h, pe.v)
                    fa2 = psA_free.get(bp)
                    if fa2 is not None:
                        nc.scalar.wait_ge(sv.h, fa2[1])
                    nc.scalar.activation(tmp_bf[:, bp], ps_h1[bp][:], AF.Silu)
                    nc.scalar.drain().then_inc(sa.h, 1)
                    sa.inc()
                    nc.vector.wait_ge(sa.h, sa.v)
                    nc.vector.tensor_tensor(act[:, i, n * 512:(n + 1) * 512], tmp_bf[:, bp], ps_h3[bp][:], ALU.mult)
                    nc.vector.drain().then_inc(sv.h, 1)
                    sv.inc()
                    psA_free[bp] = (sa.v, sv.v)
                a_done_pe[m] = pe.v
                if m_next < len(flat) and m_next <= m + 2:
                    issue_w13(m_next)
                    m_next += 1
            nc.tensor.wait_ge(ldw2.h, w2_tgt)
            nc.tensor.wait_ge(sv.h, sv.v)
            for ts_ in range(8):
                for half in range(2):
                    bp = (ts_ * 2 + half) % 2
                    fy = psY_free.get(bp)
                    if fy is not None:
                        nc.tensor.wait_ge(sv.h, fy)
                    for i in range(16):
                        nc.tensor.matmul(ps_y[bp][:], act[:, i, ts_ * 128:(ts_ + 1) * 128],
                                         w2s[:, i, half * 512:(half + 1) * 512],
                                         start=(i == 0), stop=(i == 15))
                    nc.tensor.drain().then_inc(pe.h, 1)
                    pe.inc()
                    nc.vector.wait_ge(pe.h, pe.v)
                    wcol = wexp[:, tt * 8 + ts_, e:e + 1]
                    dst = gacc[:, ts_, half * 512:(half + 1) * 512]
                    if e == 0:
                        if gall_sa[0] is not None:
                            nc.vector.wait_ge(sa.h, gall_sa[0])
                        nc.vector.tensor_scalar(dst, ps_y[bp][:], wcol, None, ALU.mult)
                    else:
                        nc.vector.scalar_tensor_tensor(dst, ps_y[bp][:], wcol, dst, ALU.mult, ALU.add)
                    nc.vector.drain().then_inc(sv.h, 1)
                    sv.inc()
                    psY_free[bp] = sv.v
            stageB_done_pe = pe.v
            if tt > 0:
                emit_combine_front(tt - 1, e)
                if e > 0:
                    emit_combine_back(tt - 1, e - 1)
            if e == NE - 1:
                if tt > 0:
                    emit_combine_back(tt - 1, NE - 1)
                emit_gall(tt)
                xn_free[tt % 2] = pe.v

    for c8 in range(8):
        emit_combine_front(3, c8)
        emit_combine_back(3, c8)
    nc.sync.wait_ge(sv.h, sv.v)
    nc.sync.dma_start(out=rrho_h, in_=rr[:]).then_inc(so.h, 16)
    so.inc(16)
    nc.sync.wait_ge(so.h, so.v)
    nc.sync.wait_ge(lds.h, lds.v)


def build_p2():
    p = Prog()
    io = {
        "xnfm": p.hbm("xnfm", [FC, 128, T], BF16),
        "x": p.hbm("x", [T, D], F32),
        "s0": p.hbm("s0", [T, D], F32),
        "wgT": p.hbm("wgT", [D, 8], F32),
        "w1T": p.hbm("w1T", [8, D, 2048], F32),
        "w3T": p.hbm("w3T", [8, D, 2048], F32),
        "w2T": p.hbm("w2T", [8, 2048, D], F32),
        "x1": p.hbm("x1", [T, D], F32, kind="ExternalOutput"),
        "xnfm1": p.hbm("xnfm1", [FC, 128, T], BF16, kind="ExternalOutput"),
        "rrho": p.hbm("rrho", [128, TC], F32, kind="ExternalOutput"),
    }
    p.consts = {"IDENT": p.hbm("IDENT", [128, 128], BF16)}
    emit_p2(p, io)
    p.done()
    return p.nc


def prep_shared(inputs):
    f = lambda a: np.ascontiguousarray(np.asarray(a, dtype=np.float32))
    d = {}
    d["wgT"] = f(np.asarray(inputs["L0_ffn_wg"]).T)
    d["w1T"] = f(np.asarray(inputs["L0_ffn_w1"]).transpose(0, 2, 1))
    d["w3T"] = f(np.asarray(inputs["L0_ffn_w3"]).transpose(0, 2, 1))
    d["w2T"] = f(np.asarray(inputs["L0_ffn_w2"]).transpose(0, 2, 1))
    return d


def emit_p1(p, io, pre="p1_"):
    nc = p.nc
    xnfm_h, s0_h = io["xnfm"], io["s0"]
    WqD_h, WkD_h, Wv_h, Wo_h = io["WqD"], io["WkD"], io["Wv"], io["Wo"]
    G1_h, G2_h, thetaD_h = io["G1"], io["G2"], io["thetaD"]
    gam_h, bet_h = io["gam"], io["bet"]

    ident = p.sbuf(pre + "ident", [128, 128], BF16)
    triu4 = p.sbuf(pre + "triu4", [128, 4, 128], BF16)
    xn = [p.sbuf(pre + f"xn{i}", [128, FC, 512], BF16) for i in range(2)]
    wq_s = [p.sbuf(pre + f"wqs{i}", [128, FC, 128], BF16) for i in range(2)]
    wk_s = [p.sbuf(pre + f"wks{i}", [128, FC, 128], BF16) for i in range(2)]
    Wv_sb = p.sbuf(pre + "Wv", [128, FC, 1024], BF16)
    Wo_sb = p.sbuf(pre + "Wo", [128, FC, 1024], BF16)
    G1_sb = p.sbuf(pre + "G1", [128, FC, 64], BF16)
    G2_sb = p.sbuf(pre + "G2", [64, 1024], BF16)
    thD = p.sbuf(pre + "thD", [1, 2048], F32)
    phase = p.sbuf(pre + "phase", [128, 1], F32)
    gam_sb = p.sbuf(pre + "gam", [128, 1024], BF16)
    bet_sb = p.sbuf(pre + "bet", [128, 1024], BF16)
    B1 = p.sbuf(pre + "B1", [128, 16, 512], BF16)
    B2 = p.sbuf(pre + "B2", [128, 16, 512], BF16)
    carS = p.sbuf(pre + "carS", [128, 16, 8], F32)
    carC = p.sbuf(pre + "carC", [128, 16, 8], F32)
    tb_f = p.sbuf(pre + "tbf", [128, 512], F32)
    tb_g = p.sbuf(pre + "tbg", [128, 512], F32)
    tb_i = p.sbuf(pre + "tbi", [128, 512], I32)
    cs = p.sbuf(pre + "cs", [128, 512], BF16)
    tmp5 = p.sbuf(pre + "tmp5", [128, 512], BF16)
    qd = p.sbuf(pre + "qd", [128, 16, 512], BF16)
    kd = p.sbuf(pre + "kd", [128, 16, 512], BF16)
    ktm = p.sbuf(pre + "ktm", [128, 4, 16, 128], BF16)
    v_tm = p.sbuf(pre + "vtm", [128, 4, 1024], BF16)
    u_sb = p.sbuf(pre + "u", [64, 512], BF16)
    gate = p.sbuf(pre + "gate", [128, 4, 1024], BF16)
    snap = p.sbuf(pre + "snap", [128, 1024], BF16)
    stb = p.sbuf(pre + "stb", [128, 2, 4, 128], BF16)
    ofm = p.sbuf(pre + "ofm", [128, FC, 128], BF16)
    s0g = [p.sbuf(pre + f"s0g{i}", [128, 1024], F32) for i in range(2)]
    sout = [p.sbuf(pre + f"sout{i}", [128, 1024], F32) for i in range(2)]
    st = p.sbuf(pre + "st", [128, 8], F32)
    junk = p.sbuf(pre + "junk", [128, 1024], BF16)
    cpool = p.sbuf(pre + "cpool", [128, 2], F32)
    sb_dt = p.sbuf(pre + "sbdt", [1, 512], F32)
    sb_sup = p.sbuf(pre + "sbsup", [1, 8], F32)
    zrow = p.sbuf(pre + "zrow", [1, 512], BF16)

    ps_S = p.psum(pre + "psS", [128, 1024], F32)
    ps_of = p.psum(pre + "psof", [128, 1024], F32)
    ps_st = p.psum(pre + "psst", [128, 512], F32)
    ps_s0 = p.psum(pre + "pss0", [128, 512], F32)
    ps_w = p.psum(pre + "psw", [128, 512], F32)
    ps_tr = p.psum(pre + "pstr", [128, 512], BF16)

    ldr = p.sem(pre + "ldr")
    lds = p.sem(pre + "lds")
    ldg = p.sem(pre + "ldg")
    ldgp1 = [p.sem(pre + "ldgp1a"), p.sem(pre + "ldgp1b")]
    ldgr = p.sem(pre + "ldgr")
    so = p.sem(pre + "so")
    sv = p.sem(pre + "sv")
    sa = p.sem(pre + "sa")
    pe = p.sem(pre + "pe")

    TWO_PI = 6.283185307179586
    INV2PI = 1.0 / TWO_PI

    for ap_dst, src in [(ident[:], p.consts["IDENT"]),
                        (triu4[:], p.consts["TRIU4"]),
                        (Wv_sb[:], Wv_h.rearrange("(k pp) o -> pp k o", pp=128)),
                        (Wo_sb[:], Wo_h.rearrange("(k pp) o -> pp k o", pp=128)),
                        (G1_sb[:], G1_h.rearrange("(k pp) o -> pp k o", pp=128)),
                        (G2_sb[:], G2_h)]:
        nc.gpsimd.dma_start(out=ap_dst, in_=src).then_inc(ldgr.h, 16)
        ldgr.inc(16)
    for ap_dst, src in [(gam_sb[:], gam_h), (bet_sb[:], bet_h)]:
        nc.gpsimd.dma_start(out=ap_dst, in_=src).then_inc(ldgr.h, 16)
        ldgr.inc(16)
    for ap_dst, src in [(thD[:], thetaD_h), (phase[:], p.consts["PHASE"]),
                        (sb_dt[:], p.consts["DTIDX"]), (sb_sup[:], p.consts["SUPIDX"])]:
        nc.sync.dma_start(out=ap_dst, in_=src).then_inc(ldr.h, 16)
        ldr.inc(16)
    nc.vector.memset(cpool[:, 0:1], TWO_PI)
    nc.vector.drain()
    nc.vector.memset(zrow[:], 0.0)
    nc.vector.drain().then_inc(sv.h, 1)
    sv.inc()

    def psum_zero(ps_ap_list):
        for ap_ in ps_ap_list:
            nc.tensor.matmul(ap_, zrow[0:1, 0:128], zrow[0:1, :ap_.shape[-1]],
                             start=True, stop=False, skip_group_check=True)

    def fold_sin(dst_bf, src_ps_or_sb, extra_quarter, width, use_phase):
        w = width
        nc.vector.wait_ge(sa.h, sa.v)
        if use_phase:
            nc.vector.tensor_scalar(tb_f[:, :w], src_ps_or_sb, INV2PI, phase[:, 0:1], ALU.mult, ALU.add)
        else:
            nc.vector.tensor_scalar(tb_f[:, :w], src_ps_or_sb, INV2PI, None, ALU.mult)
        nc.vector.drain()
        if extra_quarter:
            nc.vector.tensor_scalar(tb_f[:, :w], tb_f[:, :w], 1.0, 0.25, ALU.mult, ALU.add)
            nc.vector.drain()
        nc.vector.tensor_copy(tb_i[:, :w], tb_f[:, :w])
        nc.vector.drain()
        nc.vector.tensor_copy(tb_g[:, :w], tb_i[:, :w])
        nc.vector.drain()
        nc.vector.tensor_tensor(tb_f[:, :w], tb_f[:, :w], tb_g[:, :w], ALU.subtract)
        nc.vector.drain()
        nc.vector.tensor_scalar(tb_g[:, :w], tb_f[:, :w], 0.5, None, ALU.is_gt)
        nc.vector.drain()
        nc.vector.scalar_tensor_tensor(tb_f[:, :w], tb_g[:, :w], -1.0, tb_f[:, :w], ALU.mult, ALU.add)
        nc.vector.drain()
        nc.vector.tensor_scalar(tb_g[:, :w], tb_f[:, :w], -0.5, None, ALU.is_lt)
        nc.vector.drain()
        nc.vector.tensor_tensor(tb_f[:, :w], tb_f[:, :w], tb_g[:, :w], ALU.add)
        nc.vector.drain().then_inc(sv.h, 1)
        sv.inc()
        nc.scalar.wait_ge(sv.h, sv.v)
        nc.scalar.activation(dst_bf, tb_f[:, :w], AF.Sin, bias=0.0, scale=cpool[:, 0:1])
        nc.scalar.drain().then_inc(sa.h, 1)
        sa.inc()

    nc.tensor.wait_ge(ldr.h, ldr.v)
    nc.tensor.wait_ge(ldgr.h, ldgr.v)
    nc.tensor.wait_ge(sv.h, sv.v)
    for h in range(16):
        nc.tensor.wait_ge(sv.h, sv.v)
        nc.tensor.matmul(ps_w[:], thD[0:1, h * 128:(h + 1) * 128], sb_dt[0:1, :],
                         start=True, stop=True)
        nc.tensor.drain().then_inc(pe.h, 1)
        pe.inc()
        nc.vector.wait_ge(pe.h, pe.v)
        fold_sin(B1[:, h], ps_w[:], False, 512, True)
        nc.vector.wait_ge(sa.h, sa.v)
        fold_sin(B2[:, h], ps_w[:], True, 512, True)
        nc.tensor.wait_ge(sa.h, sa.v)
        nc.tensor.matmul(ps_w[:, 0:8], thD[0:1, h * 128:(h + 1) * 128], sb_sup[0:1, :],
                         start=True, stop=True)
        nc.tensor.drain().then_inc(pe.h, 1)
        pe.inc()
        nc.vector.wait_ge(pe.h, pe.v)
        fold_sin(carS[:, h], ps_w[:, 0:8], False, 8, False)
        nc.vector.wait_ge(sa.h, sa.v)
        fold_sin(carC[:, h], ps_w[:, 0:8], True, 8, False)


    def barrier():
        nc.tensor.drain().then_inc(pe.h, 1)
        pe.inc()
        nc.vector.drain().then_inc(sv.h, 1)
        sv.inc()
        nc.scalar.drain().then_inc(sa.h, 1)
        sa.inc()
        nc.tensor.wait_ge(sv.h, sv.v)
        nc.tensor.wait_ge(sa.h, sa.v)
        nc.vector.wait_ge(pe.h, pe.v)
        nc.vector.wait_ge(sa.h, sa.v)
        nc.scalar.wait_ge(pe.h, pe.v)
        nc.scalar.wait_ge(sv.h, sv.v)

    WqD_r = WqD_h.rearrange("(k pp) m -> pp k m", pp=128)
    WkD_r = WkD_h.rearrange("(k pp) m -> pp k m", pp=128)
    proj_done_pe = {}

    wqk_tgt = {}

    def issue_wqk(m):
        h = m % 16
        bw = m % 2
        if m - 2 in proj_done_pe:
            nc.gpsimd.wait_ge(pe.h, proj_done_pe[m - 2])
        nc.gpsimd.dma_start(out=wq_s[bw][:], in_=WqD_r[:, :, h * 128:(h + 1) * 128]).then_inc(ldgp1[bw].h, 16)
        nc.gpsimd.dma_start(out=wk_s[bw][:], in_=WkD_r[:, :, h * 128:(h + 1) * 128]).then_inc(ldgp1[bw].h, 16)
        ldgp1[bw].inc(32)
        wqk_tgt[m] = ldgp1[bw].v

    x_sr = xnfm_h.rearrange("k p t -> p k t")
    s0_r = s0_h.rearrange("(c pp) d -> c pp d", pp=128)
    def emit_s5b(t4, ci):
        c_idx = ci
        bo = c_idx % 2
        bos = ci % 2
        nc.vector.reduce_sum(st[:, 0:1], s0g[bos][:], axis=AX.X)
        nc.vector.drain().then_inc(sv.h, 1)
        sv.inc()
        nc.scalar.wait_ge(sv.h, sv.v)
        nc.scalar.activation(junk[:], s0g[bos][:], AF.Square, accum_out=st[:, 1:2])
        nc.scalar.drain().then_inc(sa.h, 1)
        sa.inc()
        nc.vector.wait_ge(sa.h, sa.v)
        nc.vector.tensor_scalar(st[:, 2:3], st[:, 0:1], 1.0 / D, None, ALU.mult)
        nc.vector.tensor_scalar(st[:, 3:4], st[:, 1:2], 1.0 / D, None, ALU.mult)
        nc.vector.drain()
        nc.vector.scalar_tensor_tensor(st[:, 4:5], st[:, 2:3], 1.0, st[:, 2:3], ALU.mult, ALU.mult)
        nc.vector.drain()
        nc.vector.tensor_tensor(st[:, 4:5], st[:, 3:4], st[:, 4:5], ALU.subtract)
        nc.vector.drain()
        nc.vector.tensor_scalar(st[:, 4:5], st[:, 4:5], 1.0, EPS_LN, ALU.mult, ALU.add)
        nc.vector.drain().then_inc(sv.h, 1)
        sv.inc()
        nc.scalar.wait_ge(sv.h, sv.v)
        nc.scalar.activation(st[:, 5:6], st[:, 4:5], AF.Sqrt)
        nc.scalar.drain().then_inc(sa.h, 1)
        sa.inc()
        nc.vector.wait_ge(sa.h, sa.v)
        nc.vector.reciprocal(st[:, 6:7], st[:, 5:6])
        nc.vector.drain()
        if c_idx >= 2:
            nc.vector.wait_ge(so.h, so.v)
        nc.vector.tensor_scalar(sout[bo][:], s0g[bos][:], st[:, 2:3], st[:, 6:7],
                        ALU.subtract, ALU.mult)
        nc.vector.drain()
        nc.vector.tensor_tensor(sout[bo][:], sout[bo][:], gam_sb[:], ALU.mult)
        nc.vector.drain()
        nc.vector.tensor_tensor(sout[bo][:], sout[bo][:], bet_sb[:], ALU.add)
        nc.vector.drain().then_inc(sv.h, 1)
        sv.inc()
        nc.sync.wait_ge(sv.h, sv.v)
        nc.sync.dma_start(out=s0_r[c_idx], in_=sout[bo][:]).then_inc(so.h, 16)
        so.inc(16)

    m_next = 0
    xn_used_pe = [None, None]
    ci_global = 0
    s5_pending = None
    nc.tensor.wait_ge(sv.h, sv.v)
    psum_zero([ps_S[:, 0:512], ps_S[:, 512:1024]])
    for sc in range(NSC):
        barrier()
        bx = sc % 2
        if xn_used_pe[bx] is not None:
            nc.sync.wait_ge(pe.h, xn_used_pe[bx])
        nc.sync.dma_start(out=xn[bx][:], in_=x_sr[:, :, sc * 512:(sc + 1) * 512]).then_inc(lds.h, 16)
        lds.inc(16)
        while m_next < sc * 16 + 2:
            issue_wqk(m_next)
            m_next += 1
        nc.tensor.wait_ge(lds.h, lds.v)
        evac_done = {}
        tr_done = {}

        def emit_lrpe(h):
            nc.vector.wait_ge(sa.h, evac_done[h % 2])
            nc.vector.tensor_scalar(tmp5[:], B2[:, h], carS[:, h, sc:sc + 1], None, ALU.mult)
            nc.vector.drain()
            nc.vector.scalar_tensor_tensor(cs[:], B1[:, h], carC[:, h, sc:sc + 1], tmp5[:],
                                           ALU.mult, ALU.add)
            nc.vector.drain()
            nc.vector.tensor_tensor(qd[:, h], qd[:, h], cs[:], ALU.mult)
            nc.vector.drain()
            nc.vector.tensor_tensor(kd[:, h], kd[:, h], cs[:], ALU.mult)
            nc.vector.drain().then_inc(sv.h, 1)
            sv.inc()
            return sv.v

        lrpe_done = {}

        def emit_ktr(h):
            nc.tensor.wait_ge(sv.h, lrpe_done[h])
            td = tr_done.get("prev")
            if td is not None:
                nc.tensor.wait_ge(sv.h, td)
            for t4_ in range(4):
                nc.tensor.transpose(ps_tr[:, t4_ * 128:(t4_ + 1) * 128],
                                    kd[:, h, t4_ * 128:(t4_ + 1) * 128], ident[:])
            nc.tensor.drain().then_inc(pe.h, 1)
            pe.inc()
            nc.vector.wait_ge(pe.h, pe.v)
            nc.vector.tensor_copy(ktm[:].rearrange("p t4 hh q -> p hh t4 q")[:, h],
                                  ps_tr[:].rearrange("p (t4 q) -> p t4 q", t4=4))
            nc.vector.drain().then_inc(sv.h, 1)
            sv.inc()
            tr_done["prev"] = sv.v

        for h in range(16):
            m = sc * 16 + h
            pq_ = ps_of[:, 0:512] if h % 2 == 0 else ps_s0[:]
            pk_ = ps_of[:, 512:1024] if h % 2 == 0 else ps_w[:]
            nc.tensor.wait_ge(ldgp1[m % 2].h, wqk_tgt[m])
            ed = evac_done.get(h % 2)
            if ed is not None:
                nc.tensor.wait_ge(sa.h, ed)
            for k in range(FC):
                nc.tensor.matmul(pq_, wq_s[m % 2][:, k], xn[bx][:, k],
                                 start=(k == 0), stop=(k == FC - 1))
            for k in range(FC):
                nc.tensor.matmul(pk_, wk_s[m % 2][:, k], xn[bx][:, k],
                                 start=(k == 0), stop=(k == FC - 1))
            nc.tensor.drain().then_inc(pe.h, 1)
            pe.inc()
            proj_done_pe[m] = pe.v
            nc.scalar.wait_ge(pe.h, pe.v)
            nc.scalar.activation(qd[:, h], pq_, AF.Silu)
            nc.scalar.drain()
            nc.scalar.activation(kd[:, h], pk_, AF.Silu)
            nc.scalar.drain().then_inc(sa.h, 1)
            sa.inc()
            evac_done[h % 2] = sa.v
            lrpe_done[h] = emit_lrpe(h)
            if h > 0:
                emit_ktr(h - 1)
            if m_next < NSC * 16 and m_next <= m + 2:
                issue_wqk(m_next)
                m_next += 1
        emit_ktr(15)
        nc.tensor.wait_ge(sa.h, sa.v)
        for t4 in range(4):
            for half in range(2):
                nc.tensor.wait_ge(sv.h, sv.v)
                for k in range(FC):
                    nc.tensor.matmul(ps_s0[:], xn[bx][:, k, t4 * 128:(t4 + 1) * 128],
                                     Wv_sb[:, k, half * 512:(half + 1) * 512],
                                     start=(k == 0), stop=(k == FC - 1))
                nc.tensor.drain().then_inc(pe.h, 1)
                pe.inc()
                nc.vector.wait_ge(pe.h, pe.v)
                nc.vector.tensor_copy(v_tm[:, t4, half * 512:(half + 1) * 512], ps_s0[:])
                nc.vector.drain().then_inc(sv.h, 1)
                sv.inc()
        nc.tensor.wait_ge(sv.h, sv.v)
        for k in range(FC):
            nc.tensor.matmul(ps_st[0:64, :], G1_sb[:, k], xn[bx][:, k],
                             start=(k == 0), stop=(k == FC - 1))
        nc.tensor.drain().then_inc(pe.h, 1)
        pe.inc()
        nc.vector.wait_ge(pe.h, pe.v)
        nc.vector.tensor_copy(u_sb[:], ps_st[0:64, :])
        nc.vector.drain().then_inc(sv.h, 1)
        sv.inc()
        nc.tensor.wait_ge(sv.h, sv.v)
        for t4 in range(4):
            for half in range(2):
                if not (t4 == 0 and half == 0):
                    nc.tensor.wait_ge(sa.h, sa.v)
                nc.tensor.matmul(ps_st[:, :], u_sb[:, t4 * 128:(t4 + 1) * 128],
                                 G2_sb[:, half * 512:(half + 1) * 512], start=True, stop=True)
                nc.tensor.drain().then_inc(pe.h, 1)
                pe.inc()
                nc.scalar.wait_ge(pe.h, pe.v)
                nc.scalar.activation(gate[:, t4, half * 512:(half + 1) * 512], ps_st[:, :], AF.Sigmoid)
                nc.scalar.drain().then_inc(sa.h, 1)
                sa.inc()
        for t4 in range(4):
            ci = ci_global
            ci_global += 1
            if ci > 0:
                nc.vector.wait_ge(pe.h, pe.v)
                nc.vector.tensor_copy(snap[:], ps_S[:])
                nc.vector.drain().then_inc(sv.h, 1)
                sv.inc()
            snap_sv = sv.v
            nc.tensor.wait_ge(sv.h, sv.v)
            nc.tensor.wait_ge(sa.h, sa.v)
            psum_zero([ps_of[:, 0:512], ps_of[:, 512:1024]])
            mask_done = {}

            def emit_st(hg):
                pst = ps_st if hg % 2 == 0 else ps_w
                md = mask_done.get(hg % 2)
                if md is not None:
                    nc.tensor.wait_ge(sv.h, md[1])
                psum_zero([pst[:]])
                for hh in range(4):
                    h = hg * 4 + hh
                    nc.tensor.matmul(pst[:, hh * 128:(hh + 1) * 128],
                                     kd[:, h, t4 * 128:(t4 + 1) * 128],
                                     qd[:, h, t4 * 128:(t4 + 1) * 128],
                                     start=False, stop=(hh == 3), skip_group_check=True)
                nc.tensor.drain().then_inc(pe.h, 1)
                pe.inc()
                nc.vector.wait_ge(pe.h, pe.v)
                nc.vector.tensor_tensor(stb[:, hg % 2], pst[:].rearrange("p (hh q) -> p hh q", hh=4),
                                        triu4[:], ALU.mult)
                nc.vector.drain().then_inc(sv.h, 1)
                sv.inc()
                mask_done[hg % 2] = (sv.v, sv.v)

            def emit_o(hg):
                nc.tensor.wait_ge(sv.h, mask_done[hg % 2][0])
                for hh in range(4):
                    h = hg * 4 + hh
                    colg = (0, 64) if (h % 2) == 1 else None
                    of_ap = ps_of[(h % 2) * 64:(h % 2) * 64 + 64, (h // 2) * 128:(h // 2) * 128 + 128]
                    if ci > 0:
                        nc.tensor.matmul(of_ap, snap[:, h * 64:(h + 1) * 64],
                                         qd[:, h, t4 * 128:(t4 + 1) * 128],
                                         start=False, stop=False, tile_position=colg,
                                         skip_group_check=True)
                    nc.tensor.matmul(of_ap, v_tm[:, t4, h * 64:(h + 1) * 64], stb[:, hg % 2, hh],
                                     start=False, stop=(h in (14, 15)), tile_position=colg,
                                     skip_group_check=True)
                    nc.tensor.matmul(ps_S[:, h * 64:(h + 1) * 64], ktm[:, t4, h, :],
                                     v_tm[:, t4, h * 64:(h + 1) * 64],
                                     start=False, stop=(ci == TC - 1 and h in (14, 15)),
                                     skip_group_check=True)

            prev_hg = None
            for hg in range(4):
                emit_st(hg)
                if prev_hg is not None:
                    emit_o(prev_hg)
                prev_hg = hg
            emit_o(prev_hg)
            nc.tensor.drain().then_inc(pe.h, 1)
            pe.inc()
            nc.vector.wait_ge(pe.h, pe.v)
            nc.vector.tensor_copy(ofm[:], ps_of[:].rearrange("p (m q) -> p m q", m=8))
            nc.vector.drain().then_inc(sv.h, 1)
            sv.inc()
            c_idx = ci
            bo = c_idx % 2
            nc.tensor.wait_ge(sv.h, sv.v)
            for half in range(2):
                if half == 1:
                    nc.tensor.wait_ge(sv.h, sv.v)
                for mc in range(FC):
                    nc.tensor.matmul(ps_s0[:], ofm[:, mc], Wo_sb[:, mc, half * 512:(half + 1) * 512],
                                     start=(mc == 0), stop=(mc == FC - 1))
                nc.tensor.drain().then_inc(pe.h, 1)
                pe.inc()
                nc.vector.wait_ge(pe.h, pe.v)
                nc.vector.tensor_tensor(s0g[ci % 2][:, half * 512:(half + 1) * 512], ps_s0[:],
                                        gate[:, t4, half * 512:(half + 1) * 512], ALU.mult)
                nc.vector.drain().then_inc(sv.h, 1)
                sv.inc()
            if s5_pending is not None:
                emit_s5b(*s5_pending)
            s5_pending = (t4, ci)
        xn_used_pe[bx] = pe.v
    if s5_pending is not None:
        emit_s5b(*s5_pending)
        s5_pending = None
    nc.sync.wait_ge(so.h, so.v)


def make_consts_p1():
    c = {}
    c["TRIU4"] = np.broadcast_to(
        np.triu(np.ones((128, 128), dtype=np.float32))[:, None, :], (128, 4, 128)).copy()
    c["PHASE"] = np.where(np.arange(128) < 64, 0.25, 0.0).astype(np.float32).reshape(128, 1)
    c["DTIDX"] = np.arange(512, dtype=np.float32).reshape(1, 512)
    c["SUPIDX"] = (np.arange(8, dtype=np.float32) * 512.0).reshape(1, 8)
    return c


def prep_p1(inputs):
    f = lambda a: np.ascontiguousarray(np.asarray(a, dtype=np.float32))
    d = {}
    Wq = np.asarray(inputs["L0_attn_Wq"], dtype=np.float32)
    Wk = np.asarray(inputs["L0_attn_Wk"], dtype=np.float32)
    idx = (np.arange(2048) // 128) * 64 + (np.arange(2048) % 64)
    d["WqD"] = f(Wq[:, idx])
    d["WkD"] = f(Wk[:, idx])
    d["Wv"] = f(inputs["L0_attn_Wv"])
    d["Wo"] = f(inputs["L0_attn_Wo"])
    d["G1"] = f(inputs["L0_attn_G1"])
    d["G2"] = f(inputs["L0_attn_G2"])
    th = np.asarray(inputs["L0_attn_theta"], dtype=np.float32).reshape(16, 64)
    thD = th[np.arange(2048) // 128, np.arange(2048) % 64]
    d["thetaD"] = f(thD.reshape(1, 2048))
    d["gam"] = f(np.broadcast_to(np.asarray(inputs["L0_attn_gamma"], dtype=np.float32), (128, D)))
    d["bet"] = f(np.broadcast_to(np.asarray(inputs["L0_attn_beta"], dtype=np.float32), (128, D)))
    return d


def build_p1():
    p = Prog()
    io = {
        "xnfm": p.hbm("xnfm", [FC, 128, T], BF16),
        "WqD": p.hbm("WqD", [D, 2048], F32),
        "WkD": p.hbm("WkD", [D, 2048], F32),
        "Wv": p.hbm("Wv", [D, D], F32),
        "Wo": p.hbm("Wo", [D, D], F32),
        "G1": p.hbm("G1", [D, 64], F32),
        "G2": p.hbm("G2", [64, D], F32),
        "thetaD": p.hbm("thetaD", [1, 2048], F32),
        "gam": p.hbm("gam", [128, D], F32),
        "bet": p.hbm("bet", [128, D], F32),
        "s0": p.hbm("s0", [T, D], F32, kind="ExternalOutput"),
    }
    cst = make_consts_p1()
    p.consts = {
        "IDENT": p.hbm("IDENT", [128, 128], BF16),
        "TRIU4": p.hbm("TRIU4", [128, 4, 128], F32),
        "PHASE": p.hbm("PHASE", [128, 1], F32),
        "DTIDX": p.hbm("DTIDX", [1, 512], F32),
        "SUPIDX": p.hbm("SUPIDX", [1, 8], F32),
    }
    emit_p1(p, io)
    p.done()
    return p.nc


ALIBI_C_SAFE = 12.0
SQRT_F = float(np.sqrt(192.0))


def _slopes16():
    return [8.0 * 2.0 ** (-(h + 1) / 2.0) for h in range(16)]


def _win_chunks(h):
    s8 = _slopes16()[h]
    return max(1, int(np.ceil(30.0 / (128.0 * s8)))) + 1


def emit_p3(p, io, pre="p3_"):
    nc = p.nc
    xnfm1_h, s1_h = io["xnfm1"], io["s1"]
    Wq1_h, Wk1_h, Wv1_h = io["Wq1"], io["Wk1"], io["Wv1"]
    bq_h, bk_h, bv_h, bout_h = io["bq1"], io["bk1"], io["bv1"], io["bout"]
    WoutF_h = io["WoutF"]
    COS_h, SINF_h = io["COS"], io["SINF"]
    qfm_h, kfm_h, vaug_h = io["qfm"], io["kfm"], io["vaug"]

    es1 = ExitStack()
    sb1 = lambda n, shp, dt=F32: es1.enter_context(nc.sbuf_tensor(pre + n, shp, dt))
    xnfm = sb1("xnfm", [128, FC, T], BF16)
    Wq_sb = sb1("Wq", [128, FC, 1024], BF16)
    Wk_sb = sb1("Wk", [128, FC, 1024], BF16)
    Wv_sb = sb1("Wv", [128, FC, 1024], BF16)
    brow = sb1("brow", [1, 3, 1024], BF16)
    onesr = sb1("onesr", [2, 512], BF16)
    ident1 = sb1("ident1", [128, 128], BF16)
    rtab = [sb1(f"rtab{i}", [128, 2, 1024], BF16) for i in range(2)]
    q_tm = sb1("qtm", [128, 1024], F32)
    k_tm = sb1("ktm", [128, 1024], F32)
    tmpr = sb1("tmpr", [128, 1024], F32)
    qp_bf = sb1("qpbf", [128, 2, 1024], BF16)
    kp_bf = sb1("kpbf", [128, 2, 1024], BF16)
    stg1 = [sb1(f"stg1{i}", [128, 2, FC, 128], BF16) for i in range(2)]
    vstg = [sb1(f"vstg{i}", [128, 16, 65], BF16) for i in range(2)]
    ps_q = [es1.enter_context(nc.psum_tensor(pre + f"psq{i}", [128, 512], F32)) for i in range(2)]
    ps_k = [es1.enter_context(nc.psum_tensor(pre + f"psk{i}", [128, 512], F32)) for i in range(2)]
    ps_v = [es1.enter_context(nc.psum_tensor(pre + f"psv{i}", [128, 512], F32)) for i in range(2)]
    ps_t1 = es1.enter_context(nc.psum_tensor(pre + "pst1", [128, 512], BF16))
    ps_t2 = es1.enter_context(nc.psum_tensor(pre + "pst2", [128, 512], BF16))

    ldr = p.sem(pre + "ldr")
    lds = p.sem(pre + "lds")
    so = p.sem(pre + "so")
    sv = p.sem(pre + "sv")
    sa = p.sem(pre + "sa")
    pe = p.sem(pre + "pe")
    sg = p.sem(pre + "sg")

    ldrq = [p.sem(pre + f"ldrq{i}") for i in range(3)]
    xn_r = xnfm1_h.rearrange("k p t -> p k t")
    if P3_SPLIT_LOADS:
        batches = (
            (nc.gpsimd, ldr, [(Wq_sb[:], Wq1_h.rearrange("(k pp) o -> pp k o", pp=128)),
                              (Wk_sb[:], Wk1_h.rearrange("(k pp) o -> pp k o", pp=128)),
                              (Wv_sb[:], Wv1_h.rearrange("(k pp) o -> pp k o", pp=128)),
                              (brow[0:1, 0], bq_h), (brow[0:1, 1], bk_h), (brow[0:1, 2], bv_h)]),
            (nc.sync, ldrq[0], [(xnfm[:, :, 0:2048], xn_r[:, :, 0:2048]),
                                (ident1[:], p.consts["IDENT"]),
                                (onesr[:], p.consts["ONES2"])]),
            (nc.scalar, ldrq[1], [(xnfm[:, :, 2048:4096], xn_r[:, :, 2048:4096])]),
        )
    else:
        batches = (
            (nc.gpsimd, ldr, [(xnfm[:], xn_r),
                              (Wq_sb[:], Wq1_h.rearrange("(k pp) o -> pp k o", pp=128)),
                              (Wk_sb[:], Wk1_h.rearrange("(k pp) o -> pp k o", pp=128)),
                              (Wv_sb[:], Wv1_h.rearrange("(k pp) o -> pp k o", pp=128)),
                              (ident1[:], p.consts["IDENT"]),
                              (brow[0:1, 0], bq_h), (brow[0:1, 1], bk_h), (brow[0:1, 2], bv_h),
                              (onesr[:], p.consts["ONES2"])]),
        )
    for q_eng, q_sem, pairs in batches:
        for dst, src in pairs:
            q_eng.dma_start(out=dst, in_=src).then_inc(q_sem.h, 16)
            q_sem.inc(16)

    for eng in (nc.tensor, nc.scalar, nc.vector, nc.gpsimd):
        eng.wait_ge(ldr.h, ldr.v)
        for qs in ldrq:
            eng.wait_ge(qs.h, qs.v)
    rt_used = [None, None]
    evq = [None, None]
    evk = [None, None]
    evv = [None, None]
    evt1 = [None]
    evt2 = [None]
    rope_done = [None, None]
    trq_done = [None, None]
    pend_tr = None
    for tc in range(TC):
        bb = tc % 2
        if rt_used[bb] is not None:
            nc.sync.wait_ge(sv.h, rt_used[bb])
        nc.sync.dma_start(out=rtab[bb][:, 0], in_=COS_h[tc * 128:(tc + 1) * 128, :]).then_inc(lds.h, 16)
        nc.sync.dma_start(out=rtab[bb][:, 1], in_=SINF_h[tc * 128:(tc + 1) * 128, :]).then_inc(lds.h, 16)
        lds.inc(32)
        lds_tgt = lds.v
        for tgt, W, ps_, bi, ev in ((q_tm, Wq_sb, ps_q, 0, evq), (k_tm, Wk_sb, ps_k, 1, evk)):
            for half in range(2):
                if ev[half] is not None:
                    nc.tensor.wait_ge(sv.h, ev[half])
                for k in range(FC):
                    nc.tensor.matmul(ps_[half][:], xnfm[:, k, tc * 128:(tc + 1) * 128],
                                     W[:, k, half * 512:(half + 1) * 512],
                                     start=(k == 0), stop=False)
                nc.tensor.matmul(ps_[half][:], onesr[0:1, 0:128], brow[0:1, bi, half * 512:(half + 1) * 512],
                                 start=False, stop=True)
                nc.tensor.drain().then_inc(pe.h, 1)
                pe.inc()
                nc.vector.wait_ge(pe.h, pe.v)
                nc.vector.tensor_copy(tgt[:, half * 512:(half + 1) * 512], ps_[half][:])
                nc.vector.drain().then_inc(sv.h, 1)
                sv.inc()
                ev[half] = sv.v
        bv_ = vstg[bb]
        if tc >= 2:
            nc.vector.wait_ge(so.h, so.v)
        for half in range(2):
            if evv[half] is not None:
                nc.tensor.wait_ge(sv.h, evv[half])
            for k in range(FC):
                nc.tensor.matmul(ps_v[half][:], xnfm[:, k, tc * 128:(tc + 1) * 128],
                                 Wv_sb[:, k, half * 512:(half + 1) * 512],
                                 start=(k == 0), stop=False)
            nc.tensor.matmul(ps_v[half][:], onesr[0:1, 0:128], brow[0:1, 2, half * 512:(half + 1) * 512],
                             start=False, stop=True)
            nc.tensor.drain().then_inc(pe.h, 1)
            pe.inc()
            nc.vector.wait_ge(pe.h, pe.v)
            nc.vector.tensor_copy(bv_[:, half * 8:(half + 1) * 8, 0:64],
                                  ps_v[half][:].rearrange("p (g d) -> p g d", d=64))
            nc.vector.drain().then_inc(sv.h, 1)
            sv.inc()
            evv[half] = sv.v
        nc.vector.memset(bv_[:, :, 64:65], 1.0)
        nc.vector.drain().then_inc(sv.h, 1)
        sv.inc()
        nc.sync.wait_ge(sv.h, sv.v)
        nc.sync.dma_start(out=vaug_h[:, tc], in_=bv_[:]).then_inc(so.h, 16)
        so.inc(16)
        for ti_, (src_t, dst_bf) in enumerate(((q_tm, qp_bf), (k_tm, kp_bf))):
            if ti_ == 0:
                nc.vector.wait_ge(lds.h, lds_tgt)
            if trq_done[ti_] is not None:
                nc.vector.wait_ge(pe.h, trq_done[ti_])
            a3 = src_t[:].rearrange("p (h j) -> p h j", j=64)
            t3 = tmpr[:].rearrange("p (h j) -> p h j", j=64)
            s3 = rtab[bb][:, 1].rearrange("p (h j) -> p h j", j=64)
            nc.vector.tensor_tensor(t3[:, :, 0:32], a3[:, :, 32:64], s3[:, :, 0:32], ALU.mult)
            nc.vector.drain()
            nc.vector.tensor_tensor(t3[:, :, 32:64], a3[:, :, 0:32], s3[:, :, 32:64], ALU.mult)
            nc.vector.drain()
            nc.vector.tensor_tensor(src_t[:], src_t[:], rtab[bb][:, 0], ALU.mult)
            nc.vector.drain()
            nc.vector.tensor_tensor(dst_bf[:, bb], src_t[:], tmpr[:], ALU.add)
            nc.vector.drain().then_inc(sv.h, 1)
            sv.inc()
            rope_done[ti_] = sv.v
        rt_used[bb] = sv.v

        def emit_tr_store(ptc, rdq, rdk):
            pb = ptc % 2
            sgb = stg1[pb]
            for ti, (src_bf, pst, evt_, rd) in enumerate(((qp_bf, ps_t1, evt1, rdq),
                                                          (kp_bf, ps_t2, evt2, rdk))):
                nc.tensor.wait_ge(sv.h, rd)
                for g in range(2):
                    if evt_[0] is not None:
                        nc.tensor.wait_ge(sv.h, evt_[0])
                    for k in range(4):
                        kk = g * 4 + k
                        nc.tensor.transpose(pst[:, k * 128:(k + 1) * 128],
                                            src_bf[:, pb, kk * 128:(kk + 1) * 128], ident1[:])
                    nc.tensor.drain().then_inc(pe.h, 1)
                    pe.inc()
                    nc.vector.wait_ge(pe.h, pe.v)
                    nc.vector.tensor_copy(sgb[:, ti, g * 4:(g + 1) * 4, :],
                                          pst[:].rearrange("p (k t) -> p k t", k=4))
                    nc.vector.drain().then_inc(sv.h, 1)
                    sv.inc()
                    evt_[0] = sv.v
                trq_done[ti] = pe.v
            nc.sync.wait_ge(sv.h, sv.v)
            nc.sync.dma_start(out=qfm_h[:, :, ptc * 128:(ptc + 1) * 128].rearrange("k p t -> p k t"),
                              in_=sgb[:, 0]).then_inc(so.h, 16)
            nc.sync.dma_start(out=kfm_h[:, :, ptc * 128:(ptc + 1) * 128].rearrange("k p t -> p k t"),
                              in_=sgb[:, 1]).then_inc(so.h, 16)
            so.inc(32)

        if pend_tr is not None:
            emit_tr_store(*pend_tr)
        pend_tr = (tc, rope_done[0], rope_done[1])
    emit_tr_store(*pend_tr)
    nc.sync.wait_ge(so.h, so.v)
    nc.sync.wait_ge(lds.h, lds.v)
    nc.tensor.drain().then_inc(pe.h, 1)
    pe.inc()
    nc.vector.drain().then_inc(sv.h, 1)
    sv.inc()
    nc.scalar.drain().then_inc(sa.h, 1)
    sa.inc()
    nc.gpsimd.drain().then_inc(sg.h, 1)
    sg.inc()
    for eng in (nc.tensor, nc.vector, nc.scalar, nc.gpsimd, nc.sync):
        eng.wait_ge(pe.h, pe.v)
        eng.wait_ge(sv.h, sv.v)
        eng.wait_ge(sa.h, sa.v)
        eng.wait_ge(sg.h, sg.v)
        eng.wait_ge(so.h, so.v)
    es1.close()

    es2 = ExitStack()
    sb2 = lambda n, shp, dt=F32: es2.enter_context(nc.sbuf_tensor(pre + n, shp, dt))
    kfm = sb2("kfm2", [128, FC, T], BF16)
    vaug = sb2("vaug2", [128, TC, 16, 65], BF16)
    Wout_sb = sb2("Wout", [128, FC, 1024], BF16)
    bo_row = sb2("borow", [1, 1024], BF16)
    ones2 = sb2("ones2b", [2, 512], BF16)
    ident2 = sb2("ident2", [128, 128], BF16)
    arel = sb2("arel", [128, 35, 16], F32)
    rneg = sb2("rneg", [2, 16, 512], BF16)
    mcorn = sb2("mcorn", [128, 128], F32)
    cpool2 = sb2("cpool2", [128, 1], F32)
    qt = [sb2(f"qt{i}", [128, FC, 512], BF16) for i in range(2)]
    pt_sb = sb2("ptsb", [128, 3, 512], BF16)
    corn_sb = sb2("cornsb", [128, 3, 128], F32)
    att = sb2("att", [128, 4, 1024], BF16)
    afm = sb2("afm", [128, FC, 128], BF16)
    osb = sb2("osb", [128, 4, 66], F32)
    s1st = [sb2(f"s1st{i}", [128, 1024], F32) for i in range(2)]
    ps_sc = [es2.enter_context(nc.psum_tensor(pre + f"pssc{i}", [128, 512], F32)) for i in range(3)]
    _pv3 = [es2.enter_context(nc.psum_tensor(pre + f"pspv{i}", [128, 264], F32)) for i in range(2)]
    pvv = [[_pv3[b][:, sub * 66:sub * 66 + 65] for sub in range(4)] for b in range(2)]
    ps_s1 = [es2.enter_context(nc.psum_tensor(pre + f"pss1{i}", [128, 512], F32)) for i in range(2)]
    ps_tr2 = es2.enter_context(nc.psum_tensor(pre + "pstr2", [128, 512], BF16))
    zrow3 = sb2("zrow3", [1, 264], BF16)

    ld2 = p.sem(pre + "ld2")
    ldq = p.sem(pre + "ldq")
    ld2q = [p.sem(pre + f"ld2q{i}") for i in range(3)]

    kfm_r = kfm_h.rearrange("k p t -> p k t")
    if P3_SPLIT_LOADS:
        batches2 = (
            (nc.gpsimd, ld2, [(Wout_sb[:], WoutF_h.rearrange("(k pp) o -> pp k o", pp=128)),
                              (bo_row[:], bout_h)]),
            (nc.sync, ld2q[0], [(kfm[:, :, 0:2048], kfm_r[:, :, 0:2048]),
                                (vaug[:, 0:16], vaug_h[:, 0:16]),
                                (ident2[:], p.consts["IDENT"]),
                                (arel[:], p.consts["AREL"]),
                                (rneg[:], p.consts["RNEG"]),
                                (mcorn[:], p.consts["MCORN"]),
                                (ones2[:], p.consts["ONES2"])]),
            (nc.scalar, ld2q[1], [(kfm[:, :, 2048:4096], kfm_r[:, :, 2048:4096]),
                                  (vaug[:, 16:32], vaug_h[:, 16:32])]),
        )
    else:
        batches2 = (
            (nc.gpsimd, ld2, [(kfm[:], kfm_r),
                              (vaug[:], vaug_h),
                              (Wout_sb[:], WoutF_h.rearrange("(k pp) o -> pp k o", pp=128)),
                              (bo_row[:], bout_h),
                              (ident2[:], p.consts["IDENT"]),
                              (arel[:], p.consts["AREL"]),
                              (rneg[:], p.consts["RNEG"]),
                              (mcorn[:], p.consts["MCORN"]),
                              (ones2[:], p.consts["ONES2"])]),
        )
    for q_eng, q_sem, pairs in batches2:
        for dst, src in pairs:
            q_eng.dma_start(out=dst, in_=src).then_inc(q_sem.h, 16)
            q_sem.inc(16)
    nc.vector.memset(cpool2[:, 0:1], 1.0 / SQRT_F)
    nc.vector.drain()
    nc.vector.memset(zrow3[:], 0.0)
    nc.vector.drain().then_inc(sv.h, 1)
    sv.inc()
    for eng in (nc.tensor, nc.scalar, nc.vector):
        eng.wait_ge(ld2.h, ld2.v)
        eng.wait_ge(sv.h, sv.v)
        for qs in ld2q:
            eng.wait_ge(qs.h, qs.v)

    s1_r = s1_h.rearrange("(c pp) d -> c pp d", pp=128)
    qt_used = [None, None]
    osb_done = [None, None]
    s1_evac = [None, None]
    tr2_evac = [None]
    slopes = _slopes16()
    for tqt in range(8):
        bq_ = tqt % 2
        tq0 = tqt * 512
        if qt_used[bq_] is not None:
            nc.sync.wait_ge(pe.h, qt_used[bq_])
        nc.sync.dma_start(out=qt[bq_][:], in_=qfm_h[:, :, tq0:tq0 + 512].rearrange("k p t -> p k t")).then_inc(ldq.h, 16)
        ldq.inc(16)
        nc.tensor.wait_ge(ldq.h, ldq.v)
        exp_done, corn_done, pv_done = {}, {}, {}
        for h in range(16):
            hb, hfc = (h % 2) * 64, h // 2
            hpv = h % 2
            nA = min(4 * tqt, _win_chunks(h))
            tkcs = list(range(4 * tqt - nA, 4 * tqt)) + [4 * tqt + r for r in range(4)]
            pv_zeroed = [False]

            use_rneg = (h <= 10) or not P3_RNEG_SKIP

            def emit_scores(tkc):
                r = tkc - 4 * tqt
                col0 = max(0, r) * 128
                bp = tkc % 3
                fe = exp_done.get(bp)
                if fe is not None:
                    nc.tensor.wait_ge(sa.h, fe)
                    cd = corn_done.get(bp)
                    if cd is not None:
                        nc.tensor.wait_ge(sv.h, cd)
                nc.tensor.matmul(ps_sc[bp][:, col0:512],
                                 kfm[hb:hb + 64, hfc, tkc * 128:(tkc + 1) * 128],
                                 qt[bq_][hb:hb + 64, hfc, col0:512],
                                 start=True, stop=not use_rneg)
                if use_rneg:
                    nc.tensor.matmul(ps_sc[bp][:, col0:512], ones2[:, 0:128],
                                     rneg[:, h, col0:512], start=False, stop=True)
                nc.tensor.drain().then_inc(pe.h, 1)
                pe.inc()

            def emit_exp(tkc):
                r = tkc - 4 * tqt
                col0 = max(0, r) * 128
                bp = tkc % 3
                dC = tkc - 4 * tqt + 31
                bias_col = arel[:, dC, h:h + 1]
                nc.scalar.wait_ge(pe.h, pe.v)
                fp_ = pv_done.get(bp)
                if fp_ is not None:
                    nc.scalar.wait_ge(pe.h, fp_)
                if r >= 0:
                    nc.vector.wait_ge(pe.h, pe.v)
                    nc.vector.tensor_tensor(corn_sb[:, bp], ps_sc[bp][:, col0:col0 + 128], mcorn[:], ALU.add)
                    nc.vector.drain().then_inc(sv.h, 1)
                    sv.inc()
                    corn_done[bp] = sv.v
                    nc.scalar.wait_ge(sv.h, sv.v)
                    nc.scalar.activation(pt_sb[:, bp, col0:col0 + 128], corn_sb[:, bp], AF.Exp,
                                         bias=bias_col, scale=cpool2[:, 0:1])
                    if col0 + 128 < 512:
                        nc.scalar.activation(pt_sb[:, bp, col0 + 128:512], ps_sc[bp][:, col0 + 128:512],
                                             AF.Exp, bias=bias_col, scale=cpool2[:, 0:1])
                else:
                    nc.scalar.activation(pt_sb[:, bp], ps_sc[bp][:], AF.Exp,
                                         bias=bias_col, scale=cpool2[:, 0:1])
                nc.scalar.drain().then_inc(sa.h, 1)
                sa.inc()
                exp_done[bp] = sa.v

            def emit_pv(tkc):
                r = tkc - 4 * tqt
                bp = tkc % 3
                if not pv_zeroed[0]:
                    if osb_done[hpv] is not None:
                        nc.tensor.wait_ge(sv.h, osb_done[hpv])
                    nc.tensor.matmul(_pv3[hpv][:], zrow3[0:1, 0:128], zrow3[0:1, 0:264],
                                     start=True, stop=False, skip_group_check=True)
                    pv_zeroed[0] = True
                nc.tensor.wait_ge(sa.h, exp_done[bp])
                for sub in range(max(0, r), 4):
                    nc.tensor.matmul(pvv[hpv][sub], pt_sb[:, bp, sub * 128:(sub + 1) * 128],
                                     vaug[:, tkc, h, :], start=False, stop=False,
                                     skip_group_check=True)
                nc.tensor.drain().then_inc(pe.h, 1)
                pe.inc()
                pv_done[bp] = pe.v

            for idx, tkc in enumerate(tkcs):
                emit_scores(tkc)
                emit_exp(tkc)
                if idx > 0:
                    emit_pv(tkcs[idx - 1])
            emit_pv(tkcs[-1])
            nc.tensor.drain().then_inc(pe.h, 1)
            pe.inc()
            nc.vector.wait_ge(pe.h, pe.v)
            nc.vector.tensor_copy(osb[:].rearrange("p s o -> p (s o)"), _pv3[hpv][:])
            nc.vector.drain()
            nc.vector.reciprocal(osb[:, 0:4, 65:66].rearrange("p s o -> p (s o)"),
                                 osb[:, 0:4, 64:65].rearrange("p s o -> p (s o)"))
            nc.vector.drain()
            for sub in range(4):
                nc.vector.tensor_scalar(att[:, sub, h * 64:(h + 1) * 64], osb[:, sub, 0:64],
                                        osb[:, sub, 65:66], None, ALU.mult)
            nc.vector.drain().then_inc(sv.h, 1)
            sv.inc()
            osb_done[hpv] = sv.v
        qt_used[bq_] = pe.v
        att_sv = sv.v
        for sub in range(4):
            c_idx = tqt * 4 + sub
            bo_ = sub % 2
            nc.tensor.wait_ge(sv.h, att_sv)
            for g in range(2):
                if tr2_evac[0] is not None:
                    nc.tensor.wait_ge(sv.h, tr2_evac[0])
                for k in range(4):
                    kk = g * 4 + k
                    nc.tensor.transpose(ps_tr2[:, k * 128:(k + 1) * 128],
                                        att[:, sub, kk * 128:(kk + 1) * 128], ident2[:])
                nc.tensor.drain().then_inc(pe.h, 1)
                pe.inc()
                nc.vector.wait_ge(pe.h, pe.v)
                nc.vector.tensor_copy(afm[:, g * 4:(g + 1) * 4, :],
                                      ps_tr2[:].rearrange("p (k t) -> p k t", k=4))
                nc.vector.drain().then_inc(sv.h, 1)
                sv.inc()
                tr2_evac[0] = sv.v
            afm_sv = sv.v
            for half in range(2):
                nc.tensor.wait_ge(sv.h, afm_sv)
                if s1_evac[half] is not None:
                    nc.tensor.wait_ge(sv.h, s1_evac[half])
                for fc in range(FC):
                    nc.tensor.matmul(ps_s1[half][:], afm[:, fc, :],
                                     Wout_sb[:, fc, half * 512:(half + 1) * 512],
                                     start=(fc == 0), stop=False)
                nc.tensor.matmul(ps_s1[half][:], ones2[0:1, 0:128], bo_row[0:1, half * 512:(half + 1) * 512],
                                 start=False, stop=True)
                nc.tensor.drain().then_inc(pe.h, 1)
                pe.inc()
                nc.vector.wait_ge(pe.h, pe.v)
                if half == 0 and c_idx >= 2:
                    nc.vector.wait_ge(so.h, so.v)
                nc.vector.tensor_copy(s1st[bo_][:, half * 512:(half + 1) * 512], ps_s1[half][:])
                nc.vector.drain().then_inc(sv.h, 1)
                sv.inc()
                s1_evac[half] = sv.v
            nc.sync.wait_ge(sv.h, sv.v)
            nc.sync.dma_start(out=s1_r[c_idx], in_=s1st[bo_][:]).then_inc(so.h, 16)
            so.inc(16)
    nc.sync.wait_ge(so.h, so.v)
    es2.close()


def make_consts_p3():
    c = {}
    slopes = np.array(_slopes16(), dtype=np.float64)
    C = ALIBI_C_SAFE
    p_ = np.arange(128)[:, None, None]
    dC = np.arange(35)[None, :, None] - 31
    c["AREL"] = ((p_ + 128 * dC) * slopes[None, None, :] - C).astype(np.float32)
    import ml_dtypes
    v = -(np.arange(512)[None, :] * slopes[:, None] * np.sqrt(192.0))
    hi = v.astype(ml_dtypes.bfloat16).astype(np.float64)
    lo = (v - hi).astype(ml_dtypes.bfloat16)
    c["RNEG"] = np.stack([hi.astype(ml_dtypes.bfloat16), lo], axis=0).reshape(2, 16, 512)
    c["MCORN"] = np.where(np.arange(128)[:, None] <= np.arange(128)[None, :],
                          0.0, -1.39e7).astype(np.float32)
    c["ONES2"] = np.ones((2, 512), dtype=ml_dtypes.bfloat16)
    return c


def _rope_tables():
    import ml_dtypes
    dh = 64
    inv_freq = 1.0 / 10000.0 ** (np.arange(0, dh, 2, dtype=np.float32) / dh)
    t = np.arange(T, dtype=np.float32)
    fr = np.outer(t, inv_freq)
    sin, cos = np.sin(fr), np.cos(fr)
    cosr = np.concatenate([cos, cos], axis=1)
    sinf = np.concatenate([-sin, sin], axis=1)
    COS = np.tile(cosr, (1, 16)).astype(ml_dtypes.bfloat16)
    SINF = np.tile(sinf, (1, 16)).astype(ml_dtypes.bfloat16)
    return COS, SINF


def prep_p3(inputs):
    f = lambda a: np.ascontiguousarray(np.asarray(a, dtype=np.float32))
    d = {}
    wq = np.asarray(inputs["L1_attn_w_qkv"], dtype=np.float32)
    d["Wq1"] = f(wq[:, :, 0:64].reshape(D, D))
    d["Wk1"] = f(wq[:, :, 64:128].reshape(D, D))
    d["Wv1"] = f(wq[:, :, 128:192].reshape(D, D))
    b = np.asarray(inputs["L1_attn_b_qkv"], dtype=np.float32)
    d["bq1"] = f(b[:, 0:64].reshape(1, D))
    d["bk1"] = f(b[:, 64:128].reshape(1, D))
    d["bv1"] = f(b[:, 128:192].reshape(1, D))
    d["WoutF"] = f(np.asarray(inputs["L1_attn_w_out"], dtype=np.float32).reshape(D, D))
    d["bout"] = f(np.asarray(inputs["L1_attn_b_out"], dtype=np.float32).reshape(1, D))
    return d


def build_p3():
    p = Prog()
    io = {
        "xnfm1": p.hbm("xnfm1", [FC, 128, T], BF16),
        "Wq1": p.hbm("Wq1", [D, D], F32),
        "Wk1": p.hbm("Wk1", [D, D], F32),
        "Wv1": p.hbm("Wv1", [D, D], F32),
        "bq1": p.hbm("bq1", [1, D], F32),
        "bk1": p.hbm("bk1", [1, D], F32),
        "bv1": p.hbm("bv1", [1, D], F32),
        "WoutF": p.hbm("WoutF", [D, D], F32),
        "bout": p.hbm("bout", [1, D], F32),
        "COS": p.hbm("COS", [T, D], BF16),
        "SINF": p.hbm("SINF", [T, D], BF16),
        "qfm": p.hbm("qfm", [FC, 128, T], BF16, kind="Internal"),
        "kfm": p.hbm("kfm", [FC, 128, T], BF16, kind="Internal"),
        "vaug": p.hbm("vaug", [128, TC, 1040], BF16, kind="Internal"),
        "s1": p.hbm("s1", [T, D], F32, kind="ExternalOutput"),
    }
    c3 = make_consts_p3()
    p.consts = {
        "IDENT": p.hbm("IDENT", [128, 128], BF16),
        "AREL": p.hbm("AREL", [128, 35, 16], F32),
        "RNEG": p.hbm("RNEG", [2, 16, 512], BF16),
        "MCORN": p.hbm("MCORN", [128, 128], F32),
        "ONES2": p.hbm("ONES2", [2, 512], BF16),
    }
    emit_p3(p, io)
    p.done()
    return p.nc


def emit_p4(p, io, pre="p4_"):
    nc = p.nc
    xnfm1_h, rrho_h = io["xnfm1"], io["rrho"]
    sim_h, gates_h = io["sim"], io["gates"]
    qc_h, kc_h, vc_h, op_h = io["qcat"], io["kcat"], io["vcat"], io["oproj"]
    x1_h, s1_h, out_h = io["x1"], io["s1"], io["out"]
    SQ128 = float(np.sqrt(128.0))
    BIG = 1.0e9

    xnfm = p.sbuf(pre + "xnfm", [128, FC, T], BF16)
    rr = p.sbuf(pre + "rr", [128, TC], F32)
    sim_sb = p.sbuf(pre + "sim", [128, FC, 8], BF16)
    simsq = p.sbuf(pre + "simsq", [128, FC, 8], BF16)
    onesc = p.sbuf(pre + "onesc", [128, 1], BF16)
    onesr = p.sbuf(pre + "onesr4", [1, 128], F32)
    grow = p.sbuf(pre + "grow", [1, 8], F32)
    rn8 = p.sbuf(pre + "rn8", [1, 8], F32)
    rnb = p.sbuf(pre + "rnb", [128, 8], F32)
    sgb = p.sbuf(pre + "sgb", [128, 8], F32)
    ident = p.sbuf(pre + "ident", [128, 128], BF16)
    mcorn = p.sbuf(pre + "mcorn", [128, 128], F32)
    cp = p.sbuf(pre + "cp", [128, 2], F32)
    rt = p.sbuf(pre + "rt", [128, 48], F32)
    mw_sb = p.sbuf(pre + "mw", [128, TC, 8], F32)
    mk_sb = p.sbuf(pre + "mk", [128, TC, 8], F32)
    qw_sb = p.sbuf(pre + "qw", [128, FC, 1024], BF16)
    kw_sb = p.sbuf(pre + "kw", [128, FC, 1024], BF16)
    vw_sb = p.sbuf(pre + "vw", [128, FC, 1024], BF16)
    ow_sb = p.sbuf(pre + "ow", [128, 8, 1024], BF16)
    acc = p.sbuf(pre + "acc", [128, 3, 128], F32)
    qfm4 = p.sbuf(pre + "qfm4", [128, TC, 128], BF16)
    kfm4 = p.sbuf(pre + "kfm4", [128, TC, 128], BF16)
    vaug4 = p.sbuf(pre + "vaug4", [128, TC, 132], BF16)
    qtbf = p.sbuf(pre + "qtbf", [128, 2, 128], BF16)
    pt4 = p.sbuf(pre + "pt4", [128, 2, 512], BF16)
    corn4 = p.sbuf(pre + "corn4", [128, 2, 128], F32)
    att4 = p.sbuf(pre + "att4", [128, 4, 132], F32)
    scat = p.sbuf(pre + "scat", [128, 2, 8, 128], BF16)
    afm8 = p.sbuf(pre + "afm8", [128, 2, 8, 128], BF16)
    g1 = p.sbuf(pre + "g1", [128, 1024], F32)
    zr4 = p.sbuf(pre + "zr4", [1, 264], BF16)
    xs = [p.sbuf(pre + f"xs{i}", [128, 1024], F32) for i in range(2)]
    ss = [p.sbuf(pre + f"ss{i}", [128, 1024], F32) for i in range(2)]
    ob = [p.sbuf(pre + f"ob{i}", [128, 1024], F32) for i in range(2)]

    ldr = p.sem(pre + "ldr")
    lds = p.sem(pre + "lds")
    so = p.sem(pre + "so")
    sv = p.sem(pre + "sv")
    sa = p.sem(pre + "sa")
    pe = p.sem(pre + "pe")

    ldrq4 = [p.sem(pre + f"ldrq4{i}") for i in range(3)]
    xn4_r = xnfm1_h.rearrange("k p t -> p k t")
    for q_eng, q_sem, pairs in (
        (nc.gpsimd, ldr, [(sim_sb[:], sim_h.rearrange("(k pp) e -> pp k e", pp=128)),
                          (qw_sb[:], qc_h.rearrange("(k pp) o -> pp k o", pp=128)),
                          (kw_sb[:], kc_h.rearrange("(k pp) o -> pp k o", pp=128)),
                          (vw_sb[:], vc_h.rearrange("(k pp) o -> pp k o", pp=128)),
                          (ow_sb[:], op_h.rearrange("e d o -> d e o"))]),
        (nc.sync, ldrq4[0], [(xnfm[:, :, 0:2048], xn4_r[:, :, 0:2048]),
                             (rr[:], rrho_h),
                             (grow[:], gates_h),
                             (mcorn[:], p.consts["MCORN"])]),
        (nc.scalar, ldrq4[1], [(xnfm[:, :, 2048:4096], xn4_r[:, :, 2048:4096]),
                               (ident[:], p.consts["IDENT"])]),
    ):
        for dst, src in pairs:
            q_eng.dma_start(out=dst, in_=src).then_inc(q_sem.h, 16)
            q_sem.inc(16)
    nc.vector.memset(cp[:, 0:1], -ALIBI_C_SAFE)
    nc.vector.drain()
    nc.vector.memset(cp[:, 1:2], 1.0 / SQ128)
    nc.vector.drain()
    nc.vector.memset(onesc[:], 1.0)
    nc.vector.drain()
    nc.vector.memset(onesr[:], 1.0)
    nc.vector.drain()
    nc.vector.memset(zr4[:], 0.0)
    nc.vector.drain().then_inc(sv.h, 1)
    sv.inc()

    es0 = ExitStack()
    ps_r = es0.enter_context(nc.psum_tensor(pre + "psr", [128, 8], F32))
    ps_a = [es0.enter_context(nc.psum_tensor(pre + f"psa{i}", [128, 512], F32)) for i in range(2)]
    ps_b = es0.enter_context(nc.psum_tensor(pre + "psb", [128, 8], F32))
    ps_t = es0.enter_context(nc.psum_tensor(pre + "pst", [128, 512], BF16))

    nc.scalar.wait_ge(ldr.h, ldr.v)
    for qs in ldrq4:
        nc.scalar.wait_ge(qs.h, qs.v)
    nc.scalar.wait_ge(sv.h, sv.v)
    nc.scalar.activation(simsq[:], sim_sb[:], AF.Square)
    nc.scalar.drain().then_inc(sa.h, 1)
    sa.inc()
    nc.tensor.wait_ge(sa.h, sa.v)
    for k in range(FC):
        nc.tensor.matmul(ps_b[:1, :], onesc[:], simsq[:, k], start=(k == 0), stop=(k == FC - 1))
    nc.tensor.drain().then_inc(pe.h, 1)
    pe.inc()
    nc.vector.wait_ge(pe.h, pe.v)
    nc.vector.tensor_copy(rn8[:], ps_b[0:1, :])
    nc.vector.drain().then_inc(sv.h, 1)
    sv.inc()
    nc.scalar.wait_ge(sv.h, sv.v)
    nc.scalar.activation(rn8[:], rn8[:], AF.Sqrt)
    nc.scalar.drain().then_inc(sa.h, 1)
    sa.inc()
    nc.vector.wait_ge(sa.h, sa.v)
    nc.vector.reciprocal(rn8[:], rn8[:])
    nc.vector.drain().then_inc(sv.h, 1)
    sv.inc()
    nc.scalar.wait_ge(sv.h, sv.v)
    nc.scalar.activation(grow[:], grow[:], AF.Sigmoid)
    nc.scalar.drain().then_inc(sa.h, 1)
    sa.inc()
    nc.tensor.wait_ge(sa.h, sa.v)
    nc.tensor.matmul(ps_b[:], onesr[:], rn8[:], start=True, stop=True)
    nc.tensor.drain().then_inc(pe.h, 1)
    pe.inc()
    nc.vector.wait_ge(pe.h, pe.v)
    nc.vector.tensor_copy(rnb[:], ps_b[:])
    nc.vector.drain().then_inc(sv.h, 1)
    sv.inc()
    nc.tensor.wait_ge(sv.h, sv.v)
    nc.tensor.matmul(ps_b[:], onesr[:], grow[:], start=True, stop=True)
    nc.tensor.drain().then_inc(pe.h, 1)
    pe.inc()
    nc.vector.wait_ge(pe.h, pe.v)
    nc.vector.tensor_copy(sgb[:], ps_b[:])
    nc.vector.drain().then_inc(sv.h, 1)
    sv.inc()

    ps_r_free = [None]

    def emit_routing(tc):
        if ps_r_free[0] is not None:
            nc.tensor.wait_ge(sv.h, ps_r_free[0])
        else:
            nc.tensor.wait_ge(sv.h, sv.v)
        for k in range(FC):
            nc.tensor.matmul(ps_r[:], xnfm[:, k, tc * 128:(tc + 1) * 128], sim_sb[:, k],
                             start=(k == 0), stop=(k == FC - 1))
        nc.tensor.drain().then_inc(pe.h, 1)
        pe.inc()
        nc.vector.wait_ge(pe.h, pe.v)
        nc.vector.tensor_scalar(rt[:, 0:8], ps_r[:], rr[:, tc:tc + 1], None, ALU.mult)
        nc.vector.drain().then_inc(sv.h, 1)
        sv.inc()
        ps_r_free[0] = sv.v
        nc.vector.tensor_tensor(rt[:, 0:8], rt[:, 0:8], rnb[:], ALU.mult)
        nc.vector.drain()
        nc.vector.tensor_tensor(rt[:, 0:8], rt[:, 0:8], sgb[:], ALU.subtract)
        nc.vector.drain()
        nc.vector.tensor_scalar(rt[:, 8:16], rt[:, 0:8], 0.0, None, ALU.is_gt)
        nc.vector.drain()
        nc.vector.reduce_sum(rt[:, 16:17], rt[:, 8:16], axis=AX.X)
        nc.vector.drain()
        nc.vector.tensor_scalar(rt[:, 17:18], rt[:, 16:17], 0.0, None, ALU.is_gt)
        nc.vector.drain()
        nc.vector.reduce_max(rt[:, 18:19], rt[:, 0:8], axis=AX.X)
        nc.vector.drain()
        nc.vector.tensor_scalar(rt[:, 24:32], rt[:, 0:8], rt[:, 18:19], None, ALU.is_ge)
        nc.vector.drain()
        nc.vector.scalar_tensor_tensor(rt[:, 24:32], rt[:, 24:32], -BIG, rt[:, 0:8],
                                       ALU.mult, ALU.add)
        nc.vector.drain()
        nc.vector.reduce_max(rt[:, 19:20], rt[:, 24:32], axis=AX.X)
        nc.vector.drain()
        nc.vector.tensor_scalar(rt[:, 24:32], rt[:, 0:8], rt[:, 19:20], None, ALU.is_ge)
        nc.vector.drain()
        nc.vector.tensor_scalar(rt[:, 32:40], rt[:, 8:16], rt[:, 17:18], None, ALU.mult)
        nc.vector.drain()
        nc.vector.tensor_scalar(rt[:, 20:21], rt[:, 17:18], -1.0, 1.0, ALU.mult, ALU.add)
        nc.vector.drain()
        nc.vector.tensor_scalar(rt[:, 24:32], rt[:, 24:32], rt[:, 20:21], None, ALU.mult)
        nc.vector.drain()
        nc.vector.tensor_tensor(mk_sb[:, tc], rt[:, 32:40], rt[:, 24:32], ALU.add)
        nc.vector.drain()
        nc.vector.tensor_scalar(rt[:, 24:32], rt[:, 0:8], 0.0, BIG, ALU.max, ALU.add)
        nc.vector.drain()
        nc.vector.tensor_tensor(rt[:, 24:32], rt[:, 24:32], mk_sb[:, tc], ALU.mult)
        nc.vector.drain()
        nc.vector.tensor_scalar(rt[:, 24:32], rt[:, 24:32], 1.0, -BIG, ALU.mult, ALU.add)
        nc.vector.drain()
        nc.vector.reduce_max(rt[:, 18:19], rt[:, 24:32], axis=AX.X)
        nc.vector.drain()
        nc.vector.tensor_scalar(rt[:, 19:20], rt[:, 18:19], -1.0, None, ALU.mult)
        nc.vector.drain().then_inc(sv.h, 1)
        sv.inc()
        nc.scalar.wait_ge(sv.h, sv.v)
        nc.scalar.activation(rt[:, 24:32], rt[:, 24:32], AF.Exp, bias=rt[:, 19:20], scale=1.0)
        nc.scalar.drain().then_inc(sa.h, 1)
        sa.inc()
        nc.vector.wait_ge(sa.h, sa.v)
        nc.vector.reduce_sum(rt[:, 18:19], rt[:, 24:32], axis=AX.X)
        nc.vector.drain()
        nc.vector.reciprocal(rt[:, 19:20], rt[:, 18:19])
        nc.vector.drain()
        nc.vector.tensor_scalar(rt[:, 24:32], rt[:, 24:32], rt[:, 19:20], None, ALU.mult)
        nc.vector.drain()
        nc.vector.tensor_tensor(mw_sb[:, tc], rt[:, 24:32], mk_sb[:, tc], ALU.mult)
        nc.vector.drain().then_inc(sv.h, 1)
        sv.inc()

    a_free = {}

    def emit_mqkv(tc):
        for ti, (W, dst_fm, ai) in enumerate(((qw_sb, qfm4, 0), (kw_sb, kfm4, 1), (vw_sb, None, 2))):
            for grp in range(2):
                bp = grp
                fa = a_free.get(bp)
                if fa is not None:
                    nc.tensor.wait_ge(sv.h, fa)
                for k in range(FC):
                    nc.tensor.matmul(ps_a[bp][:], xnfm[:, k, tc * 128:(tc + 1) * 128],
                                     W[:, k, grp * 512:(grp + 1) * 512],
                                     start=(k == 0), stop=(k == FC - 1))
                nc.tensor.drain().then_inc(pe.h, 1)
                pe.inc()
                nc.vector.wait_ge(pe.h, pe.v)
                for ee in range(4):
                    e = grp * 4 + ee
                    mcol = mk_sb[:, tc, e:e + 1]
                    dst = acc[:, ai, :]
                    if e == 0:
                        nc.vector.tensor_scalar(dst, ps_a[bp][:, ee * 128:(ee + 1) * 128],
                                                mcol, None, ALU.mult)
                    else:
                        nc.vector.scalar_tensor_tensor(dst, ps_a[bp][:, ee * 128:(ee + 1) * 128],
                                                       mcol, dst, ALU.mult, ALU.add)
                    nc.vector.drain()
                nc.vector.drain().then_inc(sv.h, 1)
                sv.inc()
                a_free[bp] = sv.v
            if ti < 2:
                nc.vector.tensor_copy(qtbf[:, ti, :], acc[:, ai, :])
                nc.vector.drain().then_inc(sv.h, 1)
                sv.inc()
                nc.tensor.wait_ge(sv.h, sv.v)
                nc.tensor.transpose(ps_t[:, ti * 128:(ti + 1) * 128], qtbf[:, ti, :], ident[:])
                nc.tensor.drain().then_inc(pe.h, 1)
                pe.inc()
            else:
                nc.vector.tensor_copy(vaug4[:, tc, 0:128], acc[:, 2, :])
                nc.vector.drain()
                nc.vector.memset(vaug4[:, tc, 128:129], 1.0)
                nc.vector.drain().then_inc(sv.h, 1)
                sv.inc()
        nc.vector.wait_ge(pe.h, pe.v)
        nc.vector.tensor_copy(qfm4[:, tc, :], ps_t[:, 0:128])
        nc.vector.drain()
        nc.vector.tensor_copy(kfm4[:, tc, :], ps_t[:, 128:256])
        nc.vector.drain().then_inc(sv.h, 1)
        sv.inc()

    emit_routing(0)
    for tc in range(1, TC):
        emit_mqkv(tc - 1)
        emit_routing(tc)
    emit_mqkv(TC - 1)
    p.quiesce()
    es0.close()

    es2 = ExitStack()
    ps_sc = [es2.enter_context(nc.psum_tensor(pre + f"pssc{i}", [128, 512], F32)) for i in range(3)]
    _pv = [es2.enter_context(nc.psum_tensor(pre + f"pspv{i}", [128, 264], F32)) for i in range(2)]
    ps_pv = [_pv[i // 2][:, (i % 2) * 132:(i % 2) * 132 + 129] for i in range(4)]
    ps_tr = es2.enter_context(nc.psum_tensor(pre + "pstr", [128, 512], BF16))
    ps_g = [es2.enter_context(nc.psum_tensor(pre + f"psg{i}", [128, 512], F32)) for i in range(2)]

    x1_r = x1_h.rearrange("(c pp) d -> c pp d", pp=128)
    s1_r = s1_h.rearrange("(c pp) d -> c pp d", pp=128)
    out_r = out_h.rearrange("(c pp) d -> c pp d", pp=128)
    ob_done = [None]
    oproj_pe = [None, None]
    tr_evac = [None]
    g_free = [None, None]
    comb_sv = [None]
    comb_sv_b = [None, None]
    xsld = {}

    def prefetch_xs(j):
        if j >= TC:
            return
        bbj = j % 2
        if comb_sv_b[bbj] is not None:
            nc.sync.wait_ge(sv.h, comb_sv_b[bbj])
        nc.sync.dma_start(out=xs[bbj][:], in_=x1_r[j]).then_inc(lds.h, 16)
        nc.sync.dma_start(out=ss[bbj][:], in_=s1_r[j]).then_inc(lds.h, 16)
        lds.inc(32)
        xsld[j] = lds.v

    prefetch_xs(0)
    prefetch_xs(1)
    for tqt in range(8):
        tq0c = tqt * 4
        first_pv = [True] * 4
        zeroed = [False]
        exp_done, corn_done, pv_done = {}, {}, {}

        def emit_sc4(tkc):
            r = tkc - tq0c
            col0 = max(0, r) * 128
            bp = tkc % 2
            fe = exp_done.get(bp)
            if fe is not None:
                nc.tensor.wait_ge(sa.h, fe)
                cd = corn_done.get(bp)
                if cd is not None:
                    nc.tensor.wait_ge(sv.h, cd)
            nc.tensor.matmul(ps_sc[bp][:, col0:512], kfm4[:, tkc, :],
                             qfm4[:, tq0c:tq0c + 4, :].rearrange("p c t -> p (c t)")[:, col0:512],
                             start=True, stop=True)
            nc.tensor.drain().then_inc(pe.h, 1)
            pe.inc()

        def emit_exp4(tkc):
            r = tkc - tq0c
            col0 = max(0, r) * 128
            bp = tkc % 2
            nc.scalar.wait_ge(pe.h, pe.v)
            fp_ = pv_done.get(bp)
            if fp_ is not None:
                nc.scalar.wait_ge(pe.h, fp_)
            if r >= 0:
                nc.vector.wait_ge(pe.h, pe.v)
                nc.vector.tensor_tensor(corn4[:, bp], ps_sc[bp][:, col0:col0 + 128], mcorn[:], ALU.add)
                nc.vector.drain().then_inc(sv.h, 1)
                sv.inc()
                corn_done[bp] = sv.v
                nc.scalar.wait_ge(sv.h, sv.v)
                nc.scalar.activation(pt4[:, bp, col0:col0 + 128], corn4[:, bp], AF.Exp,
                                     bias=cp[:, 0:1], scale=cp[:, 1:2])
                if col0 + 128 < 512:
                    nc.scalar.activation(pt4[:, bp, col0 + 128:512], ps_sc[bp][:, col0 + 128:512],
                                         AF.Exp, bias=cp[:, 0:1], scale=cp[:, 1:2])
            else:
                nc.scalar.activation(pt4[:, bp], ps_sc[bp][:], AF.Exp, bias=cp[:, 0:1], scale=cp[:, 1:2])
            nc.scalar.drain().then_inc(sa.h, 1)
            sa.inc()
            exp_done[bp] = sa.v

        def emit_pv4(tkc):
            r = tkc - tq0c
            bp = tkc % 2
            if not zeroed[0]:
                if ob_done[0] is not None:
                    nc.tensor.wait_ge(sv.h, ob_done[0])
                for j in range(2):
                    nc.tensor.matmul(_pv[j][:], zr4[0:1, 0:128], zr4[0:1, 0:264],
                                     start=True, stop=False, skip_group_check=True)
                zeroed[0] = True
            nc.tensor.wait_ge(sa.h, exp_done[bp])
            for sub in range(max(0, r), 4):
                nc.tensor.matmul(ps_pv[sub], pt4[:, bp, sub * 128:(sub + 1) * 128],
                                 vaug4[:, tkc, 0:129], start=False, stop=False,
                                 skip_group_check=True)
                first_pv[sub] = False
            nc.tensor.drain().then_inc(pe.h, 1)
            pe.inc()
            pv_done[bp] = pe.v

        tkcs4 = list(range(tq0c + 4))
        for idx, tkc in enumerate(tkcs4):
            emit_sc4(tkc)
            emit_exp4(tkc)
            if idx > 0:
                emit_pv4(tkcs4[idx - 1])
        emit_pv4(tkcs4[-1])
        nc.tensor.drain().then_inc(pe.h, 1)
        pe.inc()
        nc.vector.wait_ge(pe.h, pe.v)
        for sub in range(4):
            nc.vector.tensor_copy(att4[:, sub, 0:129], ps_pv[sub])
        nc.vector.drain()
        nc.vector.reciprocal(att4[:, 0:4, 130:131].rearrange("p s o -> p (s o)"),
                             att4[:, 0:4, 128:129].rearrange("p s o -> p (s o)"))
        nc.vector.drain().then_inc(sv.h, 1)
        sv.inc()
        ob_done[0] = sv.v
        for sub in range(4):
            c_idx = tqt * 4 + sub
            bb = sub % 2
            if oproj_pe[bb] is not None:
                nc.vector.wait_ge(pe.h, oproj_pe[bb])
            for e in range(8):
                nc.vector.tensor_scalar(scat[:, bb, e], att4[:, sub, 0:128],
                                        att4[:, sub, 130:131], mw_sb[:, c_idx, e:e + 1],
                                        ALU.mult, ALU.mult)
            nc.vector.drain().then_inc(sv.h, 1)
            sv.inc()
            scat_sv = sv.v
            for g in range(2):
                nc.tensor.wait_ge(sv.h, scat_sv if g == 0 else tr_evac[0])
                if g == 0 and tr_evac[0] is not None:
                    nc.tensor.wait_ge(sv.h, tr_evac[0])
                for k in range(4):
                    nc.tensor.transpose(ps_tr[:, k * 128:(k + 1) * 128],
                                        scat[:, bb, g * 4 + k], ident[:])
                nc.tensor.drain().then_inc(pe.h, 1)
                pe.inc()
                nc.vector.wait_ge(pe.h, pe.v)
                nc.vector.tensor_copy(afm8[:, bb, g * 4:(g + 1) * 4, :],
                                      ps_tr[:].rearrange("p (k t) -> p k t", k=4))
                nc.vector.drain().then_inc(sv.h, 1)
                sv.inc()
                tr_evac[0] = sv.v
            afm8_sv = sv.v
            for half in range(2):
                nc.tensor.wait_ge(sv.h, afm8_sv)
                if g_free[half] is not None:
                    nc.tensor.wait_ge(sa.h, g_free[half])
                for e in range(8):
                    nc.tensor.matmul(ps_g[half][:], afm8[:, bb, e, :],
                                     ow_sb[:, e, half * 512:(half + 1) * 512],
                                     start=(e == 0), stop=(e == 7))
                nc.tensor.drain().then_inc(pe.h, 1)
                pe.inc()
                nc.scalar.wait_ge(pe.h, pe.v)
                if comb_sv[0] is not None:
                    nc.scalar.wait_ge(sv.h, comb_sv[0])
                nc.scalar.activation(g1[:, half * 512:(half + 1) * 512], ps_g[half][:], AF.Silu)
                nc.scalar.drain().then_inc(sa.h, 1)
                sa.inc()
                g_free[half] = sa.v
            oproj_pe[bb] = pe.v
            nc.vector.wait_ge(sa.h, sa.v)
            nc.vector.wait_ge(lds.h, xsld[c_idx])
            if c_idx >= 2:
                nc.vector.wait_ge(so.h, so.v)
            nc.vector.tensor_tensor(g1[:], ss[bb][:], g1[:], ALU.mult)
            nc.vector.drain()
            nc.vector.tensor_tensor(ob[bb][:], xs[bb][:], g1[:], ALU.add)
            nc.vector.drain().then_inc(sv.h, 1)
            sv.inc()
            comb_sv[0] = sv.v
            comb_sv_b[bb] = sv.v
            nc.sync.wait_ge(sv.h, sv.v)
            nc.sync.dma_start(out=out_r[c_idx], in_=ob[bb][:]).then_inc(so.h, 16)
            so.inc(16)
            prefetch_xs(c_idx + 2)
    nc.sync.wait_ge(so.h, so.v)
    nc.sync.wait_ge(lds.h, lds.v)


def prep_p4(inputs):
    f = lambda a: np.ascontiguousarray(np.asarray(a, dtype=np.float32))
    d = {}
    d["sim"] = f(inputs["L1_ffn_sim_matrix"])
    d["gates"] = f(np.asarray(inputs["L1_ffn_gates"], dtype=np.float32).reshape(1, 8))
    d["qcat"] = f(np.asarray(inputs["L1_ffn_q_proj"], dtype=np.float32).transpose(1, 0, 2).reshape(D, 8 * 128))
    d["kcat"] = f(np.asarray(inputs["L1_ffn_k_proj"], dtype=np.float32).transpose(1, 0, 2).reshape(D, 8 * 128))
    d["vcat"] = f(np.asarray(inputs["L1_ffn_v_proj"], dtype=np.float32).transpose(1, 0, 2).reshape(D, 8 * 128))
    d["oproj"] = f(inputs["L1_ffn_o_proj"])
    return d


def build_p4():
    p = Prog()
    io = {
        "xnfm1": p.hbm("xnfm1", [FC, 128, T], BF16),
        "rrho": p.hbm("rrho", [128, TC], F32),
        "sim": p.hbm("sim", [D, 8], F32),
        "gates": p.hbm("gates", [1, 8], F32),
        "qcat": p.hbm("qcat", [D, D], F32),
        "kcat": p.hbm("kcat", [D, D], F32),
        "vcat": p.hbm("vcat", [D, D], F32),
        "oproj": p.hbm("oproj", [8, 128, D], F32),
        "x1": p.hbm("x1", [T, D], F32),
        "s1": p.hbm("s1", [T, D], F32),
        "out": p.hbm("out", [T, D], F32, kind="ExternalOutput"),
    }
    c3 = make_consts_p3()
    p.consts = {
        "IDENT": p.hbm("IDENT", [128, 128], BF16),
        "MCORN": p.hbm("MCORN", [128, 128], F32),
    }
    emit_p4(p, io)
    p.done()
    return p.nc


def kernel_unfused(**inputs):
    x = np.ascontiguousarray(np.asarray(inputs["L0_attn_x"], dtype=np.float32))
    cst0 = make_consts()
    cst1 = make_consts_p1()
    cst3 = make_consts_p3()
    d1 = prep_p1(inputs)
    d2 = prep_shared(inputs)
    d3 = prep_p3(inputs)
    d4 = prep_p4(inputs)
    COS, SINF = _rope_tables()
    IDENT = cst0["IDENT"]

    nc = build_p0()
    res = run_bass_kernel_spmd(nc, [{"x": x[c], "IDENT": IDENT} for c in range(NCORES)],
                               core_ids=list(range(NCORES)))
    xnfm = [r["xnfm"] for r in res.results]

    nc = build_p1()
    im = [dict(xnfm=xnfm[c], IDENT=IDENT,
               **{k: d1[k] for k in ["WqD", "WkD", "Wv", "Wo", "G1", "G2", "thetaD", "gam", "bet"]},
               **{k: cst1[k] for k in ["TRIU4", "PHASE", "DTIDX", "SUPIDX"]})
          for c in range(NCORES)]
    res = run_bass_kernel_spmd(nc, im, core_ids=list(range(NCORES)))
    s0 = [r["s0"] for r in res.results]

    nc = build_p2()
    im = [dict(xnfm=xnfm[c], x=x[c], s0=s0[c], IDENT=IDENT,
               **{k: d2[k] for k in ["wgT", "w1T", "w3T", "w2T"]})
          for c in range(NCORES)]
    res = run_bass_kernel_spmd(nc, im, core_ids=list(range(NCORES)))
    x1 = [r["x1"] for r in res.results]
    xnfm1 = [r["xnfm1"] for r in res.results]
    rrho = [r["rrho"] for r in res.results]

    nc = build_p3()
    im = [dict(xnfm1=xnfm1[c], COS=COS, SINF=SINF, IDENT=IDENT,
               **{k: d3[k] for k in ["Wq1", "Wk1", "Wv1", "bq1", "bk1", "bv1", "WoutF", "bout"]},
               **{k: cst3[k] for k in ["AREL", "RNEG", "MCORN", "ONES2"]})
          for c in range(NCORES)]
    res = run_bass_kernel_spmd(nc, im, core_ids=list(range(NCORES)))
    s1 = [r["s1"] for r in res.results]

    nc = build_p4()
    im = [dict(xnfm1=xnfm1[c], rrho=rrho[c], x1=x1[c], s1=s1[c], IDENT=IDENT,
               MCORN=cst3["MCORN"],
               **{k: d4[k] for k in ["sim", "gates", "qcat", "kcat", "vcat", "oproj"]})
          for c in range(NCORES)]
    res = run_bass_kernel_spmd(nc, im, core_ids=list(range(NCORES)))
    out = np.stack([np.asarray(r["out"], dtype=np.float32) for r in res.results])
    return out


def build_fused():
    p = Prog()
    EXT, INT = "ExternalInput", "Internal"
    io = {}
    io["x"] = p.hbm("x", [T, D], F32)
    for n, shp, dt in [("xnfm", [FC, 128, T], BF16), ("s0", [T, D], F32),
                       ("x1", [T, D], F32), ("xnfm1", [FC, 128, T], BF16),
                       ("rrho", [128, TC], F32), ("s1", [T, D], F32),
                       ("qfm", [FC, 128, T], BF16), ("kfm", [FC, 128, T], BF16),
                       ("vaug", [128, TC, 1040], BF16)]:
        io[n] = p.hbm(n, shp, dt, kind=INT)
    io["out"] = p.hbm("out", [T, D], F32, kind="ExternalOutput")
    for n, shp in [("WqD", [D, 2048]), ("WkD", [D, 2048]), ("Wv", [D, D]), ("Wo", [D, D]),
                   ("G1", [D, 64]), ("G2", [64, D]), ("thetaD", [1, 2048]),
                   ("gam", [128, D]), ("bet", [128, D]),
                   ("wgT", [D, 8]), ("w1T", [8, D, 2048]), ("w3T", [8, D, 2048]),
                   ("w2T", [8, 2048, D]),
                   ("Wq1", [D, D]), ("Wk1", [D, D]), ("Wv1", [D, D]),
                   ("bq1", [1, D]), ("bk1", [1, D]), ("bv1", [1, D]),
                   ("WoutF", [D, D]), ("bout", [1, D]),
                   ("sim", [D, 8]), ("gates", [1, 8]),
                   ("qcat", [D, D]), ("kcat", [D, D]), ("vcat", [D, D]),
                   ("oproj", [8, 128, D])]:
        io[n] = p.hbm(n, shp, F32)
    for n, shp in [("COS", [T, D]), ("SINF", [T, D])]:
        io[n] = p.hbm(n, shp, BF16)
    p.consts = {
        "IDENT": p.hbm("IDENT", [128, 128], BF16),
        "TRIU4": p.hbm("TRIU4", [128, 4, 128], F32),
        "PHASE": p.hbm("PHASE", [128, 1], F32),
        "DTIDX": p.hbm("DTIDX", [1, 512], F32),
        "SUPIDX": p.hbm("SUPIDX", [1, 8], F32),
        "AREL": p.hbm("AREL", [128, 35, 16], F32),
        "RNEG": p.hbm("RNEG", [2, 16, 512], BF16),
        "MCORN": p.hbm("MCORN", [128, 128], F32),
        "ONES2": p.hbm("ONES2", [2, 512], BF16),
    }
    with p.nc.named_scope("P0"):
        p.push_scope()
        emit_p0(p, io["x"], io["xnfm"])
        p.pop_scope()
        p.quiesce()
    with p.nc.named_scope("P1"):
        p.push_scope()
        emit_p1(p, io)
        p.pop_scope()
        p.quiesce()
    with p.nc.named_scope("P2"):
        p.push_scope()
        emit_p2(p, io)
        p.pop_scope()
        p.quiesce()
    with p.nc.named_scope("P3"):
        p.push_scope()
        emit_p3(p, io)
        p.pop_scope()
        p.quiesce()
    with p.nc.named_scope("P4"):
        p.push_scope()
        emit_p4(p, io)
        p.pop_scope()
    p.done()
    return p.nc


def kernel_fused(**inputs):
    x = np.ascontiguousarray(np.asarray(inputs["L0_attn_x"], dtype=np.float32))
    cst0 = make_consts()
    cst1 = make_consts_p1()
    cst3 = make_consts_p3()
    d1 = prep_p1(inputs)
    d2 = prep_shared(inputs)
    d3 = prep_p3(inputs)
    d4 = prep_p4(inputs)
    COS, SINF = _rope_tables()
    base = dict(IDENT=cst0["IDENT"], COS=COS, SINF=SINF,
                **{k: cst1[k] for k in ["TRIU4", "PHASE", "DTIDX", "SUPIDX"]},
                **{k: cst3[k] for k in ["AREL", "RNEG", "MCORN", "ONES2"]},
                **{k: d1[k] for k in ["WqD", "WkD", "Wv", "Wo", "G1", "G2", "thetaD", "gam", "bet"]},
                **{k: d2[k] for k in ["wgT", "w1T", "w3T", "w2T"]},
                **{k: d3[k] for k in ["Wq1", "Wk1", "Wv1", "bq1", "bk1", "bv1", "WoutF", "bout"]},
                **{k: d4[k] for k in ["sim", "gates", "qcat", "kcat", "vcat", "oproj"]})
    nc = build_fused()
    im = [dict(x=x[c], **base) for c in range(NCORES)]
    res = run_bass_kernel_spmd(nc, im, core_ids=list(range(NCORES)))
    return np.stack([np.asarray(r["out"], dtype=np.float32) for r in res.results])


def kernel(**inputs):
    return kernel_fused(**inputs)


# version 14 (final)
# speedup vs baseline: 1.1544x; 1.1544x over previous
import numpy as np
from contextlib import ExitStack

import concourse.bass as bass
import concourse.mybir as mybir
from concourse.bass_utils import run_bass_kernel_spmd

F32 = mybir.dt.float32
BF16 = mybir.dt.bfloat16
I32 = mybir.dt.int32
AF = mybir.ActivationFunctionType
ALU = mybir.AluOpType
AX = mybir.AxisListType

NCORES = 8
P3_SPLIT_LOADS = True
P3_RNEG_SKIP = True
T = 4096
D = 1024
FC = D // 128
TC = T // 128
NSC = T // 512
EPS_LN = 1e-5


class Sem:
    def __init__(self, h):
        self.h = h
        self.v = 0

    def inc(self, n=1):
        self.v += n
        return self.v


class Prog:
    def __init__(self):
        self.nc = bass.Bass("TRN2", target_bir_lowering=False)
        self.es = ExitStack()
        self._scopes = [self.es]
        self._names = set()
        self.consts = {}
        self.all_sems = []

    def push_scope(self):
        st = ExitStack()
        self._scopes.append(st)
        return st

    def pop_scope(self):
        self._scopes.pop().close()

    def _nm(self, name):
        assert name not in self._names, f"duplicate tensor name {name}"
        self._names.add(name)
        return name

    def sbuf(self, name, shape, dtype=F32):
        return self._scopes[-1].enter_context(self.nc.sbuf_tensor(self._nm(name), shape, dtype))

    def psum(self, name, shape, dtype=F32):
        return self._scopes[-1].enter_context(self.nc.psum_tensor(self._nm(name), shape, dtype))

    def sem(self, name):
        h = Sem(self.es.enter_context(self.nc.semaphore(name)))
        self.all_sems.append(h)
        return h

    def hbm(self, name, shape, dtype=F32, kind="ExternalInput"):
        return self.nc.dram_tensor(self._nm(name), list(shape), dtype, kind=kind).ap()

    def quiesce(self):
        nc = self.nc
        fin = self.sem(f"qsc{len(self.all_sems)}")
        for eng in (nc.tensor, nc.vector, nc.scalar, nc.gpsimd):
            eng.drain().then_inc(fin.h, 1)
            fin.inc()
        for eng in (nc.tensor, nc.vector, nc.scalar, nc.gpsimd, nc.sync):
            eng.wait_ge(fin.h, fin.v)
            for s_ in self.all_sems:
                if s_ is not fin and s_.v > 0:
                    eng.wait_ge(s_.h, s_.v)

    def done(self):
        while len(self._scopes) > 1:
            self.pop_scope()
        self.es.close()
        return self.nc


def emit_p0(p, x_h, xnfm_h, pre="p0_"):
    nc = p.nc
    ident = p.sbuf(pre + "ident", [128, 128], BF16)
    xt = [p.sbuf(pre + f"xt{i}", [128, 4, 1024], F32) for i in range(2)]
    stg = [p.sbuf(pre + f"stg{i}", [128, 8, 512], BF16) for i in range(2)]
    xn_bf = p.sbuf(pre + "xnbf", [128, 1024], BF16)
    junk = p.sbuf(pre + "junk", [128, 4, 1024], BF16)
    st = p.sbuf(pre + "st", [128, 32], F32)
    pstr = p.psum(pre + "pstr", [128, 512], BF16)
    lds = p.sem(pre + "lds")
    ldg = p.sem(pre + "ldg")
    so = p.sem(pre + "so")
    sv = p.sem(pre + "sv")
    sa = p.sem(pre + "sa")
    pe = p.sem(pre + "pe")

    nc.gpsimd.dma_start(out=ident[:], in_=p.consts["IDENT"]).then_inc(ldg.h, 16)
    ldg.inc(16)

    x_r = x_h.rearrange("(sc c p) d -> sc p c d", p=128, c=4)

    xt_free_at = [None, None]
    stg_free_at = [None, None]
    pe_prev_chunk = 0

    for sc in range(NSC):
        b = sc % 2
        if xt_free_at[b] is not None:
            svv, sav = xt_free_at[b]
            nc.sync.wait_ge(sv.h, svv)
            nc.sync.wait_ge(sa.h, sav)
        nc.sync.dma_start(out=xt[b][:], in_=x_r[sc]).then_inc(lds.h, 16)
        lds.inc(16)
        ld_target = lds.v
        sg = stg[b]
        if stg_free_at[b] is not None:
            nc.vector.wait_ge(so.h, so.v)

        nc.vector.wait_ge(lds.h, ld_target)
        nc.scalar.wait_ge(lds.h, ld_target)
        sb_ = b * 16
        for c in range(4):
            nc.vector.reduce_sum(st[:, sb_ + c:sb_ + c + 1], xt[b][:, c], axis=AX.X)
            nc.scalar.activation(junk[:, c], xt[b][:, c], AF.Square, accum_out=st[:, sb_ + 4 + c:sb_ + 5 + c])
        nc.vector.drain().then_inc(sv.h, 1)
        sv.inc()
        nc.scalar.drain().then_inc(sa.h, 1)
        sa.inc()
        nc.vector.wait_ge(sv.h, sv.v)
        nc.vector.wait_ge(sa.h, sa.v)
        nc.vector.tensor_scalar(st[:, sb_ + 8:sb_ + 12], st[:, sb_ + 0:sb_ + 4], 1.0 / D, None, ALU.mult)
        nc.vector.drain().then_inc(sv.h, 1)
        sv.inc()
        nc.vector.wait_ge(sv.h, sv.v)
        nc.vector.tensor_scalar(st[:, sb_ + 4:sb_ + 8], st[:, sb_ + 4:sb_ + 8], 1.0 / D, None, ALU.mult)
        nc.vector.drain().then_inc(sv.h, 1)
        sv.inc()
        nc.vector.wait_ge(sv.h, sv.v)
        nc.vector.scalar_tensor_tensor(st[:, sb_ + 12:sb_ + 16], st[:, sb_ + 8:sb_ + 12], 1.0, st[:, sb_ + 8:sb_ + 12], ALU.mult, ALU.mult)
        nc.vector.drain().then_inc(sv.h, 1)
        sv.inc()
        nc.vector.wait_ge(sv.h, sv.v)
        nc.vector.tensor_tensor(st[:, sb_ + 12:sb_ + 16], st[:, sb_ + 4:sb_ + 8], st[:, sb_ + 12:sb_ + 16], ALU.subtract)
        nc.vector.drain().then_inc(sv.h, 1)
        sv.inc()
        nc.vector.wait_ge(sv.h, sv.v)
        nc.vector.tensor_scalar(st[:, sb_ + 12:sb_ + 16], st[:, sb_ + 12:sb_ + 16], 1.0, EPS_LN, ALU.mult, ALU.add)
        nc.vector.drain().then_inc(sv.h, 1)
        sv.inc()
        nc.scalar.wait_ge(sv.h, sv.v)
        nc.scalar.activation(st[:, sb_ + 12:sb_ + 16], st[:, sb_ + 12:sb_ + 16], AF.Sqrt)
        nc.scalar.drain().then_inc(sa.h, 1)
        sa.inc()
        nc.vector.wait_ge(sa.h, sa.v)
        nc.vector.reciprocal(st[:, sb_ + 12:sb_ + 16], st[:, sb_ + 12:sb_ + 16])
        nc.vector.drain().then_inc(sv.h, 1)
        sv.inc()

        for c in range(4):
            nc.vector.wait_ge(sv.h, sv.v)
            nc.vector.wait_ge(pe.h, pe_prev_chunk)
            nc.vector.tensor_scalar(xn_bf[:], xt[b][:, c], st[:, sb_ + 8 + c:sb_ + 9 + c], st[:, sb_ + 12 + c:sb_ + 13 + c],
                                    ALU.subtract, ALU.mult)
            nc.vector.drain().then_inc(sv.h, 1)
            sv.inc()
            nc.tensor.wait_ge(sv.h, sv.v)
            if sc == 0 and c == 0:
                nc.tensor.wait_ge(ldg.h, 16)
            for g in range(2):
                if pe.v > 0:
                    nc.tensor.wait_ge(sv.h, sv.v)
                for k in range(4):
                    kk = g * 4 + k
                    nc.tensor.transpose(pstr[:, k * 128:(k + 1) * 128],
                                        xn_bf[:, kk * 128:(kk + 1) * 128], ident[:])
                nc.tensor.drain().then_inc(pe.h, 1)
                pe.inc()
                nc.vector.wait_ge(pe.h, pe.v)
                nc.vector.tensor_copy(sg[:, g * 4:(g + 1) * 4, c * 128:(c + 1) * 128],
                                      pstr[:].rearrange("p (k t) -> p k t", k=4))
                nc.vector.drain().then_inc(sv.h, 1)
                sv.inc()
            pe_prev_chunk = pe.v
        xt_free_at[b] = (sv.v, sa.v)

        nc.sync.wait_ge(sv.h, sv.v)
        for k in range(8):
            nc.sync.dma_start(out=xnfm_h[k, :, sc * 512:(sc + 1) * 512], in_=sg[:, k]).then_inc(so.h, 16)
            so.inc(16)
        stg_free_at[b] = so.v
    nc.sync.wait_ge(so.h, so.v)


def make_consts():
    import ml_dtypes
    c = {}
    c["IDENT"] = np.eye(128, dtype=ml_dtypes.bfloat16)
    return c


def build_p0():
    p = Prog()
    x_h = p.hbm("x", [T, D], F32)
    xnfm_h = p.hbm("xnfm", [FC, 128, T], BF16, kind="ExternalOutput")
    p.consts = {"IDENT": p.hbm("IDENT", [128, 128], BF16)}
    emit_p0(p, x_h, xnfm_h)
    p.done()
    return p.nc


def emit_p2(p, io, pre="p2_"):
    nc = p.nc
    NE = 8
    xnfm_h, x_h, s0_h = io["xnfm"], io["x"], io["s0"]
    wgT_h, w1T_h, w3T_h, w2T_h = io["wgT"], io["w1T"], io["w3T"], io["w2T"]
    x1_h, xnfm1_h, rrho_h = io["x1"], io["xnfm1"], io["rrho"]

    ident = p.sbuf(pre + "ident", [128, 128], BF16)
    xnp = [p.sbuf(pre + f"xnp{i}", [128, FC, 1024], BF16) for i in range(2)]
    wg_sb = p.sbuf(pre + "wg", [128, FC, NE], BF16)
    wexp = p.sbuf(pre + "wexp", [128, TC, NE], F32)
    rr = p.sbuf(pre + "rr", [128, TC], F32)
    act = p.sbuf(pre + "act", [128, 16, 1024], BF16)
    w1s = [p.sbuf(pre + f"w1s{i}", [128, FC, 128], BF16) for i in range(2)]
    w3s = [p.sbuf(pre + f"w3s{i}", [128, FC, 128], BF16) for i in range(2)]
    w2s = p.sbuf(pre + "w2s", [128, 16, 1024], BF16)
    gacc = p.sbuf(pre + "gacc", [128, 8, 1024], F32)
    g_all = p.sbuf(pre + "gall", [128, 8, 1024], BF16)
    tmp_bf = p.sbuf(pre + "tmpbf", [128, 2, 512], BF16)
    rt = p.sbuf(pre + "rt", [128, 40], F32)
    g_sb = p.sbuf(pre + "g", [128, 1024], F32)
    xs_sb = [p.sbuf(pre + f"xs{i}", [128, 1024], F32) for i in range(2)]
    ss_sb = [p.sbuf(pre + f"ss{i}", [128, 1024], F32) for i in range(2)]
    x1_sb = [p.sbuf(pre + f"x1s{i}", [128, 1024], F32) for i in range(2)]
    xn1_bf = p.sbuf(pre + "xn1bf", [128, 2, 1024], BF16)
    stg = [p.sbuf(pre + f"stg{i}", [128, 8, 128], BF16) for i in range(2)]
    st = p.sbuf(pre + "st", [128, 8], F32)
    junk = p.sbuf(pre + "junk", [128, 1024], BF16)

    ps_rt = p.psum(pre + "psrt", [128, 8], F32)
    ps_h1 = [p.psum(pre + f"psh1{i}", [128, 512], F32) for i in range(2)]
    ps_h3 = [p.psum(pre + f"psh3{i}", [128, 512], F32) for i in range(2)]
    ps_y = [p.psum(pre + f"psy{i}", [128, 512], F32) for i in range(2)]
    pstr = p.psum(pre + "pstr", [128, 8, 128], BF16)

    ldr = p.sem(pre + "ldr")
    ldrp = [p.sem(pre + "ldrp0"), p.sem(pre + "ldrp1")]
    lds = p.sem(pre + "lds")
    ldg = p.sem(pre + "ldg")
    ldgp = [p.sem(pre + "ldgp0"), p.sem(pre + "ldgp1")]
    ldw2 = p.sem(pre + "ldw2")
    so = p.sem(pre + "so")
    sv = p.sem(pre + "sv")
    sa = p.sem(pre + "sa")
    pe = p.sem(pre + "pe")

    nc.gpsimd.dma_start(out=ident[:], in_=p.consts["IDENT"]).then_inc(ldg.h, 16)
    nc.gpsimd.dma_start(out=wg_sb[:], in_=wgT_h.rearrange("(k p) e -> p k e", p=128)).then_inc(ldg.h, 16)
    ldg.inc(32)


    w1T_r = w1T_h.rearrange("e (kc pp) i -> e pp kc i", pp=128)
    w3T_r = w3T_h.rearrange("e (kc pp) i -> e pp kc i", pp=128)
    w2T_r = w2T_h.rearrange("e (ic pp) o -> e pp ic o", pp=128)
    x_r = x_h.rearrange("(c pp) d -> c pp d", pp=128)
    s_r = s0_h.rearrange("(c pp) d -> c pp d", pp=128)
    x1_r = x1_h.rearrange("(c pp) d -> c pp d", pp=128)

    flat = [(tt, e, i) for tt in range(4) for e in range(NE) for i in range(16)]
    load_done_tgt = {}
    a_done_pe = {}
    psA_free = {}
    psY_free = {}

    def issue_w13(m):
        tt, e, i = flat[m]
        bw = m % 2
        if m - 2 in a_done_pe:
            nc.gpsimd.wait_ge(pe.h, a_done_pe[m - 2])
        nc.gpsimd.dma_start(out=w1s[bw][:], in_=w1T_r[e, :, :, i * 128:(i + 1) * 128]).then_inc(ldgp[bw].h, 16)
        nc.gpsimd.dma_start(out=w3s[bw][:], in_=w3T_r[e, :, :, i * 128:(i + 1) * 128]).then_inc(ldgp[bw].h, 16)
        ldgp[bw].inc(32)
        load_done_tgt[m] = ldgp[bw].v

    BIG = 1.0e9
    nc.tensor.wait_ge(ldg.h, ldg.v)

    rt_free = [None]

    def emit_router_chunk(tt, tc):
        if rt_free[0] is not None:
            nc.tensor.wait_ge(sv.h, rt_free[0])
        for k in range(FC):
            nc.tensor.matmul(ps_rt[:], xnp[tt % 2][:, k, (tc - tt * 8) * 128:(tc - tt * 8 + 1) * 128], wg_sb[:, k],
                             start=(k == 0), stop=(k == FC - 1))
        nc.tensor.drain().then_inc(pe.h, 1)
        pe.inc()
        nc.vector.wait_ge(pe.h, pe.v)
        nc.vector.tensor_copy(rt[:, 0:8], ps_rt[:])
        nc.vector.drain()
        nc.vector.reduce_max(rt[:, 8:9], rt[:, 0:8], axis=AX.X)
        nc.vector.drain()
        nc.vector.tensor_scalar(rt[:, 16:24], rt[:, 0:8], rt[:, 8:9], None, ALU.is_ge)
        nc.vector.drain()
        nc.vector.scalar_tensor_tensor(rt[:, 16:24], rt[:, 16:24], -BIG, rt[:, 0:8],
                                       ALU.mult, ALU.add)
        nc.vector.drain()
        nc.vector.reduce_max(rt[:, 9:10], rt[:, 16:24], axis=AX.X)
        nc.vector.drain().then_inc(sv.h, 1)
        sv.inc()
        rt_free[0] = sv.v
        nc.scalar.wait_ge(sv.h, sv.v)
        nc.scalar.mul(rt[:, 10:11], rt[:, 8:9], -1.0)
        nc.scalar.drain()
        nc.scalar.activation(rt[:, 16:24], rt[:, 0:8], AF.Exp, bias=rt[:, 10:11], scale=1.0)
        nc.scalar.drain().then_inc(sa.h, 1)
        sa.inc()
        nc.vector.wait_ge(sa.h, sa.v)
        nc.vector.tensor_scalar(rt[:, 24:32], rt[:, 0:8], rt[:, 9:10], None, ALU.is_ge)
        nc.vector.drain()
        nc.vector.tensor_tensor(rt[:, 24:32], rt[:, 16:24], rt[:, 24:32], ALU.mult)
        nc.vector.drain()
        nc.vector.reduce_sum(rt[:, 32:33], rt[:, 24:32], axis=AX.X)
        nc.vector.drain()
        nc.vector.reciprocal(rt[:, 33:34], rt[:, 32:33])
        nc.vector.drain()
        nc.vector.tensor_scalar(wexp[:, tc], rt[:, 24:32], rt[:, 33:34], None, ALU.mult)
        nc.vector.drain().then_inc(sv.h, 1)
        sv.inc()



    gall_sa = [None]

    def emit_gall(tt):
        nc.scalar.wait_ge(sv.h, sv.v)
        for kk8 in range(8):
            nc.scalar.activation(g_all[:, kk8], gacc[:, kk8], AF.Silu)
        nc.scalar.drain().then_inc(sa.h, 1)
        sa.inc()
        gall_sa[0] = sa.v

    cmb_sv = {}
    pstr_free = [None]

    def emit_combine_front(tt, ts_):
        c_idx = tt * 8 + ts_
        bb = ts_ % 2
        if c_idx >= 1:
            nc.sync.wait_ge(sv.h, sv.v)
        nc.sync.dma_start(out=xs_sb[bb][:], in_=x_r[c_idx]).then_inc(lds.h, 16)
        nc.sync.dma_start(out=ss_sb[bb][:], in_=s_r[c_idx]).then_inc(lds.h, 16)
        lds.inc(32)
        nc.vector.wait_ge(sa.h, gall_sa[0])
        nc.vector.wait_ge(lds.h, lds.v)
        if c_idx >= 2:
            nc.vector.wait_ge(so.h, so.v)
        nc.vector.tensor_tensor(g_sb[:], ss_sb[bb][:], g_all[:, ts_], ALU.mult)
        nc.vector.drain()
        nc.vector.tensor_tensor(x1_sb[bb][:], xs_sb[bb][:], g_sb[:], ALU.add)
        nc.vector.drain().then_inc(sv.h, 1)
        sv.inc()
        nc.sync.wait_ge(sv.h, sv.v)
        nc.sync.dma_start(out=x1_r[c_idx], in_=x1_sb[bb][:]).then_inc(so.h, 16)
        so.inc(16)
        nc.vector.reduce_sum(st[:, 0:1], x1_sb[bb][:], axis=AX.X)
        nc.vector.drain().then_inc(sv.h, 1)
        sv.inc()
        nc.scalar.wait_ge(sv.h, sv.v)
        nc.scalar.activation(junk[:], x1_sb[bb][:], AF.Square, accum_out=st[:, 1:2])
        nc.scalar.drain().then_inc(sa.h, 1)
        sa.inc()
        nc.vector.wait_ge(sa.h, sa.v)
        nc.vector.tensor_scalar(st[:, 2:3], st[:, 0:1], 1.0 / D, None, ALU.mult)
        nc.vector.tensor_scalar(st[:, 3:4], st[:, 1:2], 1.0 / D, None, ALU.mult)
        nc.vector.drain()
        nc.vector.scalar_tensor_tensor(st[:, 4:5], st[:, 2:3], 1.0, st[:, 2:3], ALU.mult, ALU.mult)
        nc.vector.drain()
        nc.vector.tensor_tensor(st[:, 4:5], st[:, 3:4], st[:, 4:5], ALU.subtract)
        nc.vector.drain()
        nc.vector.tensor_scalar(st[:, 5:6], st[:, 4:5], 1.0, EPS_LN, ALU.mult, ALU.add)
        nc.vector.drain().then_inc(sv.h, 1)
        sv.inc()
        nc.scalar.wait_ge(sv.h, sv.v)
        nc.scalar.activation(st[:, 6:7], st[:, 5:6], AF.Sqrt)
        nc.scalar.drain().then_inc(sa.h, 1)
        sa.inc()
        nc.vector.wait_ge(sa.h, sa.v)
        nc.vector.reciprocal(st[:, 7:8], st[:, 6:7])
        nc.vector.drain()
        nc.vector.reciprocal(st[:, 4:5], st[:, 4:5])
        nc.vector.drain()
        nc.vector.tensor_tensor(st[:, 4:5], st[:, 5:6], st[:, 4:5], ALU.mult)
        nc.vector.drain().then_inc(sv.h, 1)
        sv.inc()
        nc.scalar.wait_ge(sv.h, sv.v)
        nc.scalar.activation(st[:, 4:5], st[:, 4:5], AF.Sqrt, scale=1.0 / D)
        nc.scalar.drain().then_inc(sa.h, 1)
        sa.inc()
        nc.vector.wait_ge(sa.h, sa.v)
        nc.vector.tensor_copy(rr[:, c_idx:c_idx + 1], st[:, 4:5])
        nc.vector.drain()
        nc.vector.tensor_scalar(xn1_bf[:, bb], x1_sb[bb][:], st[:, 2:3], st[:, 7:8],
                                ALU.subtract, ALU.mult)
        nc.vector.drain().then_inc(sv.h, 1)
        sv.inc()
        cmb_sv[(tt, ts_)] = sv.v

    def emit_combine_back(tt, ts_):
        c_idx = tt * 8 + ts_
        bb = ts_ % 2
        nc.tensor.wait_ge(sv.h, cmb_sv[(tt, ts_)])
        if pstr_free[0] is not None:
            nc.tensor.wait_ge(sv.h, pstr_free[0])
        for k in range(FC):
            nc.tensor.transpose(pstr[:, k], xn1_bf[:, bb, k * 128:(k + 1) * 128], ident[:])
        nc.tensor.drain().then_inc(pe.h, 1)
        pe.inc()
        nc.vector.wait_ge(pe.h, pe.v)
        if c_idx >= 2:
            nc.vector.wait_ge(so.h, so.v)
        nc.vector.tensor_copy(stg[bb][:], pstr[:])
        nc.vector.drain().then_inc(sv.h, 1)
        sv.inc()
        pstr_free[0] = sv.v
        nc.sync.wait_ge(sv.h, sv.v)
        nc.sync.dma_start(out=xnfm1_h[:, :, c_idx * 128:(c_idx + 1) * 128].rearrange("k p t -> p k t"),
                          in_=stg[bb][:]).then_inc(so.h, 16)
        so.inc(16)

    router_pe = pe.v
    stageB_done_pe = router_pe
    m_next = 0
    xn_free = {}
    xn_tgt = {}

    def load_xn(tt):
        bx = tt % 2
        fr = xn_free.get(bx)
        if fr is not None:
            nc.sync.wait_ge(pe.h, fr)
        elif tt > 0:
            nc.sync.wait_ge(pe.h, pe.v)
        for k in range(FC):
            nc.sync.dma_start(out=xnp[bx][:, k],
                              in_=xnfm_h[k, :, tt * 1024:(tt + 1) * 1024]).then_inc(ldrp[bx].h, 16)
            ldrp[bx].inc(16)
        xn_tgt[tt] = ldrp[bx].v

    load_xn(0)
    nc.tensor.wait_ge(ldrp[0].h, xn_tgt[0])
    for tc8 in range(8):
        emit_router_chunk(0, tc8)
    for tt in range(4):
        nc.tensor.wait_ge(ldrp[tt % 2].h, xn_tgt[tt])
        if tt + 1 < 4:
            load_xn(tt + 1)
        for e in range(NE):
            m0 = (tt * NE + e) * 16
            nc.gpsimd.wait_ge(pe.h, stageB_done_pe)
            nc.gpsimd.dma_start(out=w2s[:], in_=w2T_r[e]).then_inc(ldw2.h, 16)
            ldw2.inc(16)
            w2_tgt = ldw2.v
            while m_next < m0 + 2:
                issue_w13(m_next)
                m_next += 1
            for i in range(16):
                m = m0 + i
                bw = m % 2
                nc.tensor.wait_ge(ldgp[bw].h, load_done_tgt[m])
                for n in range(2):
                    bp = n
                    fa = psA_free.get(bp)
                    if fa is not None:
                        nc.tensor.wait_ge(sa.h, fa[0])
                        nc.tensor.wait_ge(sv.h, fa[1])
                    tok0 = tt * 1024 + n * 512
                    for k in range(FC):
                        nc.tensor.matmul(ps_h1[bp][:], w1s[bw][:, k], xnp[tt % 2][:, k, tok0 - tt * 1024:tok0 - tt * 1024 + 512],
                                         start=(k == 0), stop=(k == FC - 1))
                    for k in range(FC):
                        nc.tensor.matmul(ps_h3[bp][:], w3s[bw][:, k], xnp[tt % 2][:, k, tok0 - tt * 1024:tok0 - tt * 1024 + 512],
                                         start=(k == 0), stop=(k == FC - 1))
                    nc.tensor.drain().then_inc(pe.h, 1)
                    pe.inc()
                    nc.scalar.wait_ge(pe.h, pe.v)
                    fa2 = psA_free.get(bp)
                    if fa2 is not None:
                        nc.scalar.wait_ge(sv.h, fa2[1])
                    nc.scalar.activation(tmp_bf[:, bp], ps_h1[bp][:], AF.Silu)
                    nc.scalar.drain().then_inc(sa.h, 1)
                    sa.inc()
                    nc.vector.wait_ge(sa.h, sa.v)
                    nc.vector.tensor_tensor(act[:, i, n * 512:(n + 1) * 512], tmp_bf[:, bp], ps_h3[bp][:], ALU.mult)
                    nc.vector.drain().then_inc(sv.h, 1)
                    sv.inc()
                    psA_free[bp] = (sa.v, sv.v)
                a_done_pe[m] = pe.v
                if m_next < len(flat) and m_next <= m + 2:
                    issue_w13(m_next)
                    m_next += 1
            nc.tensor.wait_ge(ldw2.h, w2_tgt)
            nc.tensor.wait_ge(sv.h, sv.v)
            for ts_ in range(8):
                for half in range(2):
                    bp = (ts_ * 2 + half) % 2
                    fy = psY_free.get(bp)
                    if fy is not None:
                        nc.tensor.wait_ge(sv.h, fy)
                    for i in range(16):
                        nc.tensor.matmul(ps_y[bp][:], act[:, i, ts_ * 128:(ts_ + 1) * 128],
                                         w2s[:, i, half * 512:(half + 1) * 512],
                                         start=(i == 0), stop=(i == 15))
                    nc.tensor.drain().then_inc(pe.h, 1)
                    pe.inc()
                    nc.vector.wait_ge(pe.h, pe.v)
                    wcol = wexp[:, tt * 8 + ts_, e:e + 1]
                    dst = gacc[:, ts_, half * 512:(half + 1) * 512]
                    if e == 0:
                        if gall_sa[0] is not None:
                            nc.vector.wait_ge(sa.h, gall_sa[0])
                        nc.vector.tensor_scalar(dst, ps_y[bp][:], wcol, None, ALU.mult)
                    else:
                        nc.vector.scalar_tensor_tensor(dst, ps_y[bp][:], wcol, dst, ALU.mult, ALU.add)
                    nc.vector.drain().then_inc(sv.h, 1)
                    sv.inc()
                    psY_free[bp] = sv.v
            stageB_done_pe = pe.v
            if tt + 1 < 4 and e >= 4:
                if e == 4:
                    nc.tensor.wait_ge(ldrp[(tt + 1) % 2].h, xn_tgt[tt + 1])
                for j in range(2):
                    emit_router_chunk(tt + 1, (tt + 1) * 8 + (e - 4) * 2 + j)
            if tt > 0:
                emit_combine_front(tt - 1, e)
                if e > 0:
                    emit_combine_back(tt - 1, e - 1)
            if e == NE - 1:
                if tt > 0:
                    emit_combine_back(tt - 1, NE - 1)
                emit_gall(tt)
                xn_free[tt % 2] = pe.v

    for c8 in range(8):
        emit_combine_front(3, c8)
        emit_combine_back(3, c8)
    nc.sync.wait_ge(sv.h, sv.v)
    nc.sync.dma_start(out=rrho_h, in_=rr[:]).then_inc(so.h, 16)
    so.inc(16)
    nc.sync.wait_ge(so.h, so.v)
    nc.sync.wait_ge(lds.h, lds.v)


def build_p2():
    p = Prog()
    io = {
        "xnfm": p.hbm("xnfm", [FC, 128, T], BF16),
        "x": p.hbm("x", [T, D], F32),
        "s0": p.hbm("s0", [T, D], F32),
        "wgT": p.hbm("wgT", [D, 8], F32),
        "w1T": p.hbm("w1T", [8, D, 2048], F32),
        "w3T": p.hbm("w3T", [8, D, 2048], F32),
        "w2T": p.hbm("w2T", [8, 2048, D], F32),
        "x1": p.hbm("x1", [T, D], F32, kind="ExternalOutput"),
        "xnfm1": p.hbm("xnfm1", [FC, 128, T], BF16, kind="ExternalOutput"),
        "rrho": p.hbm("rrho", [128, TC], F32, kind="ExternalOutput"),
    }
    p.consts = {"IDENT": p.hbm("IDENT", [128, 128], BF16)}
    emit_p2(p, io)
    p.done()
    return p.nc


def prep_shared(inputs):
    f = lambda a: np.ascontiguousarray(np.asarray(a, dtype=np.float32))
    d = {}
    d["wgT"] = f(np.asarray(inputs["L0_ffn_wg"]).T)
    d["w1T"] = f(np.asarray(inputs["L0_ffn_w1"]).transpose(0, 2, 1))
    d["w3T"] = f(np.asarray(inputs["L0_ffn_w3"]).transpose(0, 2, 1))
    d["w2T"] = f(np.asarray(inputs["L0_ffn_w2"]).transpose(0, 2, 1))
    return d


def emit_p1(p, io, pre="p1_"):
    nc = p.nc
    xnfm_h, s0_h = io["xnfm"], io["s0"]
    WqD_h, WkD_h, Wv_h, Wo_h = io["WqD"], io["WkD"], io["Wv"], io["Wo"]
    G1_h, G2_h, thetaD_h = io["G1"], io["G2"], io["thetaD"]
    gam_h, bet_h = io["gam"], io["bet"]

    ident = p.sbuf(pre + "ident", [128, 128], BF16)
    triu4 = p.sbuf(pre + "triu4", [128, 4, 128], BF16)
    xn = [p.sbuf(pre + f"xn{i}", [128, FC, 512], BF16) for i in range(2)]
    wq_s = [p.sbuf(pre + f"wqs{i}", [128, FC, 128], BF16) for i in range(2)]
    wk_s = [p.sbuf(pre + f"wks{i}", [128, FC, 128], BF16) for i in range(2)]
    Wv_sb = p.sbuf(pre + "Wv", [128, FC, 1024], BF16)
    Wo_sb = p.sbuf(pre + "Wo", [128, FC, 1024], BF16)
    G1_sb = p.sbuf(pre + "G1", [128, FC, 64], BF16)
    G2_sb = p.sbuf(pre + "G2", [64, 1024], BF16)
    thD = p.sbuf(pre + "thD", [1, 2048], F32)
    phase = p.sbuf(pre + "phase", [128, 1], F32)
    gam_sb = p.sbuf(pre + "gam", [128, 1024], BF16)
    bet_sb = p.sbuf(pre + "bet", [128, 1024], BF16)
    B1 = p.sbuf(pre + "B1", [128, 16, 512], BF16)
    B2 = p.sbuf(pre + "B2", [128, 16, 512], BF16)
    carS = p.sbuf(pre + "carS", [128, 16, 8], F32)
    carC = p.sbuf(pre + "carC", [128, 16, 8], F32)
    tb_f = p.sbuf(pre + "tbf", [128, 512], F32)
    tb_g = p.sbuf(pre + "tbg", [128, 512], F32)
    tb_i = p.sbuf(pre + "tbi", [128, 512], I32)
    cs = p.sbuf(pre + "cs", [128, 512], BF16)
    tmp5 = p.sbuf(pre + "tmp5", [128, 512], BF16)
    qd = p.sbuf(pre + "qd", [128, 16, 512], BF16)
    kd = p.sbuf(pre + "kd", [128, 16, 512], BF16)
    ktm = p.sbuf(pre + "ktm", [128, 4, 16, 128], BF16)
    v_tm = p.sbuf(pre + "vtm", [128, 4, 1024], BF16)
    u_sb = p.sbuf(pre + "u", [64, 512], BF16)
    gate = p.sbuf(pre + "gate", [128, 4, 1024], BF16)
    snap = p.sbuf(pre + "snap", [128, 1024], BF16)
    stb = p.sbuf(pre + "stb", [128, 2, 4, 128], BF16)
    ofm = p.sbuf(pre + "ofm", [128, FC, 128], BF16)
    s0g = [p.sbuf(pre + f"s0g{i}", [128, 1024], F32) for i in range(2)]
    sout = [p.sbuf(pre + f"sout{i}", [128, 1024], F32) for i in range(2)]
    st = p.sbuf(pre + "st", [128, 8], F32)
    junk = p.sbuf(pre + "junk", [128, 1024], BF16)
    cpool = p.sbuf(pre + "cpool", [128, 2], F32)
    sb_dt = p.sbuf(pre + "sbdt", [1, 512], F32)
    sb_sup = p.sbuf(pre + "sbsup", [1, 8], F32)
    zrow = p.sbuf(pre + "zrow", [1, 512], BF16)

    ps_S = p.psum(pre + "psS", [128, 1024], F32)
    ps_of = p.psum(pre + "psof", [128, 1024], F32)
    ps_st = p.psum(pre + "psst", [128, 512], F32)
    ps_s0 = p.psum(pre + "pss0", [128, 512], F32)
    ps_w = p.psum(pre + "psw", [128, 512], F32)
    ps_tr = p.psum(pre + "pstr", [128, 512], BF16)

    ldr = p.sem(pre + "ldr")
    lds = p.sem(pre + "lds")
    ldg = p.sem(pre + "ldg")
    ldgp1 = [p.sem(pre + "ldgp1a"), p.sem(pre + "ldgp1b")]
    ldgr = p.sem(pre + "ldgr")
    so = p.sem(pre + "so")
    sv = p.sem(pre + "sv")
    sa = p.sem(pre + "sa")
    pe = p.sem(pre + "pe")

    TWO_PI = 6.283185307179586
    INV2PI = 1.0 / TWO_PI

    for ap_dst, src in [(ident[:], p.consts["IDENT"]),
                        (triu4[:], p.consts["TRIU4"]),
                        (Wv_sb[:], Wv_h.rearrange("(k pp) o -> pp k o", pp=128)),
                        (Wo_sb[:], Wo_h.rearrange("(k pp) o -> pp k o", pp=128)),
                        (G1_sb[:], G1_h.rearrange("(k pp) o -> pp k o", pp=128)),
                        (G2_sb[:], G2_h)]:
        nc.gpsimd.dma_start(out=ap_dst, in_=src).then_inc(ldgr.h, 16)
        ldgr.inc(16)
    for ap_dst, src in [(gam_sb[:], gam_h), (bet_sb[:], bet_h)]:
        nc.gpsimd.dma_start(out=ap_dst, in_=src).then_inc(ldgr.h, 16)
        ldgr.inc(16)
    for ap_dst, src in [(thD[:], thetaD_h), (phase[:], p.consts["PHASE"]),
                        (sb_dt[:], p.consts["DTIDX"]), (sb_sup[:], p.consts["SUPIDX"])]:
        nc.sync.dma_start(out=ap_dst, in_=src).then_inc(ldr.h, 16)
        ldr.inc(16)
    nc.vector.memset(cpool[:, 0:1], TWO_PI)
    nc.vector.drain()
    nc.vector.memset(zrow[:], 0.0)
    nc.vector.drain().then_inc(sv.h, 1)
    sv.inc()

    def psum_zero(ps_ap_list):
        for ap_ in ps_ap_list:
            nc.tensor.matmul(ap_, zrow[0:1, 0:128], zrow[0:1, :ap_.shape[-1]],
                             start=True, stop=False, skip_group_check=True)

    def fold_sin(dst_bf, src_ps_or_sb, extra_quarter, width, use_phase):
        w = width
        nc.vector.wait_ge(sa.h, sa.v)
        if use_phase:
            nc.vector.tensor_scalar(tb_f[:, :w], src_ps_or_sb, INV2PI, phase[:, 0:1], ALU.mult, ALU.add)
        else:
            nc.vector.tensor_scalar(tb_f[:, :w], src_ps_or_sb, INV2PI, None, ALU.mult)
        nc.vector.drain()
        if extra_quarter:
            nc.vector.tensor_scalar(tb_f[:, :w], tb_f[:, :w], 1.0, 0.25, ALU.mult, ALU.add)
            nc.vector.drain()
        nc.vector.tensor_copy(tb_i[:, :w], tb_f[:, :w])
        nc.vector.drain()
        nc.vector.tensor_copy(tb_g[:, :w], tb_i[:, :w])
        nc.vector.drain()
        nc.vector.tensor_tensor(tb_f[:, :w], tb_f[:, :w], tb_g[:, :w], ALU.subtract)
        nc.vector.drain()
        nc.vector.tensor_scalar(tb_g[:, :w], tb_f[:, :w], 0.5, None, ALU.is_gt)
        nc.vector.drain()
        nc.vector.scalar_tensor_tensor(tb_f[:, :w], tb_g[:, :w], -1.0, tb_f[:, :w], ALU.mult, ALU.add)
        nc.vector.drain()
        nc.vector.tensor_scalar(tb_g[:, :w], tb_f[:, :w], -0.5, None, ALU.is_lt)
        nc.vector.drain()
        nc.vector.tensor_tensor(tb_f[:, :w], tb_f[:, :w], tb_g[:, :w], ALU.add)
        nc.vector.drain().then_inc(sv.h, 1)
        sv.inc()
        nc.scalar.wait_ge(sv.h, sv.v)
        nc.scalar.activation(dst_bf, tb_f[:, :w], AF.Sin, bias=0.0, scale=cpool[:, 0:1])
        nc.scalar.drain().then_inc(sa.h, 1)
        sa.inc()

    nc.tensor.wait_ge(ldr.h, ldr.v)
    nc.tensor.wait_ge(ldgr.h, ldgr.v)
    nc.tensor.wait_ge(sv.h, sv.v)
    for h in range(16):
        nc.tensor.wait_ge(sv.h, sv.v)
        nc.tensor.matmul(ps_w[:], thD[0:1, h * 128:(h + 1) * 128], sb_dt[0:1, :],
                         start=True, stop=True)
        nc.tensor.drain().then_inc(pe.h, 1)
        pe.inc()
        nc.vector.wait_ge(pe.h, pe.v)
        fold_sin(B1[:, h], ps_w[:], False, 512, True)
        nc.vector.wait_ge(sa.h, sa.v)
        fold_sin(B2[:, h], ps_w[:], True, 512, True)
        nc.tensor.wait_ge(sa.h, sa.v)
        nc.tensor.matmul(ps_w[:, 0:8], thD[0:1, h * 128:(h + 1) * 128], sb_sup[0:1, :],
                         start=True, stop=True)
        nc.tensor.drain().then_inc(pe.h, 1)
        pe.inc()
        nc.vector.wait_ge(pe.h, pe.v)
        fold_sin(carS[:, h], ps_w[:, 0:8], False, 8, False)
        nc.vector.wait_ge(sa.h, sa.v)
        fold_sin(carC[:, h], ps_w[:, 0:8], True, 8, False)


    def barrier():
        nc.tensor.drain().then_inc(pe.h, 1)
        pe.inc()
        nc.vector.drain().then_inc(sv.h, 1)
        sv.inc()
        nc.scalar.drain().then_inc(sa.h, 1)
        sa.inc()
        nc.tensor.wait_ge(sv.h, sv.v)
        nc.tensor.wait_ge(sa.h, sa.v)
        nc.vector.wait_ge(pe.h, pe.v)
        nc.vector.wait_ge(sa.h, sa.v)
        nc.scalar.wait_ge(pe.h, pe.v)
        nc.scalar.wait_ge(sv.h, sv.v)

    WqD_r = WqD_h.rearrange("(k pp) m -> pp k m", pp=128)
    WkD_r = WkD_h.rearrange("(k pp) m -> pp k m", pp=128)
    proj_done_pe = {}

    wqk_tgt = {}

    def issue_wqk(m):
        h = m % 16
        bw = m % 2
        if m - 2 in proj_done_pe:
            nc.gpsimd.wait_ge(pe.h, proj_done_pe[m - 2])
        nc.gpsimd.dma_start(out=wq_s[bw][:], in_=WqD_r[:, :, h * 128:(h + 1) * 128]).then_inc(ldgp1[bw].h, 16)
        nc.gpsimd.dma_start(out=wk_s[bw][:], in_=WkD_r[:, :, h * 128:(h + 1) * 128]).then_inc(ldgp1[bw].h, 16)
        ldgp1[bw].inc(32)
        wqk_tgt[m] = ldgp1[bw].v

    x_sr = xnfm_h.rearrange("k p t -> p k t")
    s0_r = s0_h.rearrange("(c pp) d -> c pp d", pp=128)
    def emit_s5b(t4, ci):
        c_idx = ci
        bo = c_idx % 2
        bos = ci % 2
        nc.vector.reduce_sum(st[:, 0:1], s0g[bos][:], axis=AX.X)
        nc.vector.drain().then_inc(sv.h, 1)
        sv.inc()
        nc.scalar.wait_ge(sv.h, sv.v)
        nc.scalar.activation(junk[:], s0g[bos][:], AF.Square, accum_out=st[:, 1:2])
        nc.scalar.drain().then_inc(sa.h, 1)
        sa.inc()
        nc.vector.wait_ge(sa.h, sa.v)
        nc.vector.tensor_scalar(st[:, 2:3], st[:, 0:1], 1.0 / D, None, ALU.mult)
        nc.vector.tensor_scalar(st[:, 3:4], st[:, 1:2], 1.0 / D, None, ALU.mult)
        nc.vector.drain()
        nc.vector.scalar_tensor_tensor(st[:, 4:5], st[:, 2:3], 1.0, st[:, 2:3], ALU.mult, ALU.mult)
        nc.vector.drain()
        nc.vector.tensor_tensor(st[:, 4:5], st[:, 3:4], st[:, 4:5], ALU.subtract)
        nc.vector.drain()
        nc.vector.tensor_scalar(st[:, 4:5], st[:, 4:5], 1.0, EPS_LN, ALU.mult, ALU.add)
        nc.vector.drain().then_inc(sv.h, 1)
        sv.inc()
        nc.scalar.wait_ge(sv.h, sv.v)
        nc.scalar.activation(st[:, 5:6], st[:, 4:5], AF.Sqrt)
        nc.scalar.drain().then_inc(sa.h, 1)
        sa.inc()
        nc.vector.wait_ge(sa.h, sa.v)
        nc.vector.reciprocal(st[:, 6:7], st[:, 5:6])
        nc.vector.drain()
        if c_idx >= 2:
            nc.vector.wait_ge(so.h, so.v)
        nc.vector.tensor_scalar(sout[bo][:], s0g[bos][:], st[:, 2:3], st[:, 6:7],
                        ALU.subtract, ALU.mult)
        nc.vector.drain()
        nc.vector.tensor_tensor(sout[bo][:], sout[bo][:], gam_sb[:], ALU.mult)
        nc.vector.drain()
        nc.vector.tensor_tensor(sout[bo][:], sout[bo][:], bet_sb[:], ALU.add)
        nc.vector.drain().then_inc(sv.h, 1)
        sv.inc()
        nc.sync.wait_ge(sv.h, sv.v)
        nc.sync.dma_start(out=s0_r[c_idx], in_=sout[bo][:]).then_inc(so.h, 16)
        so.inc(16)

    m_next = 0
    xn_used_pe = [None, None]
    ci_global = 0
    s5_pending = None
    nc.tensor.wait_ge(sv.h, sv.v)
    psum_zero([ps_S[:, 0:512], ps_S[:, 512:1024]])
    for sc in range(NSC):
        barrier()
        bx = sc % 2
        if xn_used_pe[bx] is not None:
            nc.sync.wait_ge(pe.h, xn_used_pe[bx])
        nc.sync.dma_start(out=xn[bx][:], in_=x_sr[:, :, sc * 512:(sc + 1) * 512]).then_inc(lds.h, 16)
        lds.inc(16)
        while m_next < sc * 16 + 2:
            issue_wqk(m_next)
            m_next += 1
        nc.tensor.wait_ge(lds.h, lds.v)
        evac_done = {}
        tr_done = {}

        def emit_lrpe(h):
            nc.vector.wait_ge(sa.h, evac_done[h % 2])
            nc.vector.tensor_scalar(tmp5[:], B2[:, h], carS[:, h, sc:sc + 1], None, ALU.mult)
            nc.vector.drain()
            nc.vector.scalar_tensor_tensor(cs[:], B1[:, h], carC[:, h, sc:sc + 1], tmp5[:],
                                           ALU.mult, ALU.add)
            nc.vector.drain()
            nc.vector.tensor_tensor(qd[:, h], qd[:, h], cs[:], ALU.mult)
            nc.vector.drain()
            nc.vector.tensor_tensor(kd[:, h], kd[:, h], cs[:], ALU.mult)
            nc.vector.drain().then_inc(sv.h, 1)
            sv.inc()
            return sv.v

        lrpe_done = {}

        def emit_ktr(h):
            nc.tensor.wait_ge(sv.h, lrpe_done[h])
            td = tr_done.get("prev")
            if td is not None:
                nc.tensor.wait_ge(sv.h, td)
            for t4_ in range(4):
                nc.tensor.transpose(ps_tr[:, t4_ * 128:(t4_ + 1) * 128],
                                    kd[:, h, t4_ * 128:(t4_ + 1) * 128], ident[:])
            nc.tensor.drain().then_inc(pe.h, 1)
            pe.inc()
            nc.vector.wait_ge(pe.h, pe.v)
            nc.vector.tensor_copy(ktm[:].rearrange("p t4 hh q -> p hh t4 q")[:, h],
                                  ps_tr[:].rearrange("p (t4 q) -> p t4 q", t4=4))
            nc.vector.drain().then_inc(sv.h, 1)
            sv.inc()
            tr_done["prev"] = sv.v

        for h in range(16):
            m = sc * 16 + h
            pq_ = ps_of[:, 0:512] if h % 2 == 0 else ps_s0[:]
            pk_ = ps_of[:, 512:1024] if h % 2 == 0 else ps_w[:]
            nc.tensor.wait_ge(ldgp1[m % 2].h, wqk_tgt[m])
            ed = evac_done.get(h % 2)
            if ed is not None:
                nc.tensor.wait_ge(sa.h, ed)
            for k in range(FC):
                nc.tensor.matmul(pq_, wq_s[m % 2][:, k], xn[bx][:, k],
                                 start=(k == 0), stop=(k == FC - 1))
            for k in range(FC):
                nc.tensor.matmul(pk_, wk_s[m % 2][:, k], xn[bx][:, k],
                                 start=(k == 0), stop=(k == FC - 1))
            nc.tensor.drain().then_inc(pe.h, 1)
            pe.inc()
            proj_done_pe[m] = pe.v
            nc.scalar.wait_ge(pe.h, pe.v)
            nc.scalar.activation(qd[:, h], pq_, AF.Silu)
            nc.scalar.drain()
            nc.scalar.activation(kd[:, h], pk_, AF.Silu)
            nc.scalar.drain().then_inc(sa.h, 1)
            sa.inc()
            evac_done[h % 2] = sa.v
            lrpe_done[h] = emit_lrpe(h)
            if h > 0:
                emit_ktr(h - 1)
            if m_next < NSC * 16 and m_next <= m + 2:
                issue_wqk(m_next)
                m_next += 1
        emit_ktr(15)
        nc.tensor.wait_ge(sa.h, sa.v)
        for t4 in range(4):
            for half in range(2):
                nc.tensor.wait_ge(sv.h, sv.v)
                for k in range(FC):
                    nc.tensor.matmul(ps_s0[:], xn[bx][:, k, t4 * 128:(t4 + 1) * 128],
                                     Wv_sb[:, k, half * 512:(half + 1) * 512],
                                     start=(k == 0), stop=(k == FC - 1))
                nc.tensor.drain().then_inc(pe.h, 1)
                pe.inc()
                nc.vector.wait_ge(pe.h, pe.v)
                nc.vector.tensor_copy(v_tm[:, t4, half * 512:(half + 1) * 512], ps_s0[:])
                nc.vector.drain().then_inc(sv.h, 1)
                sv.inc()
        nc.tensor.wait_ge(sv.h, sv.v)
        for k in range(FC):
            nc.tensor.matmul(ps_st[0:64, :], G1_sb[:, k], xn[bx][:, k],
                             start=(k == 0), stop=(k == FC - 1))
        nc.tensor.drain().then_inc(pe.h, 1)
        pe.inc()
        nc.vector.wait_ge(pe.h, pe.v)
        nc.vector.tensor_copy(u_sb[:], ps_st[0:64, :])
        nc.vector.drain().then_inc(sv.h, 1)
        sv.inc()
        nc.tensor.wait_ge(sv.h, sv.v)
        for t4 in range(4):
            for half in range(2):
                if not (t4 == 0 and half == 0):
                    nc.tensor.wait_ge(sa.h, sa.v)
                nc.tensor.matmul(ps_st[:, :], u_sb[:, t4 * 128:(t4 + 1) * 128],
                                 G2_sb[:, half * 512:(half + 1) * 512], start=True, stop=True)
                nc.tensor.drain().then_inc(pe.h, 1)
                pe.inc()
                nc.scalar.wait_ge(pe.h, pe.v)
                nc.scalar.activation(gate[:, t4, half * 512:(half + 1) * 512], ps_st[:, :], AF.Sigmoid)
                nc.scalar.drain().then_inc(sa.h, 1)
                sa.inc()
        for t4 in range(4):
            ci = ci_global
            ci_global += 1
            if ci > 0:
                nc.vector.wait_ge(pe.h, pe.v)
                nc.vector.tensor_copy(snap[:], ps_S[:])
                nc.vector.drain().then_inc(sv.h, 1)
                sv.inc()
            snap_sv = sv.v
            nc.tensor.wait_ge(sv.h, sv.v)
            nc.tensor.wait_ge(sa.h, sa.v)
            psum_zero([ps_of[:, 0:512], ps_of[:, 512:1024]])
            mask_done = {}

            def emit_st(hg):
                pst = ps_st if hg % 2 == 0 else ps_w
                md = mask_done.get(hg % 2)
                if md is not None:
                    nc.tensor.wait_ge(sv.h, md[1])
                psum_zero([pst[:]])
                for hh in range(4):
                    h = hg * 4 + hh
                    nc.tensor.matmul(pst[:, hh * 128:(hh + 1) * 128],
                                     kd[:, h, t4 * 128:(t4 + 1) * 128],
                                     qd[:, h, t4 * 128:(t4 + 1) * 128],
                                     start=False, stop=(hh == 3), skip_group_check=True)
                nc.tensor.drain().then_inc(pe.h, 1)
                pe.inc()
                nc.vector.wait_ge(pe.h, pe.v)
                nc.vector.tensor_tensor(stb[:, hg % 2], pst[:].rearrange("p (hh q) -> p hh q", hh=4),
                                        triu4[:], ALU.mult)
                nc.vector.drain().then_inc(sv.h, 1)
                sv.inc()
                mask_done[hg % 2] = (sv.v, sv.v)

            def emit_o(hg):
                nc.tensor.wait_ge(sv.h, mask_done[hg % 2][0])
                for hh in range(4):
                    h = hg * 4 + hh
                    colg = (0, 64) if (h % 2) == 1 else None
                    of_ap = ps_of[(h % 2) * 64:(h % 2) * 64 + 64, (h // 2) * 128:(h // 2) * 128 + 128]
                    if ci > 0:
                        nc.tensor.matmul(of_ap, snap[:, h * 64:(h + 1) * 64],
                                         qd[:, h, t4 * 128:(t4 + 1) * 128],
                                         start=False, stop=False, tile_position=colg,
                                         skip_group_check=True)
                    nc.tensor.matmul(of_ap, v_tm[:, t4, h * 64:(h + 1) * 64], stb[:, hg % 2, hh],
                                     start=False, stop=(h in (14, 15)), tile_position=colg,
                                     skip_group_check=True)
                    nc.tensor.matmul(ps_S[:, h * 64:(h + 1) * 64], ktm[:, t4, h, :],
                                     v_tm[:, t4, h * 64:(h + 1) * 64],
                                     start=False, stop=(ci == TC - 1 and h in (14, 15)),
                                     skip_group_check=True)

            prev_hg = None
            for hg in range(4):
                emit_st(hg)
                if prev_hg is not None:
                    emit_o(prev_hg)
                prev_hg = hg
            emit_o(prev_hg)
            nc.tensor.drain().then_inc(pe.h, 1)
            pe.inc()
            nc.vector.wait_ge(pe.h, pe.v)
            nc.vector.tensor_copy(ofm[:], ps_of[:].rearrange("p (m q) -> p m q", m=8))
            nc.vector.drain().then_inc(sv.h, 1)
            sv.inc()
            c_idx = ci
            bo = c_idx % 2
            nc.tensor.wait_ge(sv.h, sv.v)
            for half in range(2):
                if half == 1:
                    nc.tensor.wait_ge(sv.h, sv.v)
                for mc in range(FC):
                    nc.tensor.matmul(ps_s0[:], ofm[:, mc], Wo_sb[:, mc, half * 512:(half + 1) * 512],
                                     start=(mc == 0), stop=(mc == FC - 1))
                nc.tensor.drain().then_inc(pe.h, 1)
                pe.inc()
                nc.vector.wait_ge(pe.h, pe.v)
                nc.vector.tensor_tensor(s0g[ci % 2][:, half * 512:(half + 1) * 512], ps_s0[:],
                                        gate[:, t4, half * 512:(half + 1) * 512], ALU.mult)
                nc.vector.drain().then_inc(sv.h, 1)
                sv.inc()
            if s5_pending is not None:
                emit_s5b(*s5_pending)
            s5_pending = (t4, ci)
        xn_used_pe[bx] = pe.v
    if s5_pending is not None:
        emit_s5b(*s5_pending)
        s5_pending = None
    nc.sync.wait_ge(so.h, so.v)


def make_consts_p1():
    c = {}
    c["TRIU4"] = np.broadcast_to(
        np.triu(np.ones((128, 128), dtype=np.float32))[:, None, :], (128, 4, 128)).copy()
    c["PHASE"] = np.where(np.arange(128) < 64, 0.25, 0.0).astype(np.float32).reshape(128, 1)
    c["DTIDX"] = np.arange(512, dtype=np.float32).reshape(1, 512)
    c["SUPIDX"] = (np.arange(8, dtype=np.float32) * 512.0).reshape(1, 8)
    return c


def prep_p1(inputs):
    f = lambda a: np.ascontiguousarray(np.asarray(a, dtype=np.float32))
    d = {}
    Wq = np.asarray(inputs["L0_attn_Wq"], dtype=np.float32)
    Wk = np.asarray(inputs["L0_attn_Wk"], dtype=np.float32)
    idx = (np.arange(2048) // 128) * 64 + (np.arange(2048) % 64)
    d["WqD"] = f(Wq[:, idx])
    d["WkD"] = f(Wk[:, idx])
    d["Wv"] = f(inputs["L0_attn_Wv"])
    d["Wo"] = f(inputs["L0_attn_Wo"])
    d["G1"] = f(inputs["L0_attn_G1"])
    d["G2"] = f(inputs["L0_attn_G2"])
    th = np.asarray(inputs["L0_attn_theta"], dtype=np.float32).reshape(16, 64)
    thD = th[np.arange(2048) // 128, np.arange(2048) % 64]
    d["thetaD"] = f(thD.reshape(1, 2048))
    d["gam"] = f(np.broadcast_to(np.asarray(inputs["L0_attn_gamma"], dtype=np.float32), (128, D)))
    d["bet"] = f(np.broadcast_to(np.asarray(inputs["L0_attn_beta"], dtype=np.float32), (128, D)))
    return d


def build_p1():
    p = Prog()
    io = {
        "xnfm": p.hbm("xnfm", [FC, 128, T], BF16),
        "WqD": p.hbm("WqD", [D, 2048], F32),
        "WkD": p.hbm("WkD", [D, 2048], F32),
        "Wv": p.hbm("Wv", [D, D], F32),
        "Wo": p.hbm("Wo", [D, D], F32),
        "G1": p.hbm("G1", [D, 64], F32),
        "G2": p.hbm("G2", [64, D], F32),
        "thetaD": p.hbm("thetaD", [1, 2048], F32),
        "gam": p.hbm("gam", [128, D], F32),
        "bet": p.hbm("bet", [128, D], F32),
        "s0": p.hbm("s0", [T, D], F32, kind="ExternalOutput"),
    }
    cst = make_consts_p1()
    p.consts = {
        "IDENT": p.hbm("IDENT", [128, 128], BF16),
        "TRIU4": p.hbm("TRIU4", [128, 4, 128], F32),
        "PHASE": p.hbm("PHASE", [128, 1], F32),
        "DTIDX": p.hbm("DTIDX", [1, 512], F32),
        "SUPIDX": p.hbm("SUPIDX", [1, 8], F32),
    }
    emit_p1(p, io)
    p.done()
    return p.nc


ALIBI_C_SAFE = 12.0
SQRT_F = float(np.sqrt(192.0))


def _slopes16():
    return [8.0 * 2.0 ** (-(h + 1) / 2.0) for h in range(16)]


def _win_chunks(h):
    s8 = _slopes16()[h]
    return max(1, int(np.ceil(30.0 / (128.0 * s8)))) + 1


def emit_p3(p, io, pre="p3_"):
    nc = p.nc
    xnfm1_h, s1_h = io["xnfm1"], io["s1"]
    Wq1_h, Wk1_h, Wv1_h = io["Wq1"], io["Wk1"], io["Wv1"]
    bq_h, bk_h, bv_h, bout_h = io["bq1"], io["bk1"], io["bv1"], io["bout"]
    WoutF_h = io["WoutF"]
    COS_h, SINF_h = io["COS"], io["SINF"]
    qfm_h, kfm_h, vaug_h = io["qfm"], io["kfm"], io["vaug"]

    es1 = ExitStack()
    sb1 = lambda n, shp, dt=F32: es1.enter_context(nc.sbuf_tensor(pre + n, shp, dt))
    xnfm = sb1("xnfm", [128, FC, T], BF16)
    Wq_sb = sb1("Wq", [128, FC, 1024], BF16)
    Wk_sb = sb1("Wk", [128, FC, 1024], BF16)
    Wv_sb = sb1("Wv", [128, FC, 1024], BF16)
    brow = sb1("brow", [1, 3, 1024], BF16)
    onesr = sb1("onesr", [2, 512], BF16)
    ident1 = sb1("ident1", [128, 128], BF16)
    rtab = [sb1(f"rtab{i}", [128, 2, 1024], BF16) for i in range(2)]
    q_tm = sb1("qtm", [128, 1024], F32)
    k_tm = sb1("ktm", [128, 1024], F32)
    tmpr = sb1("tmpr", [128, 1024], F32)
    qp_bf = sb1("qpbf", [128, 2, 1024], BF16)
    kp_bf = sb1("kpbf", [128, 2, 1024], BF16)
    stg1 = [sb1(f"stg1{i}", [128, 2, FC, 128], BF16) for i in range(2)]
    vstg = [sb1(f"vstg{i}", [128, 16, 65], BF16) for i in range(2)]
    ps_q = [es1.enter_context(nc.psum_tensor(pre + f"psq{i}", [128, 512], F32)) for i in range(2)]
    ps_k = [es1.enter_context(nc.psum_tensor(pre + f"psk{i}", [128, 512], F32)) for i in range(2)]
    ps_v = [es1.enter_context(nc.psum_tensor(pre + f"psv{i}", [128, 512], F32)) for i in range(2)]
    ps_t1 = es1.enter_context(nc.psum_tensor(pre + "pst1", [128, 512], BF16))
    ps_t2 = es1.enter_context(nc.psum_tensor(pre + "pst2", [128, 512], BF16))

    ldr = p.sem(pre + "ldr")
    lds = p.sem(pre + "lds")
    so = p.sem(pre + "so")
    sv = p.sem(pre + "sv")
    sa = p.sem(pre + "sa")
    pe = p.sem(pre + "pe")
    sg = p.sem(pre + "sg")

    ldrq = [p.sem(pre + f"ldrq{i}") for i in range(3)]
    xn_r = xnfm1_h.rearrange("k p t -> p k t")
    if P3_SPLIT_LOADS:
        batches = (
            (nc.gpsimd, ldr, [(Wq_sb[:], Wq1_h.rearrange("(k pp) o -> pp k o", pp=128)),
                              (Wk_sb[:], Wk1_h.rearrange("(k pp) o -> pp k o", pp=128)),
                              (Wv_sb[:], Wv1_h.rearrange("(k pp) o -> pp k o", pp=128)),
                              (brow[0:1, 0], bq_h), (brow[0:1, 1], bk_h), (brow[0:1, 2], bv_h)]),
            (nc.sync, ldrq[0], [(xnfm[:, :, 0:2048], xn_r[:, :, 0:2048]),
                                (ident1[:], p.consts["IDENT"]),
                                (onesr[:], p.consts["ONES2"])]),
            (nc.scalar, ldrq[1], [(xnfm[:, :, 2048:4096], xn_r[:, :, 2048:4096])]),
        )
    else:
        batches = (
            (nc.gpsimd, ldr, [(xnfm[:], xn_r),
                              (Wq_sb[:], Wq1_h.rearrange("(k pp) o -> pp k o", pp=128)),
                              (Wk_sb[:], Wk1_h.rearrange("(k pp) o -> pp k o", pp=128)),
                              (Wv_sb[:], Wv1_h.rearrange("(k pp) o -> pp k o", pp=128)),
                              (ident1[:], p.consts["IDENT"]),
                              (brow[0:1, 0], bq_h), (brow[0:1, 1], bk_h), (brow[0:1, 2], bv_h),
                              (onesr[:], p.consts["ONES2"])]),
        )
    for q_eng, q_sem, pairs in batches:
        for dst, src in pairs:
            q_eng.dma_start(out=dst, in_=src).then_inc(q_sem.h, 16)
            q_sem.inc(16)

    for eng in (nc.tensor, nc.scalar, nc.vector, nc.gpsimd):
        eng.wait_ge(ldr.h, ldr.v)
        for qs in ldrq:
            eng.wait_ge(qs.h, qs.v)
    rt_used = [None, None]
    evq = [None, None]
    evk = [None, None]
    evv = [None, None]
    evt1 = [None]
    evt2 = [None]
    rope_done = [None, None]
    trq_done = [None, None]
    pend_tr = None
    for tc in range(TC):
        bb = tc % 2
        if rt_used[bb] is not None:
            nc.sync.wait_ge(sv.h, rt_used[bb])
        nc.sync.dma_start(out=rtab[bb][:, 0], in_=COS_h[tc * 128:(tc + 1) * 128, :]).then_inc(lds.h, 16)
        nc.sync.dma_start(out=rtab[bb][:, 1], in_=SINF_h[tc * 128:(tc + 1) * 128, :]).then_inc(lds.h, 16)
        lds.inc(32)
        lds_tgt = lds.v
        for tgt, W, ps_, bi, ev in ((q_tm, Wq_sb, ps_q, 0, evq), (k_tm, Wk_sb, ps_k, 1, evk)):
            for half in range(2):
                if ev[half] is not None:
                    nc.tensor.wait_ge(sv.h, ev[half])
                for k in range(FC):
                    nc.tensor.matmul(ps_[half][:], xnfm[:, k, tc * 128:(tc + 1) * 128],
                                     W[:, k, half * 512:(half + 1) * 512],
                                     start=(k == 0), stop=False)
                nc.tensor.matmul(ps_[half][:], onesr[0:1, 0:128], brow[0:1, bi, half * 512:(half + 1) * 512],
                                 start=False, stop=True)
                nc.tensor.drain().then_inc(pe.h, 1)
                pe.inc()
                nc.vector.wait_ge(pe.h, pe.v)
                nc.vector.tensor_copy(tgt[:, half * 512:(half + 1) * 512], ps_[half][:])
                nc.vector.drain().then_inc(sv.h, 1)
                sv.inc()
                ev[half] = sv.v
        bv_ = vstg[bb]
        if tc >= 2:
            nc.vector.wait_ge(so.h, so.v)
        for half in range(2):
            if evv[half] is not None:
                nc.tensor.wait_ge(sv.h, evv[half])
            for k in range(FC):
                nc.tensor.matmul(ps_v[half][:], xnfm[:, k, tc * 128:(tc + 1) * 128],
                                 Wv_sb[:, k, half * 512:(half + 1) * 512],
                                 start=(k == 0), stop=False)
            nc.tensor.matmul(ps_v[half][:], onesr[0:1, 0:128], brow[0:1, 2, half * 512:(half + 1) * 512],
                             start=False, stop=True)
            nc.tensor.drain().then_inc(pe.h, 1)
            pe.inc()
            nc.vector.wait_ge(pe.h, pe.v)
            nc.vector.tensor_copy(bv_[:, half * 8:(half + 1) * 8, 0:64],
                                  ps_v[half][:].rearrange("p (g d) -> p g d", d=64))
            nc.vector.drain().then_inc(sv.h, 1)
            sv.inc()
            evv[half] = sv.v
        nc.vector.memset(bv_[:, :, 64:65], 1.0)
        nc.vector.drain().then_inc(sv.h, 1)
        sv.inc()
        nc.sync.wait_ge(sv.h, sv.v)
        nc.sync.dma_start(out=vaug_h[:, tc], in_=bv_[:]).then_inc(so.h, 16)
        so.inc(16)
        for ti_, (src_t, dst_bf) in enumerate(((q_tm, qp_bf), (k_tm, kp_bf))):
            if ti_ == 0:
                nc.vector.wait_ge(lds.h, lds_tgt)
            if trq_done[ti_] is not None:
                nc.vector.wait_ge(pe.h, trq_done[ti_])
            a3 = src_t[:].rearrange("p (h j) -> p h j", j=64)
            t3 = tmpr[:].rearrange("p (h j) -> p h j", j=64)
            s3 = rtab[bb][:, 1].rearrange("p (h j) -> p h j", j=64)
            nc.vector.tensor_tensor(t3[:, :, 0:32], a3[:, :, 32:64], s3[:, :, 0:32], ALU.mult)
            nc.vector.drain()
            nc.vector.tensor_tensor(t3[:, :, 32:64], a3[:, :, 0:32], s3[:, :, 32:64], ALU.mult)
            nc.vector.drain()
            nc.vector.tensor_tensor(src_t[:], src_t[:], rtab[bb][:, 0], ALU.mult)
            nc.vector.drain()
            nc.vector.tensor_tensor(dst_bf[:, bb], src_t[:], tmpr[:], ALU.add)
            nc.vector.drain().then_inc(sv.h, 1)
            sv.inc()
            rope_done[ti_] = sv.v
        rt_used[bb] = sv.v

        def emit_tr_store(ptc, rdq, rdk):
            pb = ptc % 2
            sgb = stg1[pb]
            for ti, (src_bf, pst, evt_, rd) in enumerate(((qp_bf, ps_t1, evt1, rdq),
                                                          (kp_bf, ps_t2, evt2, rdk))):
                nc.tensor.wait_ge(sv.h, rd)
                for g in range(2):
                    if evt_[0] is not None:
                        nc.tensor.wait_ge(sv.h, evt_[0])
                    for k in range(4):
                        kk = g * 4 + k
                        nc.tensor.transpose(pst[:, k * 128:(k + 1) * 128],
                                            src_bf[:, pb, kk * 128:(kk + 1) * 128], ident1[:])
                    nc.tensor.drain().then_inc(pe.h, 1)
                    pe.inc()
                    nc.vector.wait_ge(pe.h, pe.v)
                    nc.vector.tensor_copy(sgb[:, ti, g * 4:(g + 1) * 4, :],
                                          pst[:].rearrange("p (k t) -> p k t", k=4))
                    nc.vector.drain().then_inc(sv.h, 1)
                    sv.inc()
                    evt_[0] = sv.v
                trq_done[ti] = pe.v
            nc.sync.wait_ge(sv.h, sv.v)
            nc.sync.dma_start(out=qfm_h[:, :, ptc * 128:(ptc + 1) * 128].rearrange("k p t -> p k t"),
                              in_=sgb[:, 0]).then_inc(so.h, 16)
            nc.sync.dma_start(out=kfm_h[:, :, ptc * 128:(ptc + 1) * 128].rearrange("k p t -> p k t"),
                              in_=sgb[:, 1]).then_inc(so.h, 16)
            so.inc(32)

        if pend_tr is not None:
            emit_tr_store(*pend_tr)
        pend_tr = (tc, rope_done[0], rope_done[1])
    emit_tr_store(*pend_tr)
    nc.sync.wait_ge(so.h, so.v)
    nc.sync.wait_ge(lds.h, lds.v)
    nc.tensor.drain().then_inc(pe.h, 1)
    pe.inc()
    nc.vector.drain().then_inc(sv.h, 1)
    sv.inc()
    nc.scalar.drain().then_inc(sa.h, 1)
    sa.inc()
    nc.gpsimd.drain().then_inc(sg.h, 1)
    sg.inc()
    for eng in (nc.tensor, nc.vector, nc.scalar, nc.gpsimd, nc.sync):
        eng.wait_ge(pe.h, pe.v)
        eng.wait_ge(sv.h, sv.v)
        eng.wait_ge(sa.h, sa.v)
        eng.wait_ge(sg.h, sg.v)
        eng.wait_ge(so.h, so.v)
    es1.close()

    es2 = ExitStack()
    sb2 = lambda n, shp, dt=F32: es2.enter_context(nc.sbuf_tensor(pre + n, shp, dt))
    kfm = sb2("kfm2", [128, FC, T], BF16)
    vaug = sb2("vaug2", [128, TC, 16, 65], BF16)
    Wout_sb = sb2("Wout", [128, FC, 1024], BF16)
    bo_row = sb2("borow", [1, 1024], BF16)
    ones2 = sb2("ones2b", [2, 512], BF16)
    ident2 = sb2("ident2", [128, 128], BF16)
    arel = sb2("arel", [128, 35, 16], F32)
    rneg = sb2("rneg", [2, 16, 512], BF16)
    mcorn = sb2("mcorn", [128, 128], F32)
    cpool2 = sb2("cpool2", [128, 1], F32)
    qt = [sb2(f"qt{i}", [128, FC, 512], BF16) for i in range(2)]
    pt_sb = sb2("ptsb", [128, 3, 512], BF16)
    corn_sb = sb2("cornsb", [128, 3, 128], F32)
    att = sb2("att", [128, 4, 1024], BF16)
    afm = sb2("afm", [128, FC, 128], BF16)
    osb = sb2("osb", [128, 4, 66], F32)
    s1st = [sb2(f"s1st{i}", [128, 1024], F32) for i in range(2)]
    ps_sc = [es2.enter_context(nc.psum_tensor(pre + f"pssc{i}", [128, 512], F32)) for i in range(3)]
    _pv3 = [es2.enter_context(nc.psum_tensor(pre + f"pspv{i}", [128, 264], F32)) for i in range(2)]
    pvv = [[_pv3[b][:, sub * 66:sub * 66 + 65] for sub in range(4)] for b in range(2)]
    ps_s1 = [es2.enter_context(nc.psum_tensor(pre + f"pss1{i}", [128, 512], F32)) for i in range(2)]
    ps_tr2 = es2.enter_context(nc.psum_tensor(pre + "pstr2", [128, 512], BF16))
    zrow3 = sb2("zrow3", [1, 264], BF16)

    ld2 = p.sem(pre + "ld2")
    ldq = p.sem(pre + "ldq")
    ld2q = [p.sem(pre + f"ld2q{i}") for i in range(3)]

    kfm_r = kfm_h.rearrange("k p t -> p k t")
    if P3_SPLIT_LOADS:
        batches2 = (
            (nc.gpsimd, ld2, [(Wout_sb[:], WoutF_h.rearrange("(k pp) o -> pp k o", pp=128)),
                              (bo_row[:], bout_h)]),
            (nc.sync, ld2q[0], [(kfm[:, :, 0:2048], kfm_r[:, :, 0:2048]),
                                (vaug[:, 0:16], vaug_h[:, 0:16]),
                                (ident2[:], p.consts["IDENT"]),
                                (arel[:], p.consts["AREL"]),
                                (rneg[:], p.consts["RNEG"]),
                                (mcorn[:], p.consts["MCORN"]),
                                (ones2[:], p.consts["ONES2"])]),
            (nc.scalar, ld2q[1], [(kfm[:, :, 2048:4096], kfm_r[:, :, 2048:4096]),
                                  (vaug[:, 16:32], vaug_h[:, 16:32])]),
        )
    else:
        batches2 = (
            (nc.gpsimd, ld2, [(kfm[:], kfm_r),
                              (vaug[:], vaug_h),
                              (Wout_sb[:], WoutF_h.rearrange("(k pp) o -> pp k o", pp=128)),
                              (bo_row[:], bout_h),
                              (ident2[:], p.consts["IDENT"]),
                              (arel[:], p.consts["AREL"]),
                              (rneg[:], p.consts["RNEG"]),
                              (mcorn[:], p.consts["MCORN"]),
                              (ones2[:], p.consts["ONES2"])]),
        )
    for q_eng, q_sem, pairs in batches2:
        for dst, src in pairs:
            q_eng.dma_start(out=dst, in_=src).then_inc(q_sem.h, 16)
            q_sem.inc(16)
    nc.vector.memset(cpool2[:, 0:1], 1.0 / SQRT_F)
    nc.vector.drain()
    nc.vector.memset(zrow3[:], 0.0)
    nc.vector.drain().then_inc(sv.h, 1)
    sv.inc()
    for eng in (nc.tensor, nc.scalar, nc.vector):
        eng.wait_ge(ld2.h, ld2.v)
        eng.wait_ge(sv.h, sv.v)
        for qs in ld2q:
            eng.wait_ge(qs.h, qs.v)

    s1_r = s1_h.rearrange("(c pp) d -> c pp d", pp=128)
    qt_used = [None, None]
    osb_done = [None, None]
    s1_evac = [None, None]
    tr2_evac = [None]
    slopes = _slopes16()
    for tqt in range(8):
        bq_ = tqt % 2
        tq0 = tqt * 512
        if qt_used[bq_] is not None:
            nc.sync.wait_ge(pe.h, qt_used[bq_])
        nc.sync.dma_start(out=qt[bq_][:], in_=qfm_h[:, :, tq0:tq0 + 512].rearrange("k p t -> p k t")).then_inc(ldq.h, 16)
        ldq.inc(16)
        nc.tensor.wait_ge(ldq.h, ldq.v)
        exp_done, corn_done, pv_done = {}, {}, {}
        for h in range(16):
            hb, hfc = (h % 2) * 64, h // 2
            hpv = h % 2
            nA = min(4 * tqt, _win_chunks(h))
            tkcs = list(range(4 * tqt - nA, 4 * tqt)) + [4 * tqt + r for r in range(4)]
            pv_zeroed = [False]

            use_rneg = (h <= 10) or not P3_RNEG_SKIP

            def emit_scores(tkc):
                r = tkc - 4 * tqt
                col0 = max(0, r) * 128
                bp = tkc % 3
                fe = exp_done.get(bp)
                if fe is not None:
                    nc.tensor.wait_ge(sa.h, fe)
                    cd = corn_done.get(bp)
                    if cd is not None:
                        nc.tensor.wait_ge(sv.h, cd)
                nc.tensor.matmul(ps_sc[bp][:, col0:512],
                                 kfm[hb:hb + 64, hfc, tkc * 128:(tkc + 1) * 128],
                                 qt[bq_][hb:hb + 64, hfc, col0:512],
                                 start=True, stop=not use_rneg)
                if use_rneg:
                    nc.tensor.matmul(ps_sc[bp][:, col0:512], ones2[:, 0:128],
                                     rneg[:, h, col0:512], start=False, stop=True)
                nc.tensor.drain().then_inc(pe.h, 1)
                pe.inc()

            def emit_exp(tkc):
                r = tkc - 4 * tqt
                col0 = max(0, r) * 128
                bp = tkc % 3
                dC = tkc - 4 * tqt + 31
                bias_col = arel[:, dC, h:h + 1]
                nc.scalar.wait_ge(pe.h, pe.v)
                fp_ = pv_done.get(bp)
                if fp_ is not None:
                    nc.scalar.wait_ge(pe.h, fp_)
                if r >= 0:
                    nc.vector.wait_ge(pe.h, pe.v)
                    nc.vector.tensor_tensor(corn_sb[:, bp], ps_sc[bp][:, col0:col0 + 128], mcorn[:], ALU.add)
                    nc.vector.drain().then_inc(sv.h, 1)
                    sv.inc()
                    corn_done[bp] = sv.v
                    nc.scalar.wait_ge(sv.h, sv.v)
                    nc.scalar.activation(pt_sb[:, bp, col0:col0 + 128], corn_sb[:, bp], AF.Exp,
                                         bias=bias_col, scale=cpool2[:, 0:1])
                    if col0 + 128 < 512:
                        nc.scalar.activation(pt_sb[:, bp, col0 + 128:512], ps_sc[bp][:, col0 + 128:512],
                                             AF.Exp, bias=bias_col, scale=cpool2[:, 0:1])
                else:
                    nc.scalar.activation(pt_sb[:, bp], ps_sc[bp][:], AF.Exp,
                                         bias=bias_col, scale=cpool2[:, 0:1])
                nc.scalar.drain().then_inc(sa.h, 1)
                sa.inc()
                exp_done[bp] = sa.v

            def emit_pv(tkc):
                r = tkc - 4 * tqt
                bp = tkc % 3
                if not pv_zeroed[0]:
                    if osb_done[hpv] is not None:
                        nc.tensor.wait_ge(sv.h, osb_done[hpv])
                    nc.tensor.matmul(_pv3[hpv][:], zrow3[0:1, 0:128], zrow3[0:1, 0:264],
                                     start=True, stop=False, skip_group_check=True)
                    pv_zeroed[0] = True
                nc.tensor.wait_ge(sa.h, exp_done[bp])
                for sub in range(max(0, r), 4):
                    nc.tensor.matmul(pvv[hpv][sub], pt_sb[:, bp, sub * 128:(sub + 1) * 128],
                                     vaug[:, tkc, h, :], start=False, stop=False,
                                     skip_group_check=True)
                nc.tensor.drain().then_inc(pe.h, 1)
                pe.inc()
                pv_done[bp] = pe.v

            for idx, tkc in enumerate(tkcs):
                emit_scores(tkc)
                emit_exp(tkc)
                if idx > 0:
                    emit_pv(tkcs[idx - 1])
            emit_pv(tkcs[-1])
            nc.tensor.drain().then_inc(pe.h, 1)
            pe.inc()
            nc.vector.wait_ge(pe.h, pe.v)
            nc.vector.tensor_copy(osb[:].rearrange("p s o -> p (s o)"), _pv3[hpv][:])
            nc.vector.drain()
            nc.vector.reciprocal(osb[:, 0:4, 65:66].rearrange("p s o -> p (s o)"),
                                 osb[:, 0:4, 64:65].rearrange("p s o -> p (s o)"))
            nc.vector.drain()
            for sub in range(4):
                nc.vector.tensor_scalar(att[:, sub, h * 64:(h + 1) * 64], osb[:, sub, 0:64],
                                        osb[:, sub, 65:66], None, ALU.mult)
            nc.vector.drain().then_inc(sv.h, 1)
            sv.inc()
            osb_done[hpv] = sv.v
        qt_used[bq_] = pe.v
        att_sv = sv.v
        for sub in range(4):
            c_idx = tqt * 4 + sub
            bo_ = sub % 2
            nc.tensor.wait_ge(sv.h, att_sv)
            for g in range(2):
                if tr2_evac[0] is not None:
                    nc.tensor.wait_ge(sv.h, tr2_evac[0])
                for k in range(4):
                    kk = g * 4 + k
                    nc.tensor.transpose(ps_tr2[:, k * 128:(k + 1) * 128],
                                        att[:, sub, kk * 128:(kk + 1) * 128], ident2[:])
                nc.tensor.drain().then_inc(pe.h, 1)
                pe.inc()
                nc.vector.wait_ge(pe.h, pe.v)
                nc.vector.tensor_copy(afm[:, g * 4:(g + 1) * 4, :],
                                      ps_tr2[:].rearrange("p (k t) -> p k t", k=4))
                nc.vector.drain().then_inc(sv.h, 1)
                sv.inc()
                tr2_evac[0] = sv.v
            afm_sv = sv.v
            for half in range(2):
                nc.tensor.wait_ge(sv.h, afm_sv)
                if s1_evac[half] is not None:
                    nc.tensor.wait_ge(sv.h, s1_evac[half])
                for fc in range(FC):
                    nc.tensor.matmul(ps_s1[half][:], afm[:, fc, :],
                                     Wout_sb[:, fc, half * 512:(half + 1) * 512],
                                     start=(fc == 0), stop=False)
                nc.tensor.matmul(ps_s1[half][:], ones2[0:1, 0:128], bo_row[0:1, half * 512:(half + 1) * 512],
                                 start=False, stop=True)
                nc.tensor.drain().then_inc(pe.h, 1)
                pe.inc()
                nc.vector.wait_ge(pe.h, pe.v)
                if half == 0 and c_idx >= 2:
                    nc.vector.wait_ge(so.h, so.v)
                nc.vector.tensor_copy(s1st[bo_][:, half * 512:(half + 1) * 512], ps_s1[half][:])
                nc.vector.drain().then_inc(sv.h, 1)
                sv.inc()
                s1_evac[half] = sv.v
            nc.sync.wait_ge(sv.h, sv.v)
            nc.sync.dma_start(out=s1_r[c_idx], in_=s1st[bo_][:]).then_inc(so.h, 16)
            so.inc(16)
    nc.sync.wait_ge(so.h, so.v)
    es2.close()


def make_consts_p3():
    c = {}
    slopes = np.array(_slopes16(), dtype=np.float64)
    C = ALIBI_C_SAFE
    p_ = np.arange(128)[:, None, None]
    dC = np.arange(35)[None, :, None] - 31
    c["AREL"] = ((p_ + 128 * dC) * slopes[None, None, :] - C).astype(np.float32)
    import ml_dtypes
    v = -(np.arange(512)[None, :] * slopes[:, None] * np.sqrt(192.0))
    hi = v.astype(ml_dtypes.bfloat16).astype(np.float64)
    lo = (v - hi).astype(ml_dtypes.bfloat16)
    c["RNEG"] = np.stack([hi.astype(ml_dtypes.bfloat16), lo], axis=0).reshape(2, 16, 512)
    c["MCORN"] = np.where(np.arange(128)[:, None] <= np.arange(128)[None, :],
                          0.0, -1.39e7).astype(np.float32)
    c["ONES2"] = np.ones((2, 512), dtype=ml_dtypes.bfloat16)
    return c


def _rope_tables():
    import ml_dtypes
    dh = 64
    inv_freq = 1.0 / 10000.0 ** (np.arange(0, dh, 2, dtype=np.float32) / dh)
    t = np.arange(T, dtype=np.float32)
    fr = np.outer(t, inv_freq)
    sin, cos = np.sin(fr), np.cos(fr)
    cosr = np.concatenate([cos, cos], axis=1)
    sinf = np.concatenate([-sin, sin], axis=1)
    COS = np.tile(cosr, (1, 16)).astype(ml_dtypes.bfloat16)
    SINF = np.tile(sinf, (1, 16)).astype(ml_dtypes.bfloat16)
    return COS, SINF


def prep_p3(inputs):
    f = lambda a: np.ascontiguousarray(np.asarray(a, dtype=np.float32))
    d = {}
    wq = np.asarray(inputs["L1_attn_w_qkv"], dtype=np.float32)
    d["Wq1"] = f(wq[:, :, 0:64].reshape(D, D))
    d["Wk1"] = f(wq[:, :, 64:128].reshape(D, D))
    d["Wv1"] = f(wq[:, :, 128:192].reshape(D, D))
    b = np.asarray(inputs["L1_attn_b_qkv"], dtype=np.float32)
    d["bq1"] = f(b[:, 0:64].reshape(1, D))
    d["bk1"] = f(b[:, 64:128].reshape(1, D))
    d["bv1"] = f(b[:, 128:192].reshape(1, D))
    d["WoutF"] = f(np.asarray(inputs["L1_attn_w_out"], dtype=np.float32).reshape(D, D))
    d["bout"] = f(np.asarray(inputs["L1_attn_b_out"], dtype=np.float32).reshape(1, D))
    return d


def build_p3():
    p = Prog()
    io = {
        "xnfm1": p.hbm("xnfm1", [FC, 128, T], BF16),
        "Wq1": p.hbm("Wq1", [D, D], F32),
        "Wk1": p.hbm("Wk1", [D, D], F32),
        "Wv1": p.hbm("Wv1", [D, D], F32),
        "bq1": p.hbm("bq1", [1, D], F32),
        "bk1": p.hbm("bk1", [1, D], F32),
        "bv1": p.hbm("bv1", [1, D], F32),
        "WoutF": p.hbm("WoutF", [D, D], F32),
        "bout": p.hbm("bout", [1, D], F32),
        "COS": p.hbm("COS", [T, D], BF16),
        "SINF": p.hbm("SINF", [T, D], BF16),
        "qfm": p.hbm("qfm", [FC, 128, T], BF16, kind="Internal"),
        "kfm": p.hbm("kfm", [FC, 128, T], BF16, kind="Internal"),
        "vaug": p.hbm("vaug", [128, TC, 1040], BF16, kind="Internal"),
        "s1": p.hbm("s1", [T, D], F32, kind="ExternalOutput"),
    }
    c3 = make_consts_p3()
    p.consts = {
        "IDENT": p.hbm("IDENT", [128, 128], BF16),
        "AREL": p.hbm("AREL", [128, 35, 16], F32),
        "RNEG": p.hbm("RNEG", [2, 16, 512], BF16),
        "MCORN": p.hbm("MCORN", [128, 128], F32),
        "ONES2": p.hbm("ONES2", [2, 512], BF16),
    }
    emit_p3(p, io)
    p.done()
    return p.nc


def emit_p4(p, io, pre="p4_"):
    nc = p.nc
    xnfm1_h, rrho_h = io["xnfm1"], io["rrho"]
    sim_h, gates_h = io["sim"], io["gates"]
    qc_h, kc_h, vc_h, op_h = io["qcat"], io["kcat"], io["vcat"], io["oproj"]
    x1_h, s1_h, out_h = io["x1"], io["s1"], io["out"]
    SQ128 = float(np.sqrt(128.0))
    BIG = 1.0e9

    xnfm = p.sbuf(pre + "xnfm", [128, FC, T], BF16)
    rr = p.sbuf(pre + "rr", [128, TC], F32)
    sim_sb = p.sbuf(pre + "sim", [128, FC, 8], BF16)
    simsq = p.sbuf(pre + "simsq", [128, FC, 8], BF16)
    onesc = p.sbuf(pre + "onesc", [128, 1], BF16)
    onesr = p.sbuf(pre + "onesr4", [1, 128], F32)
    grow = p.sbuf(pre + "grow", [1, 8], F32)
    rn8 = p.sbuf(pre + "rn8", [1, 8], F32)
    rnb = p.sbuf(pre + "rnb", [128, 8], F32)
    sgb = p.sbuf(pre + "sgb", [128, 8], F32)
    ident = p.sbuf(pre + "ident", [128, 128], BF16)
    mcorn = p.sbuf(pre + "mcorn", [128, 128], F32)
    cp = p.sbuf(pre + "cp", [128, 2], F32)
    rt = p.sbuf(pre + "rt", [128, 48], F32)
    mw_sb = p.sbuf(pre + "mw", [128, TC, 8], F32)
    mk_sb = p.sbuf(pre + "mk", [128, TC, 8], F32)
    qw_sb = p.sbuf(pre + "qw", [128, FC, 1024], BF16)
    kw_sb = p.sbuf(pre + "kw", [128, FC, 1024], BF16)
    vw_sb = p.sbuf(pre + "vw", [128, FC, 1024], BF16)
    ow_sb = p.sbuf(pre + "ow", [128, 8, 1024], BF16)
    acc = p.sbuf(pre + "acc", [128, 3, 128], F32)
    qfm4 = p.sbuf(pre + "qfm4", [128, TC, 128], BF16)
    kfm4 = p.sbuf(pre + "kfm4", [128, TC, 128], BF16)
    vaug4 = p.sbuf(pre + "vaug4", [128, TC, 132], BF16)
    qtbf = p.sbuf(pre + "qtbf", [128, 2, 128], BF16)
    pt4 = p.sbuf(pre + "pt4", [128, 2, 512], BF16)
    corn4 = p.sbuf(pre + "corn4", [128, 2, 128], F32)
    att4 = p.sbuf(pre + "att4", [128, 4, 132], F32)
    scat = p.sbuf(pre + "scat", [128, 2, 8, 128], BF16)
    afm8 = p.sbuf(pre + "afm8", [128, 2, 8, 128], BF16)
    g1 = p.sbuf(pre + "g1", [128, 1024], F32)
    zr4 = p.sbuf(pre + "zr4", [1, 264], BF16)
    xs = [p.sbuf(pre + f"xs{i}", [128, 1024], F32) for i in range(2)]
    ss = [p.sbuf(pre + f"ss{i}", [128, 1024], F32) for i in range(2)]
    ob = [p.sbuf(pre + f"ob{i}", [128, 1024], F32) for i in range(2)]

    ldr = p.sem(pre + "ldr")
    lds = p.sem(pre + "lds")
    so = p.sem(pre + "so")
    sv = p.sem(pre + "sv")
    sa = p.sem(pre + "sa")
    pe = p.sem(pre + "pe")

    ldrq4 = [p.sem(pre + f"ldrq4{i}") for i in range(3)]
    xn4_r = xnfm1_h.rearrange("k p t -> p k t")
    for q_eng, q_sem, pairs in (
        (nc.gpsimd, ldr, [(sim_sb[:], sim_h.rearrange("(k pp) e -> pp k e", pp=128)),
                          (qw_sb[:], qc_h.rearrange("(k pp) o -> pp k o", pp=128)),
                          (kw_sb[:], kc_h.rearrange("(k pp) o -> pp k o", pp=128)),
                          (vw_sb[:], vc_h.rearrange("(k pp) o -> pp k o", pp=128)),
                          (ow_sb[:], op_h.rearrange("e d o -> d e o"))]),
        (nc.sync, ldrq4[0], [(xnfm[:, :, 0:2048], xn4_r[:, :, 0:2048]),
                             (rr[:], rrho_h),
                             (grow[:], gates_h),
                             (mcorn[:], p.consts["MCORN"])]),
        (nc.scalar, ldrq4[1], [(xnfm[:, :, 2048:4096], xn4_r[:, :, 2048:4096]),
                               (ident[:], p.consts["IDENT"])]),
    ):
        for dst, src in pairs:
            q_eng.dma_start(out=dst, in_=src).then_inc(q_sem.h, 16)
            q_sem.inc(16)
    nc.vector.memset(cp[:, 0:1], -ALIBI_C_SAFE)
    nc.vector.drain()
    nc.vector.memset(cp[:, 1:2], 1.0 / SQ128)
    nc.vector.drain()
    nc.vector.memset(onesc[:], 1.0)
    nc.vector.drain()
    nc.vector.memset(onesr[:], 1.0)
    nc.vector.drain()
    nc.vector.memset(zr4[:], 0.0)
    nc.vector.drain().then_inc(sv.h, 1)
    sv.inc()

    es0 = ExitStack()
    ps_r = es0.enter_context(nc.psum_tensor(pre + "psr", [128, 8], F32))
    ps_a = [es0.enter_context(nc.psum_tensor(pre + f"psa{i}", [128, 512], F32)) for i in range(2)]
    ps_b = es0.enter_context(nc.psum_tensor(pre + "psb", [128, 8], F32))
    ps_t = es0.enter_context(nc.psum_tensor(pre + "pst", [128, 512], BF16))

    nc.scalar.wait_ge(ldr.h, ldr.v)
    for qs in ldrq4:
        nc.scalar.wait_ge(qs.h, qs.v)
    nc.scalar.wait_ge(sv.h, sv.v)
    nc.scalar.activation(simsq[:], sim_sb[:], AF.Square)
    nc.scalar.drain().then_inc(sa.h, 1)
    sa.inc()
    nc.tensor.wait_ge(sa.h, sa.v)
    for k in range(FC):
        nc.tensor.matmul(ps_b[:1, :], onesc[:], simsq[:, k], start=(k == 0), stop=(k == FC - 1))
    nc.tensor.drain().then_inc(pe.h, 1)
    pe.inc()
    nc.vector.wait_ge(pe.h, pe.v)
    nc.vector.tensor_copy(rn8[:], ps_b[0:1, :])
    nc.vector.drain().then_inc(sv.h, 1)
    sv.inc()
    nc.scalar.wait_ge(sv.h, sv.v)
    nc.scalar.activation(rn8[:], rn8[:], AF.Sqrt)
    nc.scalar.drain().then_inc(sa.h, 1)
    sa.inc()
    nc.vector.wait_ge(sa.h, sa.v)
    nc.vector.reciprocal(rn8[:], rn8[:])
    nc.vector.drain().then_inc(sv.h, 1)
    sv.inc()
    nc.scalar.wait_ge(sv.h, sv.v)
    nc.scalar.activation(grow[:], grow[:], AF.Sigmoid)
    nc.scalar.drain().then_inc(sa.h, 1)
    sa.inc()
    nc.tensor.wait_ge(sa.h, sa.v)
    nc.tensor.matmul(ps_b[:], onesr[:], rn8[:], start=True, stop=True)
    nc.tensor.drain().then_inc(pe.h, 1)
    pe.inc()
    nc.vector.wait_ge(pe.h, pe.v)
    nc.vector.tensor_copy(rnb[:], ps_b[:])
    nc.vector.drain().then_inc(sv.h, 1)
    sv.inc()
    nc.tensor.wait_ge(sv.h, sv.v)
    nc.tensor.matmul(ps_b[:], onesr[:], grow[:], start=True, stop=True)
    nc.tensor.drain().then_inc(pe.h, 1)
    pe.inc()
    nc.vector.wait_ge(pe.h, pe.v)
    nc.vector.tensor_copy(sgb[:], ps_b[:])
    nc.vector.drain().then_inc(sv.h, 1)
    sv.inc()

    ps_r_free = [None]

    def emit_routing(tc):
        if ps_r_free[0] is not None:
            nc.tensor.wait_ge(sv.h, ps_r_free[0])
        else:
            nc.tensor.wait_ge(sv.h, sv.v)
        for k in range(FC):
            nc.tensor.matmul(ps_r[:], xnfm[:, k, tc * 128:(tc + 1) * 128], sim_sb[:, k],
                             start=(k == 0), stop=(k == FC - 1))
        nc.tensor.drain().then_inc(pe.h, 1)
        pe.inc()
        nc.vector.wait_ge(pe.h, pe.v)
        nc.vector.tensor_scalar(rt[:, 0:8], ps_r[:], rr[:, tc:tc + 1], None, ALU.mult)
        nc.vector.drain().then_inc(sv.h, 1)
        sv.inc()
        ps_r_free[0] = sv.v
        nc.vector.tensor_tensor(rt[:, 0:8], rt[:, 0:8], rnb[:], ALU.mult)
        nc.vector.drain()
        nc.vector.tensor_tensor(rt[:, 0:8], rt[:, 0:8], sgb[:], ALU.subtract)
        nc.vector.drain()
        nc.vector.tensor_scalar(rt[:, 8:16], rt[:, 0:8], 0.0, None, ALU.is_gt)
        nc.vector.drain()
        nc.vector.reduce_sum(rt[:, 16:17], rt[:, 8:16], axis=AX.X)
        nc.vector.drain()
        nc.vector.tensor_scalar(rt[:, 17:18], rt[:, 16:17], 0.0, None, ALU.is_gt)
        nc.vector.drain()
        nc.vector.reduce_max(rt[:, 18:19], rt[:, 0:8], axis=AX.X)
        nc.vector.drain()
        nc.vector.tensor_scalar(rt[:, 24:32], rt[:, 0:8], rt[:, 18:19], None, ALU.is_ge)
        nc.vector.drain()
        nc.vector.scalar_tensor_tensor(rt[:, 24:32], rt[:, 24:32], -BIG, rt[:, 0:8],
                                       ALU.mult, ALU.add)
        nc.vector.drain()
        nc.vector.reduce_max(rt[:, 19:20], rt[:, 24:32], axis=AX.X)
        nc.vector.drain()
        nc.vector.tensor_scalar(rt[:, 24:32], rt[:, 0:8], rt[:, 19:20], None, ALU.is_ge)
        nc.vector.drain()
        nc.vector.tensor_scalar(rt[:, 32:40], rt[:, 8:16], rt[:, 17:18], None, ALU.mult)
        nc.vector.drain()
        nc.vector.tensor_scalar(rt[:, 20:21], rt[:, 17:18], -1.0, 1.0, ALU.mult, ALU.add)
        nc.vector.drain()
        nc.vector.tensor_scalar(rt[:, 24:32], rt[:, 24:32], rt[:, 20:21], None, ALU.mult)
        nc.vector.drain()
        nc.vector.tensor_tensor(mk_sb[:, tc], rt[:, 32:40], rt[:, 24:32], ALU.add)
        nc.vector.drain()
        nc.vector.tensor_scalar(rt[:, 24:32], rt[:, 0:8], 0.0, BIG, ALU.max, ALU.add)
        nc.vector.drain()
        nc.vector.tensor_tensor(rt[:, 24:32], rt[:, 24:32], mk_sb[:, tc], ALU.mult)
        nc.vector.drain()
        nc.vector.tensor_scalar(rt[:, 24:32], rt[:, 24:32], 1.0, -BIG, ALU.mult, ALU.add)
        nc.vector.drain()
        nc.vector.reduce_max(rt[:, 18:19], rt[:, 24:32], axis=AX.X)
        nc.vector.drain()
        nc.vector.tensor_scalar(rt[:, 19:20], rt[:, 18:19], -1.0, None, ALU.mult)
        nc.vector.drain().then_inc(sv.h, 1)
        sv.inc()
        nc.scalar.wait_ge(sv.h, sv.v)
        nc.scalar.activation(rt[:, 24:32], rt[:, 24:32], AF.Exp, bias=rt[:, 19:20], scale=1.0)
        nc.scalar.drain().then_inc(sa.h, 1)
        sa.inc()
        nc.vector.wait_ge(sa.h, sa.v)
        nc.vector.reduce_sum(rt[:, 18:19], rt[:, 24:32], axis=AX.X)
        nc.vector.drain()
        nc.vector.reciprocal(rt[:, 19:20], rt[:, 18:19])
        nc.vector.drain()
        nc.vector.tensor_scalar(rt[:, 24:32], rt[:, 24:32], rt[:, 19:20], None, ALU.mult)
        nc.vector.drain()
        nc.vector.tensor_tensor(mw_sb[:, tc], rt[:, 24:32], mk_sb[:, tc], ALU.mult)
        nc.vector.drain().then_inc(sv.h, 1)
        sv.inc()

    a_free = {}

    def emit_mqkv(tc):
        for ti, (W, dst_fm, ai) in enumerate(((qw_sb, qfm4, 0), (kw_sb, kfm4, 1), (vw_sb, None, 2))):
            for grp in range(2):
                bp = grp
                fa = a_free.get(bp)
                if fa is not None:
                    nc.tensor.wait_ge(sv.h, fa)
                for k in range(FC):
                    nc.tensor.matmul(ps_a[bp][:], xnfm[:, k, tc * 128:(tc + 1) * 128],
                                     W[:, k, grp * 512:(grp + 1) * 512],
                                     start=(k == 0), stop=(k == FC - 1))
                nc.tensor.drain().then_inc(pe.h, 1)
                pe.inc()
                nc.vector.wait_ge(pe.h, pe.v)
                for ee in range(4):
                    e = grp * 4 + ee
                    mcol = mk_sb[:, tc, e:e + 1]
                    dst = acc[:, ai, :]
                    if e == 0:
                        nc.vector.tensor_scalar(dst, ps_a[bp][:, ee * 128:(ee + 1) * 128],
                                                mcol, None, ALU.mult)
                    else:
                        nc.vector.scalar_tensor_tensor(dst, ps_a[bp][:, ee * 128:(ee + 1) * 128],
                                                       mcol, dst, ALU.mult, ALU.add)
                    nc.vector.drain()
                nc.vector.drain().then_inc(sv.h, 1)
                sv.inc()
                a_free[bp] = sv.v
            if ti < 2:
                nc.vector.tensor_copy(qtbf[:, ti, :], acc[:, ai, :])
                nc.vector.drain().then_inc(sv.h, 1)
                sv.inc()
                nc.tensor.wait_ge(sv.h, sv.v)
                nc.tensor.transpose(ps_t[:, ti * 128:(ti + 1) * 128], qtbf[:, ti, :], ident[:])
                nc.tensor.drain().then_inc(pe.h, 1)
                pe.inc()
            else:
                nc.vector.tensor_copy(vaug4[:, tc, 0:128], acc[:, 2, :])
                nc.vector.drain()
                nc.vector.memset(vaug4[:, tc, 128:129], 1.0)
                nc.vector.drain().then_inc(sv.h, 1)
                sv.inc()
        nc.vector.wait_ge(pe.h, pe.v)
        nc.vector.tensor_copy(qfm4[:, tc, :], ps_t[:, 0:128])
        nc.vector.drain()
        nc.vector.tensor_copy(kfm4[:, tc, :], ps_t[:, 128:256])
        nc.vector.drain().then_inc(sv.h, 1)
        sv.inc()

    emit_routing(0)
    for tc in range(1, TC):
        emit_mqkv(tc - 1)
        emit_routing(tc)
    emit_mqkv(TC - 1)
    p.quiesce()
    es0.close()

    es2 = ExitStack()
    ps_sc = [es2.enter_context(nc.psum_tensor(pre + f"pssc{i}", [128, 512], F32)) for i in range(3)]
    _pv = [es2.enter_context(nc.psum_tensor(pre + f"pspv{i}", [128, 264], F32)) for i in range(2)]
    ps_pv = [_pv[i // 2][:, (i % 2) * 132:(i % 2) * 132 + 129] for i in range(4)]
    ps_tr = es2.enter_context(nc.psum_tensor(pre + "pstr", [128, 512], BF16))
    ps_g = [es2.enter_context(nc.psum_tensor(pre + f"psg{i}", [128, 512], F32)) for i in range(2)]

    x1_r = x1_h.rearrange("(c pp) d -> c pp d", pp=128)
    s1_r = s1_h.rearrange("(c pp) d -> c pp d", pp=128)
    out_r = out_h.rearrange("(c pp) d -> c pp d", pp=128)
    ob_done = [None]
    oproj_pe = [None, None]
    tr_evac = [None]
    g_free = [None, None]
    comb_sv = [None]
    comb_sv_b = [None, None]
    xsld = {}

    def prefetch_xs(j):
        if j >= TC:
            return
        bbj = j % 2
        if comb_sv_b[bbj] is not None:
            nc.sync.wait_ge(sv.h, comb_sv_b[bbj])
        nc.sync.dma_start(out=xs[bbj][:], in_=x1_r[j]).then_inc(lds.h, 16)
        nc.sync.dma_start(out=ss[bbj][:], in_=s1_r[j]).then_inc(lds.h, 16)
        lds.inc(32)
        xsld[j] = lds.v

    prefetch_xs(0)
    prefetch_xs(1)
    for tqt in range(8):
        tq0c = tqt * 4
        first_pv = [True] * 4
        zeroed = [False]
        exp_done, corn_done, pv_done = {}, {}, {}

        def emit_sc4(tkc):
            r = tkc - tq0c
            col0 = max(0, r) * 128
            bp = tkc % 2
            fe = exp_done.get(bp)
            if fe is not None:
                nc.tensor.wait_ge(sa.h, fe)
                cd = corn_done.get(bp)
                if cd is not None:
                    nc.tensor.wait_ge(sv.h, cd)
            nc.tensor.matmul(ps_sc[bp][:, col0:512], kfm4[:, tkc, :],
                             qfm4[:, tq0c:tq0c + 4, :].rearrange("p c t -> p (c t)")[:, col0:512],
                             start=True, stop=True)
            nc.tensor.drain().then_inc(pe.h, 1)
            pe.inc()

        def emit_exp4(tkc):
            r = tkc - tq0c
            col0 = max(0, r) * 128
            bp = tkc % 2
            nc.scalar.wait_ge(pe.h, pe.v)
            fp_ = pv_done.get(bp)
            if fp_ is not None:
                nc.scalar.wait_ge(pe.h, fp_)
            if r >= 0:
                nc.vector.wait_ge(pe.h, pe.v)
                nc.vector.tensor_tensor(corn4[:, bp], ps_sc[bp][:, col0:col0 + 128], mcorn[:], ALU.add)
                nc.vector.drain().then_inc(sv.h, 1)
                sv.inc()
                corn_done[bp] = sv.v
                nc.scalar.wait_ge(sv.h, sv.v)
                nc.scalar.activation(pt4[:, bp, col0:col0 + 128], corn4[:, bp], AF.Exp,
                                     bias=cp[:, 0:1], scale=cp[:, 1:2])
                if col0 + 128 < 512:
                    nc.scalar.activation(pt4[:, bp, col0 + 128:512], ps_sc[bp][:, col0 + 128:512],
                                         AF.Exp, bias=cp[:, 0:1], scale=cp[:, 1:2])
            else:
                nc.scalar.activation(pt4[:, bp], ps_sc[bp][:], AF.Exp, bias=cp[:, 0:1], scale=cp[:, 1:2])
            nc.scalar.drain().then_inc(sa.h, 1)
            sa.inc()
            exp_done[bp] = sa.v

        def emit_pv4(tkc):
            r = tkc - tq0c
            bp = tkc % 2
            if not zeroed[0]:
                if ob_done[0] is not None:
                    nc.tensor.wait_ge(sv.h, ob_done[0])
                for j in range(2):
                    nc.tensor.matmul(_pv[j][:], zr4[0:1, 0:128], zr4[0:1, 0:264],
                                     start=True, stop=False, skip_group_check=True)
                zeroed[0] = True
            nc.tensor.wait_ge(sa.h, exp_done[bp])
            for sub in range(max(0, r), 4):
                nc.tensor.matmul(ps_pv[sub], pt4[:, bp, sub * 128:(sub + 1) * 128],
                                 vaug4[:, tkc, 0:129], start=False, stop=False,
                                 skip_group_check=True)
                first_pv[sub] = False
            nc.tensor.drain().then_inc(pe.h, 1)
            pe.inc()
            pv_done[bp] = pe.v

        tkcs4 = list(range(tq0c + 4))
        for idx, tkc in enumerate(tkcs4):
            emit_sc4(tkc)
            emit_exp4(tkc)
            if idx > 0:
                emit_pv4(tkcs4[idx - 1])
        emit_pv4(tkcs4[-1])
        nc.tensor.drain().then_inc(pe.h, 1)
        pe.inc()
        nc.vector.wait_ge(pe.h, pe.v)
        for sub in range(4):
            nc.vector.tensor_copy(att4[:, sub, 0:129], ps_pv[sub])
        nc.vector.drain()
        nc.vector.reciprocal(att4[:, 0:4, 130:131].rearrange("p s o -> p (s o)"),
                             att4[:, 0:4, 128:129].rearrange("p s o -> p (s o)"))
        nc.vector.drain().then_inc(sv.h, 1)
        sv.inc()
        ob_done[0] = sv.v
        for sub in range(4):
            c_idx = tqt * 4 + sub
            bb = sub % 2
            if oproj_pe[bb] is not None:
                nc.vector.wait_ge(pe.h, oproj_pe[bb])
            for e in range(8):
                nc.vector.tensor_scalar(scat[:, bb, e], att4[:, sub, 0:128],
                                        att4[:, sub, 130:131], mw_sb[:, c_idx, e:e + 1],
                                        ALU.mult, ALU.mult)
            nc.vector.drain().then_inc(sv.h, 1)
            sv.inc()
            scat_sv = sv.v
            for g in range(2):
                nc.tensor.wait_ge(sv.h, scat_sv if g == 0 else tr_evac[0])
                if g == 0 and tr_evac[0] is not None:
                    nc.tensor.wait_ge(sv.h, tr_evac[0])
                for k in range(4):
                    nc.tensor.transpose(ps_tr[:, k * 128:(k + 1) * 128],
                                        scat[:, bb, g * 4 + k], ident[:])
                nc.tensor.drain().then_inc(pe.h, 1)
                pe.inc()
                nc.vector.wait_ge(pe.h, pe.v)
                nc.vector.tensor_copy(afm8[:, bb, g * 4:(g + 1) * 4, :],
                                      ps_tr[:].rearrange("p (k t) -> p k t", k=4))
                nc.vector.drain().then_inc(sv.h, 1)
                sv.inc()
                tr_evac[0] = sv.v
            afm8_sv = sv.v
            for half in range(2):
                nc.tensor.wait_ge(sv.h, afm8_sv)
                if g_free[half] is not None:
                    nc.tensor.wait_ge(sa.h, g_free[half])
                for e in range(8):
                    nc.tensor.matmul(ps_g[half][:], afm8[:, bb, e, :],
                                     ow_sb[:, e, half * 512:(half + 1) * 512],
                                     start=(e == 0), stop=(e == 7))
                nc.tensor.drain().then_inc(pe.h, 1)
                pe.inc()
                nc.scalar.wait_ge(pe.h, pe.v)
                if comb_sv[0] is not None:
                    nc.scalar.wait_ge(sv.h, comb_sv[0])
                nc.scalar.activation(g1[:, half * 512:(half + 1) * 512], ps_g[half][:], AF.Silu)
                nc.scalar.drain().then_inc(sa.h, 1)
                sa.inc()
                g_free[half] = sa.v
            oproj_pe[bb] = pe.v
            nc.vector.wait_ge(sa.h, sa.v)
            nc.vector.wait_ge(lds.h, xsld[c_idx])
            if c_idx >= 2:
                nc.vector.wait_ge(so.h, so.v)
            nc.vector.tensor_tensor(g1[:], ss[bb][:], g1[:], ALU.mult)
            nc.vector.drain()
            nc.vector.tensor_tensor(ob[bb][:], xs[bb][:], g1[:], ALU.add)
            nc.vector.drain().then_inc(sv.h, 1)
            sv.inc()
            comb_sv[0] = sv.v
            comb_sv_b[bb] = sv.v
            nc.sync.wait_ge(sv.h, sv.v)
            nc.sync.dma_start(out=out_r[c_idx], in_=ob[bb][:]).then_inc(so.h, 16)
            so.inc(16)
            prefetch_xs(c_idx + 2)
    nc.sync.wait_ge(so.h, so.v)
    nc.sync.wait_ge(lds.h, lds.v)


def prep_p4(inputs):
    f = lambda a: np.ascontiguousarray(np.asarray(a, dtype=np.float32))
    d = {}
    d["sim"] = f(inputs["L1_ffn_sim_matrix"])
    d["gates"] = f(np.asarray(inputs["L1_ffn_gates"], dtype=np.float32).reshape(1, 8))
    d["qcat"] = f(np.asarray(inputs["L1_ffn_q_proj"], dtype=np.float32).transpose(1, 0, 2).reshape(D, 8 * 128))
    d["kcat"] = f(np.asarray(inputs["L1_ffn_k_proj"], dtype=np.float32).transpose(1, 0, 2).reshape(D, 8 * 128))
    d["vcat"] = f(np.asarray(inputs["L1_ffn_v_proj"], dtype=np.float32).transpose(1, 0, 2).reshape(D, 8 * 128))
    d["oproj"] = f(inputs["L1_ffn_o_proj"])
    return d


def build_p4():
    p = Prog()
    io = {
        "xnfm1": p.hbm("xnfm1", [FC, 128, T], BF16),
        "rrho": p.hbm("rrho", [128, TC], F32),
        "sim": p.hbm("sim", [D, 8], F32),
        "gates": p.hbm("gates", [1, 8], F32),
        "qcat": p.hbm("qcat", [D, D], F32),
        "kcat": p.hbm("kcat", [D, D], F32),
        "vcat": p.hbm("vcat", [D, D], F32),
        "oproj": p.hbm("oproj", [8, 128, D], F32),
        "x1": p.hbm("x1", [T, D], F32),
        "s1": p.hbm("s1", [T, D], F32),
        "out": p.hbm("out", [T, D], F32, kind="ExternalOutput"),
    }
    c3 = make_consts_p3()
    p.consts = {
        "IDENT": p.hbm("IDENT", [128, 128], BF16),
        "MCORN": p.hbm("MCORN", [128, 128], F32),
    }
    emit_p4(p, io)
    p.done()
    return p.nc


def kernel_unfused(**inputs):
    x = np.ascontiguousarray(np.asarray(inputs["L0_attn_x"], dtype=np.float32))
    cst0 = make_consts()
    cst1 = make_consts_p1()
    cst3 = make_consts_p3()
    d1 = prep_p1(inputs)
    d2 = prep_shared(inputs)
    d3 = prep_p3(inputs)
    d4 = prep_p4(inputs)
    COS, SINF = _rope_tables()
    IDENT = cst0["IDENT"]

    nc = build_p0()
    res = run_bass_kernel_spmd(nc, [{"x": x[c], "IDENT": IDENT} for c in range(NCORES)],
                               core_ids=list(range(NCORES)))
    xnfm = [r["xnfm"] for r in res.results]

    nc = build_p1()
    im = [dict(xnfm=xnfm[c], IDENT=IDENT,
               **{k: d1[k] for k in ["WqD", "WkD", "Wv", "Wo", "G1", "G2", "thetaD", "gam", "bet"]},
               **{k: cst1[k] for k in ["TRIU4", "PHASE", "DTIDX", "SUPIDX"]})
          for c in range(NCORES)]
    res = run_bass_kernel_spmd(nc, im, core_ids=list(range(NCORES)))
    s0 = [r["s0"] for r in res.results]

    nc = build_p2()
    im = [dict(xnfm=xnfm[c], x=x[c], s0=s0[c], IDENT=IDENT,
               **{k: d2[k] for k in ["wgT", "w1T", "w3T", "w2T"]})
          for c in range(NCORES)]
    res = run_bass_kernel_spmd(nc, im, core_ids=list(range(NCORES)))
    x1 = [r["x1"] for r in res.results]
    xnfm1 = [r["xnfm1"] for r in res.results]
    rrho = [r["rrho"] for r in res.results]

    nc = build_p3()
    im = [dict(xnfm1=xnfm1[c], COS=COS, SINF=SINF, IDENT=IDENT,
               **{k: d3[k] for k in ["Wq1", "Wk1", "Wv1", "bq1", "bk1", "bv1", "WoutF", "bout"]},
               **{k: cst3[k] for k in ["AREL", "RNEG", "MCORN", "ONES2"]})
          for c in range(NCORES)]
    res = run_bass_kernel_spmd(nc, im, core_ids=list(range(NCORES)))
    s1 = [r["s1"] for r in res.results]

    nc = build_p4()
    im = [dict(xnfm1=xnfm1[c], rrho=rrho[c], x1=x1[c], s1=s1[c], IDENT=IDENT,
               MCORN=cst3["MCORN"],
               **{k: d4[k] for k in ["sim", "gates", "qcat", "kcat", "vcat", "oproj"]})
          for c in range(NCORES)]
    res = run_bass_kernel_spmd(nc, im, core_ids=list(range(NCORES)))
    out = np.stack([np.asarray(r["out"], dtype=np.float32) for r in res.results])
    return out


def build_fused():
    p = Prog()
    EXT, INT = "ExternalInput", "Internal"
    io = {}
    io["x"] = p.hbm("x", [T, D], F32)
    for n, shp, dt in [("xnfm", [FC, 128, T], BF16), ("s0", [T, D], F32),
                       ("x1", [T, D], F32), ("xnfm1", [FC, 128, T], BF16),
                       ("rrho", [128, TC], F32), ("s1", [T, D], F32),
                       ("qfm", [FC, 128, T], BF16), ("kfm", [FC, 128, T], BF16),
                       ("vaug", [128, TC, 1040], BF16)]:
        io[n] = p.hbm(n, shp, dt, kind=INT)
    io["out"] = p.hbm("out", [T, D], F32, kind="ExternalOutput")
    for n, shp in [("WqD", [D, 2048]), ("WkD", [D, 2048]), ("Wv", [D, D]), ("Wo", [D, D]),
                   ("G1", [D, 64]), ("G2", [64, D]), ("thetaD", [1, 2048]),
                   ("gam", [128, D]), ("bet", [128, D]),
                   ("wgT", [D, 8]), ("w1T", [8, D, 2048]), ("w3T", [8, D, 2048]),
                   ("w2T", [8, 2048, D]),
                   ("Wq1", [D, D]), ("Wk1", [D, D]), ("Wv1", [D, D]),
                   ("bq1", [1, D]), ("bk1", [1, D]), ("bv1", [1, D]),
                   ("WoutF", [D, D]), ("bout", [1, D]),
                   ("sim", [D, 8]), ("gates", [1, 8]),
                   ("qcat", [D, D]), ("kcat", [D, D]), ("vcat", [D, D]),
                   ("oproj", [8, 128, D])]:
        io[n] = p.hbm(n, shp, F32)
    for n, shp in [("COS", [T, D]), ("SINF", [T, D])]:
        io[n] = p.hbm(n, shp, BF16)
    p.consts = {
        "IDENT": p.hbm("IDENT", [128, 128], BF16),
        "TRIU4": p.hbm("TRIU4", [128, 4, 128], F32),
        "PHASE": p.hbm("PHASE", [128, 1], F32),
        "DTIDX": p.hbm("DTIDX", [1, 512], F32),
        "SUPIDX": p.hbm("SUPIDX", [1, 8], F32),
        "AREL": p.hbm("AREL", [128, 35, 16], F32),
        "RNEG": p.hbm("RNEG", [2, 16, 512], BF16),
        "MCORN": p.hbm("MCORN", [128, 128], F32),
        "ONES2": p.hbm("ONES2", [2, 512], BF16),
    }
    with p.nc.named_scope("P0"):
        p.push_scope()
        emit_p0(p, io["x"], io["xnfm"])
        p.pop_scope()
        p.quiesce()
    with p.nc.named_scope("P1"):
        p.push_scope()
        emit_p1(p, io)
        p.pop_scope()
        p.quiesce()
    with p.nc.named_scope("P2"):
        p.push_scope()
        emit_p2(p, io)
        p.pop_scope()
        p.quiesce()
    with p.nc.named_scope("P3"):
        p.push_scope()
        emit_p3(p, io)
        p.pop_scope()
        p.quiesce()
    with p.nc.named_scope("P4"):
        p.push_scope()
        emit_p4(p, io)
        p.pop_scope()
    p.done()
    return p.nc


def kernel_fused(**inputs):
    x = np.ascontiguousarray(np.asarray(inputs["L0_attn_x"], dtype=np.float32))
    cst0 = make_consts()
    cst1 = make_consts_p1()
    cst3 = make_consts_p3()
    d1 = prep_p1(inputs)
    d2 = prep_shared(inputs)
    d3 = prep_p3(inputs)
    d4 = prep_p4(inputs)
    COS, SINF = _rope_tables()
    base = dict(IDENT=cst0["IDENT"], COS=COS, SINF=SINF,
                **{k: cst1[k] for k in ["TRIU4", "PHASE", "DTIDX", "SUPIDX"]},
                **{k: cst3[k] for k in ["AREL", "RNEG", "MCORN", "ONES2"]},
                **{k: d1[k] for k in ["WqD", "WkD", "Wv", "Wo", "G1", "G2", "thetaD", "gam", "bet"]},
                **{k: d2[k] for k in ["wgT", "w1T", "w3T", "w2T"]},
                **{k: d3[k] for k in ["Wq1", "Wk1", "Wv1", "bq1", "bk1", "bv1", "WoutF", "bout"]},
                **{k: d4[k] for k in ["sim", "gates", "qcat", "kcat", "vcat", "oproj"]})
    nc = build_fused()
    im = [dict(x=x[c], **base) for c in range(NCORES)]
    res = run_bass_kernel_spmd(nc, im, core_ids=list(range(NCORES)))
    return np.stack([np.asarray(r["out"], dtype=np.float32) for r in res.results])


def kernel(**inputs):
    return kernel_fused(**inputs)
